# version 01
# baseline (speedup 1.0000x reference)
import sys
sys.path.insert(0, "/opt/trn_rl_repo")
from contextlib import ExitStack
import numpy as np
import ml_dtypes

import concourse.bass as bass
import concourse.mybir as mybir
from concourse.bass_utils import run_bass_kernel_spmd

F32 = mybir.dt.float32
BF16 = mybir.dt.bfloat16
AF = mybir.ActivationFunctionType
ALU = mybir.AluOpType
AX = mybir.AxisListType

SEQ = 2048
DM = 2048
H = 16
DFF = 8192
NT = 16
ND = 16
NF = 64
NCH = 4
CH = 512
SCL = float(1.0 / np.sqrt(128.0))
NEG = -1e10
N_CORES = 8

ENGS = ("sync", "scalar", "vector", "tensor", "gpsimd")


class Sem:
    __slots__ = ("name", "n", "h")

    def __init__(self, name):
        self.name = name
        self.n = 0
        self.h = None


class KB:
    def __init__(self):
        self.ops = {e: [] for e in ENGS}
        self.sems = []
        self.guar = {e: {} for e in ENGS}

    def sem(self, name):
        s = Sem(name)
        self.sems.append(s)
        return s

    def wait(self, eng, sm, val):
        if sm is None or val <= 0:
            return
        if self.guar[eng].get(sm, 0) >= val:
            return
        self.guar[eng][sm] = val
        self.ops[eng].append(("w", sm, val))

    def op(self, eng, fn, sm=None, inc=0, waits=()):
        for w in waits:
            if w is not None:
                self.wait(eng, w[0], w[1])
        if sm is not None:
            sm.n += inc
            self.ops[eng].append(("o", fn, sm, inc))
            return (sm, sm.n)
        self.ops[eng].append(("o", fn, None, 0))
        return None

    def realize(self, eng, h):
        for item in self.ops[eng]:
            if item[0] == "w":
                h.wait_ge(item[1].h, item[2])
            else:
                inst = item[1](h)
                if item[2] is not None:
                    inst.then_inc(item[2].h, item[3])


def build_masks():
    kl = np.arange(128)[:, None]
    x = np.arange(896)[None, :]
    m = np.where(x >= kl + 384, 0.0, NEG).astype(np.float32)
    return np.asarray(m.astype(ml_dtypes.bfloat16))


def build_nc():
    nc = bass.Bass("TRN2", target_bir_lowering=False)

    dx = nc.dram_tensor("x", [SEQ, DM], F32, kind="ExternalInput").ap()
    dwqkv = nc.dram_tensor("wqkv", [DM, 3 * DM], F32, kind="ExternalInput").ap()
    dwout = nc.dram_tensor("wout", [DM, DM], F32, kind="ExternalInput").ap()
    dw1 = nc.dram_tensor("w1", [DFF, DM], F32, kind="ExternalInput").ap()
    db1 = nc.dram_tensor("b1", [NF, 128], F32, kind="ExternalInput").ap()
    dw2 = nc.dram_tensor("w2", [DM, DFF], F32, kind="ExternalInput").ap()
    db2 = nc.dram_tensor("b2", [1, DM], F32, kind="ExternalInput").ap()
    dident = nc.dram_tensor("ident", [128, 128], F32, kind="ExternalInput").ap()
    dones1 = nc.dram_tensor("ones1", [1, 128], F32, kind="ExternalInput").ap()
    dmasks = nc.dram_tensor("masks", [128, 896], BF16, kind="ExternalInput").ap()
    dy = nc.dram_tensor("y", [SEQ, DM], F32, kind="ExternalOutput").ap()

    dxn = nc.dram_tensor("xn_nat", [SEQ, DM], BF16, kind="Internal").ap()
    dotd = nc.dram_tensor("otd", [DM, SEQ], BF16, kind="Internal").ap()
    dsd = nc.dram_tensor("sd", [SEQ, DM], BF16, kind="Internal").ap()
    dw1q = nc.dram_tensor("w1q", [DFF, DM], BF16, kind="Internal").ap()
    dw2q = nc.dram_tensor("w2q", [DM, DFF], BF16, kind="Internal").ap()

    kb = KB()
    sy_d = kb.sem("sy_d")
    sc_d = kb.sem("sc_d")
    gp_d = kb.sem("gp_d")
    pe = kb.sem("pe")
    ve = kb.sem("ve")
    ae = kb.sem("ae")
    ge = kb.sem("ge")

    with ExitStack() as ctx:
        E = ctx.enter_context
        sb, ps = nc.sbuf_tensor, nc.psum_tensor

        XNT = E(sb("XNT", [128, ND * SEQ], BF16))
        BIGB = E(sb("BIGB", [128, 24576], BF16))
        QKVB = E(sb("QKVB", [128, 3 * SEQ], BF16))
        WQB = E(sb("WQB", [128, 3 * DM], BF16))
        XNB = E(sb("XNB", [128, 4096], BF16))
        WP = E(sb("WP", [128, 8192], BF16))
        G = E(sb("G", [128, 4096], BF16))
        MASKS = E(sb("MASKS", [128, 896], BF16))
        OTST = E(sb("OTST", [128, 1024], BF16))
        SST = E(sb("SST", [128, 1024], BF16))
        IDENT = E(sb("IDENT", [128, 128], F32))
        ONES1 = E(sb("ONES1", [1, 128], F32))
        ONESCB = E(sb("ONESCB", [128, 1], BF16))
        ONESCF = E(sb("ONESCF", [128, 1], F32))
        EPSC = E(sb("EPSC", [128, 1], F32))
        B1RS = E(sb("B1RS", [64, 128], F32))
        B1C = E(sb("B1C", [128, 64], F32))
        B2ROW = E(sb("B2ROW", [1, DM], F32))
        LNS = E(sb("LNS", [128, 12], F32))
        WACC = E(sb("WACC", [128, 256], F32))
        ANCH = E(sb("ANCH", [128, 4], F32))
        ANCM = E(sb("ANCM", [128, 1], F32))
        NEGM = E(sb("NEGM", [128, 1], F32))
        NM1 = E(sb("NM1", [1, 1], F32))
        NM2 = E(sb("NM2", [1, 1], F32))
        SREC = E(sb("SREC", [1, CH], F32))
        SCLB = [E(sb("SCL1", [1, 1], F32)), E(sb("SCL2", [1, 1], F32))]
        TCOL = [E(sb("TC1", [128, 1], F32)), E(sb("TC2", [128, 1], F32))]
        NTCOL = [E(sb("NTC1", [128, 1], F32)), E(sb("NTC2", [128, 1], F32))]
        WSUM = [E(sb("WS1", [128, 1], F32)), E(sb("WS2", [128, 1], F32))]

        P = [E(ps(f"P{i}", [128, 512], F32)) for i in range(6)]
        PROW = E(ps("PROW", [1, 512], F32))
        P7 = E(ps("P7", [128, 512], F32))

        def xf_slot(i):
            return BIGB[:, i * 4096:(i + 1) * 4096].bitcast(F32)

        def xnb_slot(i):
            return BIGB[:, 8192 + i * 2048:8192 + (i + 1) * 2048]

        LNDUMP = BIGB[:, 12288:16384].bitcast(F32)

        def woutp_slot(i):
            return BIGB[:, i * 8192:(i + 1) * 8192]

        def otst2_slot(i):
            return BIGB[:, 16384 + i * 2048:16384 + (i + 1) * 2048]

        def w1p_slot(i):
            return BIGB[:, i * 8192:(i + 1) * 8192]

        def xnc_d(dt):
            return BIGB[:, 16384 + dt * 512:16384 + (dt + 1) * 512]

        qh = QKVB[:, 0:SEQ]
        kh = QKVB[:, SEQ:2 * SEQ]
        vh = QKVB[:, 2 * SEQ:3 * SEQ]
        wqs = WQB[:, 0:DM]
        wks = WQB[:, DM:2 * DM]
        wvs = WQB[:, 2 * DM:3 * DM]

        def pt_slot(i):
            return XNB[:, i * 512:(i + 1) * 512]

        RECIPB = XNB[:, 1024:2048].bitcast(F32)

        def w2m_slot(i):
            return XNB[:, i * 512:(i + 1) * 512]

        def wpa_slot(i):
            return WP[:, i * 2048:(i + 1) * 2048].bitcast(F32)

        WPT = WP[:, 4096:6144].bitcast(F32)

        def wpo_slot(i):
            return WP[:, 6144 + i * 1024:6144 + (i + 1) * 1024]

        def xc_slot(i):
            return WP[:, i * 1024:(i + 1) * 1024].bitcast(F32)

        def st5_slot(i):
            return WP[:, 2048 + i * 512:2048 + (i + 1) * 512]

        B2B = WQB[:, 0:4096].bitcast(F32)

        def zt_slot(i):
            return WQB[:, 4096 + i * 1024:4096 + (i + 1) * 1024].bitcast(F32)

        def g_slice(par, tt):
            return G[:, par * 2048 + tt * 512:par * 2048 + (tt + 1) * 512]

        lnsum = [LNS[:, 0:1], LNS[:, 1:2]]
        lnmean = [LNS[:, 2:3], LNS[:, 3:4]]
        lnvs = [LNS[:, 4:5], LNS[:, 5:6]]
        lnln = [LNS[:, 6:7], LNS[:, 7:8]]
        lnrstd = [LNS[:, 8:9], LNS[:, 9:10]]

        for dst, src in ((IDENT[:], dident), (ONES1[:], dones1),
                         (MASKS[:], dmasks), (B1RS[:], db1), (B2ROW[:], db2)):
            kb.op("sync", lambda e, d=dst, s=src: e.dma_start(out=d, in_=s),
                  sy_d, 16)
        init_loads = (sy_d, sy_d.n)

        kb.op("gpsimd", lambda e: e.memset(ONESCB[:], 1.0), ge, 1)
        kb.op("gpsimd", lambda e: e.memset(ONESCF[:], 1.0), ge, 1)
        kb.op("gpsimd", lambda e: e.memset(EPSC[:], 1e-5), ge, 1)
        init_memset = (ge, ge.n)

        b1t = kb.op("tensor", lambda e: e.transpose(P7[:, 0:64], B1RS[:], IDENT[0:64, 0:64]),
                    pe, 1, waits=[init_loads])
        b1c_ready = kb.op("vector", lambda e: e.tensor_copy(B1C[:], P7[:, 0:64]),
                          ve, 1, waits=[b1t])
        p7_free = b1c_ready

        def w_tile(mat, k):
            if mat == 0:
                sl = (slice((k // 2) * 128, (k // 2 + 1) * 128),
                      slice((k % 2) * 1024, (k % 2 + 1) * 1024))
                return dw1[sl[0], sl[1]], dw1q[sl[0], sl[1]]
            sl = (slice((k // 8) * 128, (k // 8 + 1) * 128),
                  slice((k % 8) * 1024, (k % 8 + 1) * 1024))
            return dw2[sl[0], sl[1]], dw2q[sl[0], sl[1]]

        passA = [(0, k) for k in range(128)] + [(1, k) for k in range(128)]
        passB = list(passA)
        st = {
            "Ai": 0, "Bi": 0,
            "wpa_war": [None, None],
            "wpo_war": [None, None],
            "w1A_done": None, "w2A_done": None,
            "s_done": [None, None],
            "prev_stt": None,
            "wq_wr": {}, "w2_wr": {},
        }

        def emit_passA(nmax):
            for _ in range(nmax):
                if st["Ai"] >= len(passA):
                    return
                mat, k = passA[st["Ai"]]
                src, _ = w_tile(mat, k)
                slot = st["Ai"] % 2
                col = mat * 128 + k
                v = kb.op("sync",
                          lambda e, s=src, sl=slot: e.dma_start(out=wpa_slot(sl),
                                                                in_=s),
                          sy_d, 16, waits=[st["wpa_war"][slot]])
                r = kb.op("vector",
                          lambda e, sl=slot, c=col: e.reduce_sum(
                              WACC[:, c:c + 1], wpa_slot(sl), axis=AX.X,
                              apply_absolute_value=True),
                          ve, 1, waits=[v])
                st["wpa_war"][slot] = r
                st["Ai"] += 1
                if st["Ai"] == 128:
                    st["w1A_done"] = r
                if st["Ai"] == 256:
                    st["w2A_done"] = r

        def finalize_scale(mat, prow_rd, p7f):
            a_done = st["w1A_done"] if mat == 0 else st["w2A_done"]
            assert a_done is not None
            v1 = kb.op("vector",
                       lambda e, m=mat: e.tensor_reduce(
                           WSUM[m][:], WACC[:, m * 128:(m + 1) * 128],
                           axis=AX.X, op=ALU.add),
                       ve, 1, waits=[a_done])
            t1 = kb.op("tensor",
                       lambda e, m=mat: e.matmul(PROW[0:1, 0:1], ONESCF[:],
                                                 WSUM[m][:], start=True, stop=True),
                       pe, 1, waits=[v1, init_memset, prow_rd])
            a1 = kb.op("scalar",
                       lambda e, m=mat: e.activation(SCLB[m][:], PROW[0:1, 0:1],
                                                     AF.Copy,
                                                     scale=0.7 / float(DFF * DM)),
                       ae, 1, waits=[t1])
            t2 = kb.op("tensor",
                       lambda e, m=mat: e.matmul(P7[:, 0:1], ONES1[:], SCLB[m][:],
                                                 start=True, stop=True),
                       pe, 1, waits=[a1, p7f, init_loads])
            v2 = kb.op("vector",
                       lambda e, m=mat: e.tensor_copy(TCOL[m][:], P7[:, 0:1]),
                       ve, 1, waits=[t2])
            v3 = kb.op("vector",
                       lambda e, m=mat: e.tensor_scalar(
                           NTCOL[m][:], TCOL[m][:], -1.0, 1.0, ALU.mult, ALU.mult),
                       ve, 1, waits=[v2])
            st["s_done"][mat] = v3
            return a1, v2

        def emit_passB(nmax):
            for _ in range(nmax):
                if st["Bi"] >= len(passB):
                    return
                mat, k = passB[st["Bi"]]
                if st["s_done"][mat] is None:
                    return
                src, qdst = w_tile(mat, k)
                slot = st["Bi"] % 2
                v = kb.op("sync",
                          lambda e, s=src, sl=slot: e.dma_start(out=wpa_slot(sl),
                                                                in_=s),
                          sy_d, 16, waits=[st["wpa_war"][slot]])
                r1 = kb.op("vector",
                           lambda e, sl=slot, m=mat: e.tensor_scalar(
                               WPT, wpa_slot(sl), TCOL[m][:], -1.0,
                               ALU.is_ge, ALU.add),
                           ve, 1, waits=[v, st["s_done"][mat], st["prev_stt"]])
                r2 = kb.op("vector",
                           lambda e, sl=slot, m=mat: e.scalar_tensor_tensor(
                               wpo_slot(sl), wpa_slot(sl), NTCOL[m][:], WPT,
                               ALU.is_gt, ALU.add),
                           ve, 1, waits=[r1, st["wpo_war"][slot]])
                st["prev_stt"] = r2
                st["wpa_war"][slot] = r2
                wr = kb.op("gpsimd",
                           lambda e, d=qdst, sl=slot: e.dma_start(
                               out=d, in_=wpo_slot(sl)),
                           gp_d, 16, waits=[r2])
                st["wpo_war"][slot] = wr
                (st["wq_wr"] if mat == 0 else st["w2_wr"])[k] = wr
                st["Bi"] += 1

        xf_war = [None, None]
        lnvs_war = [None, None]
        xn_wr = [None] * NT
        for t in range(NT):
            s = t % 2
            xd = kb.op("sync",
                       lambda e, t=t, s=s: e.dma_start(
                           out=xf_slot(s), in_=dx[t * 128:(t + 1) * 128, :]),
                       sy_d, 16, waits=[xf_war[s]])
            v1 = kb.op("vector",
                       lambda e, s=s: e.reduce_sum(lnsum[s], xf_slot(s), axis=AX.X),
                       ve, 1, waits=[xd])
            v2 = kb.op("vector",
                       lambda e, s=s: e.tensor_scalar(
                           lnmean[s], lnsum[s], 1.0 / DM, 1.0, ALU.mult, ALU.mult),
                       ve, 1, waits=[v1])
            v3 = kb.op("vector",
                       lambda e, s=s: e.scalar_tensor_tensor(
                           LNDUMP, xf_slot(s), lnmean[s], xf_slot(s),
                           ALU.subtract, ALU.mult, accum_out=lnvs[s]),
                       ve, 1, waits=[v2, lnvs_war[s]])
            a1 = kb.op("scalar",
                       lambda e, s=s: e.activation(lnln[s], lnvs[s], AF.Ln,
                                                   bias=EPSC[:], scale=1.0 / DM),
                       ae, 1, waits=[v3, init_memset])
            lnvs_war[s] = a1
            a2 = kb.op("scalar",
                       lambda e, s=s: e.activation(lnrstd[s], lnln[s], AF.Exp,
                                                   scale=-0.5),
                       ae, 1, waits=[a1])
            v4 = kb.op("vector",
                       lambda e, s=s: e.tensor_scalar(
                           xnb_slot(s), xf_slot(s), lnmean[s], lnrstd[s],
                           ALU.subtract, ALU.mult),
                       ve, 1, waits=[a2])
            xf_war[s] = v4
            xn_wr[t] = kb.op("gpsimd",
                             lambda e, t=t, s=s: e.dma_start(
                                 out=dxn[t * 128:(t + 1) * 128, :],
                                 in_=xnb_slot(s)),
                             gp_d, 16, waits=[v4])
            emit_passA(6)

        xn_all = xn_wr[NT - 1]

        xnt_all = None
        for dt in range(ND):
            xnt_all = kb.op("scalar",
                            lambda e, dt=dt: e.dma_start(
                                out=XNT[:, dt * SEQ:(dt + 1) * SEQ],
                                in_=dxn[:, dt * 128:(dt + 1) * 128],
                                transpose=True),
                            sc_d, 16, waits=[xn_all])

        pA_read = [None] * 6
        pt_war = [None, None]
        prow_rd = None
        p4_rd = None
        p5_rd = None
        wq_war = wk_war = wv_war = None
        exp_last_prev = None
        otst_war = [None, None]
        otd_wr_last = None
        xnb_last_use = None

        for h in range(H):
            if h < 9:
                emit_passA(20)
            if h == 3 and st["s_done"][0] is None:
                prow_rd, p7_free = finalize_scale(0, prow_rd, p7_free)
            if h == 9 and st["s_done"][1] is None:
                prow_rd, p7_free = finalize_scale(1, prow_rd, p7_free)
            if h >= 4:
                emit_passB(16)

            qcols, kcols, vcols = h * 128, DM + h * 128, 2 * DM + h * 128

            def cast_load(dst, c0):
                return lambda e, d=dst, c=c0: e.dma_start(
                    out=d.rearrange("p (t f) -> p t f", t=ND),
                    in_=dwqkv[:, c:c + 128].rearrange("(t p) f -> p t f", p=128))

            gq = kb.op("gpsimd", cast_load(wqs, qcols), gp_d, 16, waits=[wq_war])
            gk = kb.op("gpsimd", cast_load(wks, kcols), gp_d, 16, waits=[wk_war])
            gv = kb.op("gpsimd", cast_load(wvs, vcols), gp_d, 16, waits=[wv_war])

            qk_evac_last = None
            for pi, (wsl, dst, gld) in enumerate(((wqs, qh, gq), (wks, kh, gk))):
                for c in range(NCH):
                    bank = c % 2
                    for dt in range(ND):
                        w = ([gld, xnt_all, pA_read[bank]] if dt == 0 else [])
                        kb.op("tensor",
                              lambda e, b=bank, ww=wsl, dt=dt, c=c: e.matmul(
                                  P[b][:], ww[:, dt * 128:(dt + 1) * 128],
                                  XNT[:, dt * SEQ + c * CH:dt * SEQ + (c + 1) * CH],
                                  start=(dt == 0), stop=(dt == ND - 1)),
                              pe if dt == ND - 1 else None,
                              1 if dt == ND - 1 else 0, waits=w)
                    mm_done = (pe, pe.n)
                    ev = kb.op("scalar",
                               lambda e, b=bank, d=dst, c=c: e.copy(
                                   d[:, c * CH:(c + 1) * CH], P[b][:]),
                               ae, 1, waits=[mm_done])
                    pA_read[bank] = ev
                    qk_evac_last = ev
                if pi == 0:
                    wq_war = (pe, pe.n)
                else:
                    wk_war = (pe, pe.n)

            anch_rounds = []
            v_evac_last = None
            for g in range(4):
                bank = g % 2
                for ttl in range(4):
                    tok = 4 * g + ttl
                    for dt in range(ND):
                        w = ([gv, pA_read[bank]] if (dt == 0 and ttl == 0) else [])
                        kb.op("tensor",
                              lambda e, b=bank, tok=tok, dt=dt, ttl=ttl: e.matmul(
                                  P[b][:, ttl * 128:(ttl + 1) * 128],
                                  XNT[:, dt * SEQ + tok * 128:dt * SEQ + (tok + 1) * 128],
                                  wvs[:, dt * 128:(dt + 1) * 128],
                                  start=(dt == 0), stop=(dt == ND - 1)),
                              pe if (dt == ND - 1 and ttl == 3) else None,
                              1 if (dt == ND - 1 and ttl == 3) else 0, waits=w)
                vg_done = (pe, pe.n)
                v_evac_last = kb.op("scalar",
                                    lambda e, b=bank, g=g: e.copy(
                                        vh[:, g * CH:(g + 1) * CH], P[b][:]),
                                    ae, 1, waits=[vg_done])
                pA_read[bank] = v_evac_last

                for rr in range(4):
                    qt = 4 * g + rr
                    w = ([qk_evac_last, p7_free] if rr == 0 else [])
                    kb.op("tensor",
                          lambda e, qt=qt, rr=rr: e.matmul(
                              P7[:, rr * 128:(rr + 1) * 128],
                              qh[:, qt * 128:(qt + 1) * 128],
                              kh[:, qt * 128:(qt + 1) * 128],
                              start=True, stop=True),
                          pe if rr == 3 else None, 1 if rr == 3 else 0, waits=w)
                diag_done = (pe, pe.n)
                ar = kb.op("vector",
                           lambda e, g=g: e.reduce_max(ANCH[:, g:g + 1], P7[:],
                                                       axis=AX.X, negate=True),
                           ve, 1, waits=[diag_done])
                anch_rounds.append(ar)
                p7_free = ar
            wv_war = (pe, pe.n)

            am = kb.op("vector",
                       lambda e: e.tensor_reduce(ANCM[:], ANCH[:], axis=AX.X,
                                                 op=ALU.min),
                       ve, 1, waits=anch_rounds)
            tr = kb.op("tensor",
                       lambda e: e.transpose(PROW[0:1, 0:128], ANCM[:], IDENT[:]),
                       pe, 1, waits=[am, prow_rd])
            nm = kb.op("vector",
                       lambda e: e.tensor_reduce(NM1[:], PROW[0:1, 0:128],
                                                 axis=AX.X, op=ALU.min),
                       ve, 1, waits=[tr])
            prow_rd = nm
            nm2 = kb.op("vector",
                        lambda e: e.tensor_scalar(NM2[:], NM1[:], SCL, 1.0,
                                                  ALU.mult, ALU.mult),
                        ve, 1, waits=[nm])
            bc = kb.op("tensor",
                       lambda e: e.matmul(P7[:, 256:257], ONES1[:], NM2[:],
                                          start=True, stop=True),
                       pe, 1, waits=[nm2, p7_free])
            negm_ready = kb.op("vector",
                               lambda e: e.tensor_copy(NEGM[:], P7[:, 256:257]),
                               ve, 1, waits=[bc, exp_last_prev])
            p7_free = negm_ready

            exp_last = None
            for c in range(NCH):
                njt = 4 * c + 4
                for j in range(njt):
                    sbank = 2 + (j % 2)
                    smm = kb.op("tensor",
                                lambda e, sbk=sbank, j=j, c=c: e.matmul(
                                    P[sbk][:], kh[:, j * 128:(j + 1) * 128],
                                    qh[:, c * CH:(c + 1) * CH],
                                    start=True, stop=True),
                                pe, 1, waits=[qk_evac_last, pA_read[sbank]])
                    exp_dep = smm
                    if j >= 4 * c:
                        a0 = 384 - 128 * (j - 4 * c)
                        exp_dep = kb.op("vector",
                                        lambda e, sbk=sbank, a0=a0: e.tensor_tensor(
                                            P[sbk][:], P[sbk][:],
                                            MASKS[:, a0:a0 + 512], ALU.add),
                                        ve, 1, waits=[smm, init_loads])
                    ex = kb.op("scalar",
                               lambda e, sbk=sbank, j=j: e.activation(
                                   pt_slot(j % 2), P[sbk][:], AF.Exp,
                                   bias=NEGM[:], scale=SCL),
                               ae, 1, waits=[exp_dep, negm_ready, pt_war[j % 2]])
                    pA_read[sbank] = ex
                    exp_last = ex
                    kb.op("tensor",
                          lambda e, j=j, njt=njt: e.matmul(
                              PROW[:], ONESCB[:], pt_slot(j % 2),
                              start=(j == 0), stop=(j == njt - 1)),
                          pe if j == njt - 1 else None,
                          1 if j == njt - 1 else 0,
                          waits=([ex, prow_rd, init_memset] if j == 0 else [ex]))
                    if j == njt - 1:
                        sums_stop = (pe, pe.n)
                    pvw = ([ex, p4_rd] if j == 0 else [ex])
                    pv = kb.op("tensor",
                               lambda e, j=j, njt=njt: e.matmul(
                                   P[4][:], vh[:, j * 128:(j + 1) * 128],
                                   pt_slot(j % 2),
                                   start=(j == 0), stop=(j == njt - 1)),
                               pe, 1, waits=pvw)
                    pt_war[j % 2] = pv
                pv_stop = pv
                rc = kb.op("vector",
                           lambda e: e.reciprocal(SREC[:], PROW[:]),
                           ve, 1, waits=[sums_stop])
                prow_rd = rc
                bcm = kb.op("tensor",
                            lambda e: e.matmul(P[5][:], ONES1[:], SREC[:],
                                               start=True, stop=True),
                            pe, 1, waits=[rc, p5_rd])
                rcp = kb.op("vector",
                            lambda e: e.tensor_copy(RECIPB, P[5][:]),
                            ve, 1, waits=[bcm])
                p5_rd = rcp
                om = kb.op("vector",
                           lambda e, c=c: e.tensor_tensor(
                               OTST[:, (c % 2) * 512:(c % 2 + 1) * 512],
                               P[4][:], RECIPB, ALU.mult),
                           ve, 1, waits=[pv_stop, rcp, otst_war[c % 2]])
                p4_rd = om
                xnb_last_use = om
                otd_wr_last = kb.op("gpsimd",
                                    lambda e, h=h, c=c: e.dma_start(
                                        out=dotd[h * 128:(h + 1) * 128,
                                                 c * CH:(c + 1) * CH],
                                        in_=OTST[:, (c % 2) * 512:(c % 2 + 1) * 512]),
                                    gp_d, 16, waits=[om])
                otst_war[c % 2] = otd_wr_last
            exp_last_prev = exp_last

        attn_pe_done = (pe, pe.n)

        woutp_war = [None, None]
        otst2_war = [None, None]
        sst_war = [None, None]
        pB_read = [pA_read[0], pA_read[1]]
        sd_wr_last = None

        def emit_otl(tt):
            osl = tt % 2
            return kb.op("scalar",
                         lambda e, tt=tt, o=osl: e.dma_start(
                             out=otst2_slot(o).rearrange("p (t c) -> p t c", t=ND),
                             in_=dotd[:, tt * 128:(tt + 1) * 128].rearrange(
                                 "(t p) c -> p t c", p=128)),
                         sc_d, 16, waits=[otd_wr_last, otst2_war[osl]])

        otl_next = emit_otl(0)
        for dmc in range(4):
            wsl = dmc % 2
            wout_ld = kb.op(
                "gpsimd",
                lambda e, dmc=dmc, w=wsl: e.dma_start(
                    out=woutp_slot(w).rearrange("p (t m) -> p t m", t=ND),
                    in_=dwout[:, dmc * CH:(dmc + 1) * CH].rearrange(
                        "(t p) m -> p t m", p=128)),
                gp_d, 16,
                waits=[woutp_war[wsl], xf_war[0], xf_war[1]])
            for tt in range(NT):
                osl = tt % 2
                otl = otl_next
                if not (dmc == 3 and tt == NT - 1):
                    otl_next = emit_otl((tt + 1) % NT)
                bank = (dmc * NT + tt) % 2
                for hh in range(H):
                    w = ([otl, wout_ld, pB_read[bank], attn_pe_done]
                         if hh == 0 else [])
                    kb.op("tensor",
                          lambda e, b=bank, o=osl, hh=hh, ww=wsl: e.matmul(
                              P[b][:], otst2_slot(o)[:, hh * 128:(hh + 1) * 128],
                              woutp_slot(ww)[:, hh * CH:(hh + 1) * CH],
                              start=(hh == 0), stop=(hh == H - 1)),
                          pe if hh == H - 1 else None, 1 if hh == H - 1 else 0,
                          waits=w)
                smm = (pe, pe.n)
                otst2_war[osl] = smm
                woutp_war[wsl] = smm
                ev = kb.op("scalar",
                           lambda e, b=bank, o=osl: e.copy(
                               SST[:, o * 512:(o + 1) * 512], P[b][:]),
                           ae, 1, waits=[smm, sst_war[osl]])
                pB_read[bank] = ev
                sd_wr_last = kb.op("gpsimd",
                                   lambda e, tt=tt, dmc=dmc, o=osl: e.dma_start(
                                       out=dsd[tt * 128:(tt + 1) * 128,
                                               dmc * CH:(dmc + 1) * CH],
                                       in_=SST[:, o * 512:(o + 1) * 512]),
                                   gp_d, 16, waits=[ev])
                sst_war[osl] = sd_wr_last
            emit_passB(40)
        emit_passB(len(passB))
        assert st["Bi"] == len(passB), "passB incomplete"
        phaseB_pe_done = (pe, pe.n)

        b2b_ready = None
        for dmc in range(4):
            bm = kb.op("tensor",
                       lambda e, dmc=dmc: e.matmul(
                           P7[:], ONES1[:], B2ROW[:, dmc * CH:(dmc + 1) * CH],
                           start=True, stop=True),
                       pe, 1, waits=[p7_free, init_loads, attn_pe_done,
                                     b2b_ready])
            b2b_ready = kb.op("vector",
                              lambda e, dmc=dmc: e.tensor_copy(
                                  B2B[:, dmc * 512:(dmc + 1) * 512], P7[:]),
                              ve, 1, waits=[bm])
            p7_free = b2b_ready

        pC_h_read = list(pB_read)
        pC_g_read = [None, None, None, None]
        w1p_war = [None, None]
        xnc_war = None
        w2m_war = [None, None, None, None]
        g_war = [None, None]
        st5_war = [None, None]
        xc_war = [None, None]
        zt_war = [None, None]
        ht_war = attn_pe_done
        y_last = None

        for c in range(NCH):
            xnc_all = None
            for dt in range(ND):
                xnc_all = kb.op("scalar",
                                lambda e, c=c, dt=dt: e.dma_start(
                                    out=xnc_d(dt),
                                    in_=dxn[c * CH:(c + 1) * CH,
                                            dt * 128:(dt + 1) * 128],
                                    transpose=True),
                                sc_d, 16,
                                waits=[xnc_war,
                                       phaseB_pe_done if c == 0 else None,
                                       xn_all])
            last_gelu_of_f = {}

            def emit_panel(fc, c=c):
                psl = fc % 2
                kneed = min(127, (fc + 1) * 8 - 1)
                pl = None
                for dt in range(ND):
                    pl = kb.op("scalar",
                               lambda e, fc=fc, dt=dt, p=psl: e.dma_start(
                                   out=w1p_slot(p)[:, dt * 512:(dt + 1) * 512],
                                   in_=dw1q[fc * 512:(fc + 1) * 512,
                                            dt * 128:(dt + 1) * 128],
                                   transpose=True),
                               sc_d, 16,
                               waits=[st["wq_wr"][kneed], w1p_war[psl],
                                      phaseB_pe_done if (c == 0 and fc < 2)
                                      else None])
                return pl

            pl_pend = emit_panel(0)
            for fc in range(16):
                psl = fc % 2
                pl_all = pl_pend
                if fc + 1 < 16:
                    pl_pend = emit_panel(fc + 1)
                for fl in range(4):
                    f = fc * 4 + fl
                    bank = f % 2
                    for dt in range(ND):
                        w = ([pl_all, xnc_all, pC_h_read[bank]] if dt == 0 else [])
                        kb.op("tensor",
                              lambda e, b=bank, p=psl, dt=dt, fl=fl: e.matmul(
                                  P[b][:],
                                  w1p_slot(p)[:, dt * 512 + fl * 128:
                                              dt * 512 + (fl + 1) * 128],
                                  xnc_d(dt),
                                  start=(dt == 0), stop=(dt == ND - 1)),
                              pe if dt == ND - 1 else None,
                              1 if dt == ND - 1 else 0, waits=w)
                    hmm_done = (pe, pe.n)
                    w1p_war[psl] = hmm_done
                    gl = kb.op("scalar",
                               lambda e, b=bank, f=f: e.activation(
                                   XNT[:, f * 512:(f + 1) * 512], P[b][:],
                                   AF.Gelu, bias=B1C[:, f:f + 1]),
                               ae, 1, waits=[hmm_done, b1c_ready, ht_war])
                    pC_h_read[bank] = gl
                    last_gelu_of_f[f] = gl
            xnc_war = (pe, pe.n)

            for dmc in range(4):
                for f in range(NF):
                    msl = f % 4
                    kneed = ((dmc + 1) * 4 - 1) * 8 + (f // 8)
                    w2ld = kb.op("scalar",
                                 lambda e, dmc=dmc, f=f, m=msl: e.dma_start(
                                     out=w2m_slot(m),
                                     in_=dw2q[dmc * CH:(dmc + 1) * CH,
                                              f * 128:(f + 1) * 128],
                                     transpose=True),
                                 sc_d, 16,
                                 waits=[st["w2_wr"][kneed], w2m_war[msl],
                                        xnb_last_use, attn_pe_done])
                    for tt in range(4):
                        gbank = 2 + tt
                        w = [w2ld, last_gelu_of_f[f]]
                        if f == 0:
                            w.append(pC_g_read[tt])
                        kb.op("tensor",
                              lambda e, gb=gbank, f=f, tt=tt, m=msl: e.matmul(
                                  P[gb][:], XNT[:, f * 512 + tt * 128:
                                                f * 512 + (tt + 1) * 128],
                                  w2m_slot(m),
                                  start=(f == 0), stop=(f == NF - 1)),
                              pe if (f % 4 == 3 and tt == 3) else None,
                              1 if (f % 4 == 3 and tt == 3) else 0, waits=w)
                    if f % 4 == 3:
                        w2m_war[msl] = (pe, pe.n)
                        for mq in range(3):
                            w2m_war[mq] = (pe, pe.n)
                g_stop = (pe, pe.n)
                gpar = dmc % 2
                for tt in range(4):
                    gbank = 2 + tt
                    zb = kb.op("vector",
                               lambda e, gb=gbank, dmc=dmc, tt=tt: e.tensor_tensor(
                                   zt_slot(tt % 2), P[gb][:],
                                   B2B[:, dmc * 512:(dmc + 1) * 512], ALU.add),
                               ve, 1, waits=[g_stop, b2b_ready, zt_war[tt % 2]])
                    sl = kb.op("scalar",
                               lambda e, gp=gpar, tt=tt: e.activation(
                                   g_slice(gp, tt), zt_slot(tt % 2), AF.Silu),
                               ae, 1, waits=[zb, g_war[gpar]])
                    zt_war[tt % 2] = sl
                    pC_g_read[tt] = sl
                for tt in range(4):
                    trow = c * 4 + tt
                    k2 = (dmc * 4 + tt) % 2
                    stld = kb.op("gpsimd",
                                 lambda e, trow=trow, dmc=dmc, s=k2: e.dma_start(
                                     out=st5_slot(s),
                                     in_=dsd[trow * 128:(trow + 1) * 128,
                                             dmc * CH:(dmc + 1) * CH]),
                                 gp_d, 16,
                                 waits=[sd_wr_last, st5_war[k2],
                                        st["wpa_war"][1], st["prev_stt"]])
                    xld = kb.op("sync",
                                lambda e, trow=trow, dmc=dmc, s=k2: e.dma_start(
                                    out=xc_slot(s),
                                    in_=dx[trow * 128:(trow + 1) * 128,
                                           dmc * CH:(dmc + 1) * CH]),
                                sy_d, 16,
                                waits=[xc_war[k2], st["wpa_war"][0],
                                       st["wpa_war"][1], st["prev_stt"]])
                    sg = kb.op("vector",
                               lambda e, s=k2, gp=gpar, tt=tt: e.tensor_tensor(
                                   st5_slot(s), st5_slot(s), g_slice(gp, tt),
                                   ALU.mult),
                               ve, 1, waits=[stld, pC_g_read[tt]])
                    g_war[gpar] = sg
                    ad = kb.op("vector",
                               lambda e, s=k2: e.tensor_tensor(
                                   xc_slot(s), xc_slot(s), st5_slot(s), ALU.add),
                               ve, 1, waits=[sg, xld])
                    st5_war[k2] = ad
                    y_last = kb.op("sync",
                                   lambda e, trow=trow, dmc=dmc, s=k2: e.dma_start(
                                       out=dy[trow * 128:(trow + 1) * 128,
                                              dmc * CH:(dmc + 1) * CH],
                                       in_=xc_slot(s)),
                                   sy_d, 16, waits=[ad])
                    xc_war[k2] = y_last
                ht_war = g_stop

        kb.op("sync", lambda e: e.nop(), waits=[y_last])

        for s_ in kb.sems:
            s_.h = E(nc.semaphore(s_.name))
        block = E(nc.Block())

        @block.sync
        def _(e):
            kb.realize("sync", e)

        @block.scalar
        def _(e):
            kb.realize("scalar", e)

        @block.vector
        def _(e):
            kb.realize("vector", e)

        @block.tensor
        def _(e):
            kb.realize("tensor", e)

        @block.gpsimd
        def _(e):
            kb.realize("gpsimd", e)

    return nc


_NC_CACHE = None


def kernel(L0_attn_x, L0_attn_w_qkv, L0_attn_w_out, L0_ffn_w1, L0_ffn_b1,
           L0_ffn_w2, L0_ffn_b2):
    global _NC_CACHE
    x = np.ascontiguousarray(np.asarray(L0_attn_x, dtype=np.float32))
    wqkv = np.ascontiguousarray(np.asarray(L0_attn_w_qkv, dtype=np.float32))
    wout = np.ascontiguousarray(np.asarray(L0_attn_w_out, dtype=np.float32))
    w1 = np.ascontiguousarray(np.asarray(L0_ffn_w1, dtype=np.float32))
    b1 = np.ascontiguousarray(np.asarray(L0_ffn_b1, dtype=np.float32)).reshape(NF, 128)
    w2 = np.ascontiguousarray(np.asarray(L0_ffn_w2, dtype=np.float32))
    b2 = np.ascontiguousarray(np.asarray(L0_ffn_b2, dtype=np.float32)).reshape(1, DM)
    ident = np.eye(128, dtype=np.float32)
    ones1 = np.ones((1, 128), dtype=np.float32)
    masks = build_masks()

    if _NC_CACHE is None:
        _NC_CACHE = build_nc()
    nc = _NC_CACHE

    in_maps = []
    for i in range(N_CORES):
        in_maps.append({
            "x": np.ascontiguousarray(x[i]),
            "wqkv": wqkv, "wout": wout, "w1": w1, "b1": b1,
            "w2": w2, "b2": b2, "ident": ident, "ones1": ones1,
            "masks": masks,
        })
    res = run_bass_kernel_spmd(nc, in_maps, core_ids=list(range(N_CORES)))
    out = np.stack([np.asarray(r["y"], dtype=np.float32) for r in res.results])
    return out


# version 05
# speedup vs baseline: 1.3476x; 1.3476x over previous
import sys
sys.path.insert(0, "/opt/trn_rl_repo")
from contextlib import ExitStack
import numpy as np
import ml_dtypes

import concourse.bass as bass
import concourse.mybir as mybir
from concourse.bass_utils import run_bass_kernel_spmd

F32 = mybir.dt.float32
BF16 = mybir.dt.bfloat16
AF = mybir.ActivationFunctionType
ALU = mybir.AluOpType
AX = mybir.AxisListType

SEQ = 2048
DM = 2048
H = 16
DFF = 8192
NT = 16
ND = 16
NF = 64
NCH = 4
CH = 512
SCL = float(1.0 / np.sqrt(128.0))
NEG = -1e10
N_CORES = 8

ENGS = ("sync", "scalar", "vector", "tensor", "gpsimd")


class Sem:
    __slots__ = ("name", "n", "h")

    def __init__(self, name):
        self.name = name
        self.n = 0
        self.h = None


class KB:
    def __init__(self):
        self.ops = {e: [] for e in ENGS}
        self.sems = []
        self.guar = {e: {} for e in ENGS}

    def sem(self, name):
        s = Sem(name)
        self.sems.append(s)
        return s

    def wait(self, eng, sm, val):
        if sm is None or val <= 0:
            return
        if self.guar[eng].get(sm, 0) >= val:
            return
        self.guar[eng][sm] = val
        self.ops[eng].append(("w", sm, val))

    def op(self, eng, fn, sm=None, inc=0, waits=()):
        for w in waits:
            if w is not None:
                self.wait(eng, w[0], w[1])
        if sm is not None:
            sm.n += inc
            self.ops[eng].append(("o", fn, sm, inc))
            return (sm, sm.n)
        self.ops[eng].append(("o", fn, None, 0))
        return None

    def realize(self, eng, h):
        for item in self.ops[eng]:
            if item[0] == "w":
                h.wait_ge(item[1].h, item[2])
            else:
                inst = item[1](h)
                if item[2] is not None:
                    inst.then_inc(item[2].h, item[3])


def build_masks():
    kl = np.arange(128)[:, None]
    x = np.arange(896)[None, :]
    m = np.where(x >= kl + 384, 0.0, NEG).astype(np.float32)
    return np.asarray(m.astype(ml_dtypes.bfloat16))


DBG_SCRATCH = False


def build_nc():
    nc = bass.Bass("TRN2", target_bir_lowering=False)

    dx = nc.dram_tensor("x", [SEQ, DM], F32, kind="ExternalInput").ap()
    dwqkv = nc.dram_tensor("wqkv", [DM, 3 * DM], F32, kind="ExternalInput").ap()
    dwout = nc.dram_tensor("wout", [DM, DM], F32, kind="ExternalInput").ap()
    dw1 = nc.dram_tensor("w1", [DFF, DM], F32, kind="ExternalInput").ap()
    db1 = nc.dram_tensor("b1", [NF, 128], F32, kind="ExternalInput").ap()
    dw2 = nc.dram_tensor("w2", [DM, DFF], F32, kind="ExternalInput").ap()
    db2 = nc.dram_tensor("b2", [1, DM], F32, kind="ExternalInput").ap()
    dident = nc.dram_tensor("ident", [128, 128], F32, kind="ExternalInput").ap()
    didentb = nc.dram_tensor("identb", [128, 128], BF16, kind="ExternalInput").ap()
    dones1 = nc.dram_tensor("ones1", [1, 128], F32, kind="ExternalInput").ap()
    dmasks = nc.dram_tensor("masks", [128, 896], BF16, kind="ExternalInput").ap()
    dy = nc.dram_tensor("y", [SEQ, DM], F32, kind="ExternalOutput").ap()

    _SK = "ExternalOutput" if DBG_SCRATCH else "Internal"
    dbg_ht = (nc.dram_tensor("dbg_ht", [128, 4 * NF * CH], BF16,
                             kind="ExternalOutput").ap() if DBG_SCRATCH else None)
    dbg_g = (nc.dram_tensor("dbg_g", [128, 4 * 4 * 2048], BF16,
                            kind="ExternalOutput").ap() if DBG_SCRATCH else None)
    dxnT = nc.dram_tensor("dxnT", [128, ND * SEQ], BF16, kind=_SK).ap()
    dotd = nc.dram_tensor("dotd", [128, H * SEQ], BF16, kind=_SK).ap()
    dsd = nc.dram_tensor("sd", [SEQ, DM], BF16, kind=_SK).ap()
    dw1t = nc.dram_tensor("w1t", [128, ND * DFF], BF16, kind=_SK).ap()
    dw2t = nc.dram_tensor("w2t", [128, NF * DM], BF16, kind=_SK).ap()

    kb = KB()
    pe = kb.sem("pe")
    ve = kb.sem("ve")
    ae = kb.sem("ae")
    ge = kb.sem("ge")
    s_init = kb.sem("s_init")
    s_xf = [kb.sem("s_xf0"), kb.sem("s_xf1")]
    s_qin = [kb.sem("s_qin0"), kb.sem("s_qin1")]
    s_w1t = [kb.sem("s_w1t0"), kb.sem("s_w1t1")]
    s_w2t = [kb.sem("s_w2t0"), kb.sem("s_w2t1")]
    s_xntw = kb.sem("s_xntw")
    s_otd = kb.sem("s_otd")
    s_sd = [kb.sem("s_sd0"), kb.sem("s_sd1")]
    s_wq = [kb.sem("s_wq0"), kb.sem("s_wq1")]
    s_wk = [kb.sem("s_wk0"), kb.sem("s_wk1")]
    s_wv = [kb.sem("s_wv0"), kb.sem("s_wv1")]
    s_wop = [kb.sem("s_wop0"), kb.sem("s_wop1")]
    s_ot2 = [kb.sem("s_ot20"), kb.sem("s_ot21")]
    s_xnc = kb.sem("s_xnc")
    s_w1p = [kb.sem("s_w1p0"), kb.sem("s_w1p1")]
    s_w2m = [kb.sem(f"s_w2m{i}") for i in range(4)]
    s_st5 = [kb.sem("s_st50"), kb.sem("s_st51")]
    s_xc = [kb.sem("s_xc0"), kb.sem("s_xc1")]
    s_y = [kb.sem("s_y0"), kb.sem("s_y1")]

    with ExitStack() as ctx:
        E = ctx.enter_context
        sb, ps = nc.sbuf_tensor, nc.psum_tensor

        XNT = E(sb("XNT", [128, ND * SEQ], BF16))
        BIGB = E(sb("BIGB", [128, 24576], BF16))
        QKVB = E(sb("QKVB", [128, 3 * SEQ], BF16))
        WQB = E(sb("WQB", [128, 2 * 3 * DM], BF16))
        XNB = E(sb("XNB", [128, 4096], BF16))
        WP = E(sb("WP", [128, 8192], BF16))
        MASKS = E(sb("MASKS", [128, 896], BF16))
        IDENT = E(sb("IDENT", [128, 128], F32))
        IDENTB = E(sb("IDENTB", [128, 128], BF16))
        ONES1 = E(sb("ONES1", [1, 128], F32))
        ONESCB = E(sb("ONESCB", [128, 1], BF16))
        ONESCF = E(sb("ONESCF", [128, 1], F32))
        EPSC = E(sb("EPSC", [128, 1], F32))
        B1RS = E(sb("B1RS", [64, 128], F32))
        B1C = E(sb("B1C", [128, 64], F32))
        B2ROW = E(sb("B2ROW", [1, DM], F32))
        LNS = E(sb("LNS", [128, 12], F32))
        WACC = E(sb("WACC", [128, 128], F32))
        ANCH = E(sb("ANCH", [128, 4], F32))
        ANCM = E(sb("ANCM", [128, 1], F32))
        NEGM = E(sb("NEGM", [128, 1], F32))
        NM1 = E(sb("NM1", [1, 1], F32))
        NM2 = E(sb("NM2", [1, 1], F32))
        SRL = E(sb("SRL", [1, CH], F32))
        SREC = E(sb("SREC", [1, CH], F32))
        SCLB = [E(sb("SCL1", [1, 1], F32)), E(sb("SCL2", [1, 1], F32))]
        TCOL = [E(sb("TC1", [128, 1], F32)), E(sb("TC2", [128, 1], F32))]
        NTCOL = [E(sb("NTC1", [128, 1], F32)), E(sb("NTC2", [128, 1], F32))]
        WSUM = [E(sb("WS1", [128, 1], F32)), E(sb("WS2", [128, 1], F32))]

        P = [E(ps(f"P{i}", [128, 512], F32)) for i in range(6)]
        PROW = E(ps("PROW", [1, 512], F32))
        P7 = E(ps("P7", [128, 512], F32))

        def xf_slot(i):
            return BIGB[:, i * 4096:(i + 1) * 4096].bitcast(F32)

        def xnb_slot(i):
            return BIGB[:, 8192 + i * 2048:8192 + (i + 1) * 2048]

        LNDUMP = BIGB[:, 12288:16384].bitcast(F32)

        def qpanel_slot(i):
            return BIGB[:, i * 8192:(i + 1) * 8192]

        def qin_slot(i):
            return BIGB[:, 16384 + i * 4096:16384 + (i + 1) * 4096].bitcast(F32)

        def w1p_slot(i):
            return BIGB[:, i * 8192:(i + 1) * 8192]

        XNC = BIGB[:, 16384:24576]

        def xnc_d(dt):
            return BIGB[:, 16384 + dt * 512:16384 + (dt + 1) * 512]

        def woutp_slot(i):
            return XNT[:, i * 8192:(i + 1) * 8192]

        def otst2_slot(i):
            return XNT[:, 16384 + i * 2048:16384 + (i + 1) * 2048]

        qh = QKVB[:, 0:SEQ]
        kh = QKVB[:, SEQ:2 * SEQ]
        vh = QKVB[:, 2 * SEQ:3 * SEQ]

        def wq_s(i):
            return WQB[:, i * 6144 + 0:i * 6144 + DM]

        def wk_s(i):
            return WQB[:, i * 6144 + DM:i * 6144 + 2 * DM]

        def wv_s(i):
            return WQB[:, i * 6144 + 2 * DM:i * 6144 + 3 * DM]

        B2B = WQB[:, 0:4096].bitcast(F32)

        def zt_slot(i):
            return WQB[:, 4096 + i * 1024:4096 + (i + 1) * 1024].bitcast(F32)

        def g_slice(par, tt):
            return WQB[:, 6144 + par * 2048 + tt * 512:
                       6144 + par * 2048 + (tt + 1) * 512]

        _PT_OFF = [0, 512, 3072]

        def pt_slot(i):
            return XNB[:, _PT_OFF[i]:_PT_OFF[i] + 512]

        RECIPB = XNB[:, 1024:2048].bitcast(F32)

        def otst_slot(i):
            return XNB[:, 2048 + i * 512:2048 + (i + 1) * 512]

        def sst_slot(i):
            return XNB[:, i * 512:(i + 1) * 512]

        def w2m_dslot(i):
            return XNB[:, i * 1024:(i + 1) * 1024]

        def qout_slot(i):
            return WP[:, i * 2048:(i + 1) * 2048]

        QTMP = WP[:, 4096:8192].bitcast(F32)

        def xc_slot(i):
            return WP[:, i * 1024:(i + 1) * 1024].bitcast(F32)

        def st5_slot(i):
            return WP[:, 2048 + i * 512:2048 + (i + 1) * 512]

        lnsum = [LNS[:, 0:1], LNS[:, 1:2]]
        lnmean = [LNS[:, 2:3], LNS[:, 3:4]]
        lnvs = [LNS[:, 4:5], LNS[:, 5:6]]
        lnln = [LNS[:, 6:7], LNS[:, 7:8]]
        lnrstd = [LNS[:, 8:9], LNS[:, 9:10]]

        for dst, src in ((IDENT[:], dident), (IDENTB[:], didentb),
                         (ONES1[:], dones1), (MASKS[:], dmasks),
                         (B1RS[:], db1), (B2ROW[:], db2)):
            kb.op("sync", lambda e, d=dst, s=src: e.dma_start(out=d, in_=s),
                  s_init, 16)
        init_loads = (s_init, s_init.n)

        kb.op("gpsimd", lambda e: e.memset(ONESCB[:], 1.0), ge, 1)
        kb.op("gpsimd", lambda e: e.memset(ONESCF[:], 1.0), ge, 1)
        kb.op("gpsimd", lambda e: e.memset(EPSC[:], 1e-5), ge, 1)
        init_memset = (ge, ge.n)

        b1t = kb.op("tensor", lambda e: e.transpose(P7[:, 0:64], B1RS[:],
                                                    IDENT[0:64, 0:64]),
                    pe, 1, waits=[init_loads])
        b1c_ready = kb.op("vector", lambda e: e.tensor_copy(B1C[:], P7[:, 0:64]),
                          ve, 1, waits=[b1t])

        units = [(0, dh, ch) for dh in range(ND) for ch in range(4)] + \
                [(1, fh, 0) for fh in range(NF)]

        def unit_src(mat, pnl, ch):
            if mat == 0:
                return dw1[ch * 2048:(ch + 1) * 2048,
                           pnl * 128:(pnl + 1) * 128].rearrange(
                    "(t p) d -> p t d", p=128)
            return dw2[:, pnl * 128:(pnl + 1) * 128].rearrange(
                "(t p) d -> p t d", p=128)

        st = {
            "Ai": 0, "Bi": 0,
            "qin_war": [None, None],
            "qout_war": [None, None],
            "qpanel_war": [None, None],
            "w1A_done": None, "w2A_done": None,
            "s_done": [None, None],
            "prev_r": None,
            "w1t_done0": None, "w1t_done1": None,
            "w2t_done0": None, "w2t_done1": None,
            "qtrans_bank_read": [None, None],
            "p7f": None,
        }
        st["p7f"] = b1c_ready

        def emit_passA(nmax):
            for _ in range(nmax):
                if st["Ai"] >= len(units):
                    return
                mat, pnl, ch = units[st["Ai"]]
                slot = st["Ai"] % 2
                col = st["Ai"]
                v = kb.op("sync",
                          lambda e, m=mat, p=pnl, c=ch, sl=slot: e.dma_start(
                              out=qin_slot(sl).rearrange("p (t d) -> p t d", t=16),
                              in_=unit_src(m, p, c)),
                          s_qin[slot], 16, waits=[st["qin_war"][slot]])
                r = kb.op("vector",
                          lambda e, sl=slot, c=col: e.reduce_sum(
                              WACC[:, c:c + 1], qin_slot(sl), axis=AX.X,
                              apply_absolute_value=True),
                          ve, 1, waits=[v])
                st["qin_war"][slot] = r
                st["Ai"] += 1
                if st["Ai"] == 64:
                    st["w1A_done"] = r
                if st["Ai"] == 128:
                    st["w2A_done"] = r

        def finalize_scale(mat, prow_rd):
            a_done = st["w1A_done"] if mat == 0 else st["w2A_done"]
            assert a_done is not None
            v1 = kb.op("vector",
                       lambda e, m=mat: e.tensor_reduce(
                           WSUM[m][:], WACC[:, m * 64:(m + 1) * 64],
                           axis=AX.X, op=ALU.add),
                       ve, 1, waits=[a_done])
            t1 = kb.op("tensor",
                       lambda e, m=mat: e.matmul(PROW[0:1, 0:1], ONESCF[:],
                                                 WSUM[m][:], start=True, stop=True),
                       pe, 1, waits=[v1, init_memset, prow_rd])
            a1 = kb.op("scalar",
                       lambda e, m=mat: e.activation(SCLB[m][:], PROW[0:1, 0:1],
                                                     AF.Copy,
                                                     scale=0.7 / float(DFF * DM)),
                       ae, 1, waits=[t1])
            t2 = kb.op("tensor",
                       lambda e, m=mat: e.matmul(P7[:, 0:1], ONES1[:], SCLB[m][:],
                                                 start=True, stop=True),
                       pe, 1, waits=[a1, st["p7f"], init_loads])
            v2 = kb.op("vector",
                       lambda e, m=mat: e.tensor_copy(TCOL[m][:], P7[:, 0:1]),
                       ve, 1, waits=[t2])
            v3 = kb.op("vector",
                       lambda e, m=mat: e.tensor_scalar(
                           NTCOL[m][:], TCOL[m][:], -1.0, 1.0, ALU.mult, ALU.mult),
                       ve, 1, waits=[v2])
            st["s_done"][mat] = v3
            st["p7f"] = v2
            return a1

        def emit_passB(nmax, pA_read):
            for _ in range(nmax):
                if st["Bi"] >= len(units):
                    return
                mat, pnl, ch = units[st["Bi"]]
                if st["s_done"][mat] is None:
                    return
                ui = st["Bi"]
                islot = ui % 2
                oslot = ui % 2
                pslot = pnl % 2
                v = kb.op("sync",
                          lambda e, m=mat, p=pnl, c=ch, sl=islot: e.dma_start(
                              out=qin_slot(sl).rearrange("p (t d) -> p t d", t=16),
                              in_=unit_src(m, p, c)),
                          s_qin[islot], 16, waits=[st["qin_war"][islot]])
                r1 = kb.op("vector",
                           lambda e, sl=islot, m=mat: e.tensor_scalar(
                               QTMP, qin_slot(sl), TCOL[m][:], -1.0,
                               ALU.is_ge, ALU.add),
                           ve, 1, waits=[v, st["s_done"][mat], st["prev_r"]])
                r2 = kb.op("vector",
                           lambda e, sl=islot, o=oslot, m=mat:
                           e.scalar_tensor_tensor(
                               qout_slot(o), qin_slot(sl), NTCOL[m][:], QTMP,
                               ALU.is_gt, ALU.add),
                           ve, 1, waits=[r1, st["qout_war"][oslot]])
                st["prev_r"] = r2
                st["qin_war"][islot] = r2
                evacs = []
                pview = P7.bitcast(BF16)
                for rr in range(2):
                    for j in range(8):
                        w = ([r2, st["p7f"]] if j == 0 else [])
                        kb.op("tensor",
                              lambda e, pv=pview, o=oslot, rr=rr, j=j: e.transpose(
                                  pv[:, j * 128:(j + 1) * 128],
                                  qout_slot(o)[:, (rr * 8 + j) * 128:
                                               (rr * 8 + j + 1) * 128],
                                  IDENTB[:]),
                              pe if j == 7 else None, 1 if j == 7 else 0,
                              waits=w)
                    tdone = (pe, pe.n)
                    base = (ch * 2048 + rr * 1024) if mat == 0 else (rr * 1024)
                    evc = kb.op("scalar",
                                lambda e, pv=pview, psl=pslot, b=base: e.copy(
                                    qpanel_slot(psl)[:, b:b + 1024], pv[:]),
                                ae, 1, waits=[tdone, st["qpanel_war"][pslot]])
                    st["p7f"] = evc
                    evacs.append(evc)
                if mat == 0 and ch == 3:
                    wr = kb.op("gpsimd",
                               lambda e, p=pnl, psl=pslot: e.dma_start(
                                   out=dw1t[:, p * 8192:(p + 1) * 8192],
                                   in_=qpanel_slot(psl)),
                               s_w1t[pslot], 16, waits=evacs)
                    st["qpanel_war"][pslot] = wr
                    st["w1t_done" + str(pslot)] = wr
                elif mat == 1:
                    wr = kb.op("gpsimd",
                               lambda e, p=pnl, psl=pslot: e.dma_start(
                                   out=dw2t[:, p * 2048:(p + 1) * 2048],
                                   in_=qpanel_slot(psl)[:, 0:2048]),
                               s_w2t[pslot], 16, waits=evacs)
                    st["qpanel_war"][pslot] = wr
                    st["w2t_done" + str(pslot)] = wr
                st["qout_war"][oslot] = (pe, pe.n)
                st["Bi"] += 1

        xf_war = [None, None]
        lnvs_war = [None, None]
        xnb_war = [None, None]
        ln_bank_read = [None, None]
        lnt_tile_done = [None] * NT
        lnt_last_evac = None
        for t in range(NT):
            s = t % 2
            xd = kb.op("sync",
                       lambda e, t=t, s=s: e.dma_start(
                           out=xf_slot(s), in_=dx[t * 128:(t + 1) * 128, :]),
                       s_xf[s], 16, waits=[xf_war[s]])
            v1 = kb.op("vector",
                       lambda e, s=s: e.reduce_sum(lnsum[s], xf_slot(s), axis=AX.X),
                       ve, 1, waits=[xd])
            v2 = kb.op("vector",
                       lambda e, s=s: e.tensor_scalar(
                           lnmean[s], lnsum[s], 1.0 / DM, 1.0, ALU.mult, ALU.mult),
                       ve, 1, waits=[v1])
            v3 = kb.op("vector",
                       lambda e, s=s: e.scalar_tensor_tensor(
                           LNDUMP, xf_slot(s), lnmean[s], xf_slot(s),
                           ALU.subtract, ALU.mult, accum_out=lnvs[s]),
                       ve, 1, waits=[v2, lnvs_war[s]])
            a1 = kb.op("scalar",
                       lambda e, s=s: e.activation(lnln[s], lnvs[s], AF.Ln,
                                                   bias=EPSC[:], scale=1.0 / DM),
                       ae, 1, waits=[v3, init_memset])
            lnvs_war[s] = a1
            a2 = kb.op("scalar",
                       lambda e, s=s: e.activation(lnrstd[s], lnln[s], AF.Exp,
                                                   scale=-0.5),
                       ae, 1, waits=[a1])
            v4 = kb.op("vector",
                       lambda e, s=s: e.tensor_scalar(
                           xnb_slot(s), xf_slot(s), lnmean[s], lnrstd[s],
                           ALU.subtract, ALU.mult),
                       ve, 1, waits=[a2, xnb_war[s]])
            xf_war[s] = v4
            for rr in range(2):
                bank = rr
                pview = P[bank].bitcast(BF16)
                for j in range(8):
                    w = ([v4, ln_bank_read[bank]] if j == 0 else [])
                    kb.op("tensor",
                          lambda e, pv=pview, s=s, rr=rr, j=j: e.transpose(
                              pv[:, j * 128:(j + 1) * 128],
                              xnb_slot(s)[:, (rr * 8 + j) * 128:
                                          (rr * 8 + j + 1) * 128],
                              IDENTB[:]),
                          pe if j == 7 else None, 1 if j == 7 else 0, waits=w)
                tdone = (pe, pe.n)
                ev = kb.op("scalar",
                           lambda e, pv=pview, t=t, rr=rr: e.copy(
                               XNT[:, 0:ND * SEQ].rearrange(
                                   "p (d s) -> p d s", d=ND)[
                                   :, rr * 8:(rr + 1) * 8,
                                   t * 128:(t + 1) * 128],
                               pv[:].rearrange("p (d s) -> p d s", d=8)),
                           ae, 1, waits=[tdone])
                ln_bank_read[rr] = ev
                lnt_last_evac = ev
            lnt_tile_done[t] = lnt_last_evac
            xnb_war[s] = lnt_last_evac
            emit_passA(6)

        xnt_ready = lnt_last_evac

        def cast_load(dst, c0):
            return lambda e, d=dst, c=c0: e.dma_start(
                out=d.rearrange("p (t f) -> p t f", t=ND),
                in_=dwqkv[:, c:c + 128].rearrange("(t p) f -> p t f", p=128))

        gq = [None, None]
        gk = [None, None]
        gv = [None, None]
        gq[0] = kb.op("gpsimd", cast_load(wq_s(0), 0), s_wq[0], 16)
        gk[0] = kb.op("gpsimd", cast_load(wk_s(0), DM), s_wk[0], 16)
        gv[0] = kb.op("gpsimd", cast_load(wv_s(0), 2 * DM), s_wv[0], 16)
        xnt_wr = None
        for q in range(4):
            xnt_wr = kb.op("gpsimd",
                           lambda e, q=q: e.dma_start(
                               out=dxnT[:, q * 8192:(q + 1) * 8192],
                               in_=XNT[:, q * 8192:(q + 1) * 8192]),
                           s_xntw, 16, waits=[xnt_ready])

        pA_read = [None] * 6
        pA_read[0] = ln_bank_read[0]
        pA_read[1] = ln_bank_read[1]
        pt_war = [None, None, None]
        prow_rd = None
        pv_read = [None, None]
        wq_war = [None, None]
        wk_war = [None, None]
        wv_war = [None, None]
        exp_last_prev = None
        otd_wr_last = None
        xnt_attn_last_read = None
        xnb_attn_last = None
        sums_stop = None
        pv_stop = None



        for h in range(H):
            hs = h % 2
            if h == 1 and st["s_done"][0] is None:
                prow_rd = finalize_scale(0, prow_rd)
            if h == 3 and st["s_done"][1] is None:
                prow_rd = finalize_scale(1, prow_rd)
            if h < 3:
                emit_passA(22)

            if h + 1 < H:
                ns = (h + 1) % 2
                gq[ns] = kb.op("gpsimd", cast_load(wq_s(ns), (h + 1) * 128),
                               s_wq[ns], 16, waits=[wq_war[ns]])
                gk[ns] = kb.op("gpsimd", cast_load(wk_s(ns), DM + (h + 1) * 128),
                               s_wk[ns], 16, waits=[wk_war[ns]])
                gv[ns] = kb.op("gpsimd",
                               cast_load(wv_s(ns), 2 * DM + (h + 1) * 128),
                               s_wv[ns], 16, waits=[wv_war[ns]])

            qk_evac_last = None
            for pi, (wsl, dst, gld) in enumerate(
                    ((wq_s(hs), qh, gq[hs]), (wk_s(hs), kh, gk[hs]))):
                for c in range(NCH):
                    bank = c % 2
                    for dt in range(ND):
                        w = ([gld, lnt_tile_done[4 * c + 3], pA_read[bank],
                              st["qtrans_bank_read"][bank]] if dt == 0 else [])
                        kb.op("tensor",
                              lambda e, b=bank, ww=wsl, dt=dt, c=c: e.matmul(
                                  P[b][:], ww[:, dt * 128:(dt + 1) * 128],
                                  XNT[:, dt * SEQ + c * CH:dt * SEQ + (c + 1) * CH],
                                  start=(dt == 0), stop=(dt == ND - 1)),
                              pe if dt == ND - 1 else None,
                              1 if dt == ND - 1 else 0, waits=w)
                    mm_done = (pe, pe.n)
                    ev = kb.op("scalar",
                               lambda e, b=bank, d=dst, c=c: e.copy(
                                   d[:, c * CH:(c + 1) * CH], P[b][:]),
                               ae, 1, waits=[mm_done])
                    pA_read[bank] = ev
                    qk_evac_last = ev
                if pi == 0:
                    wq_war[hs] = (pe, pe.n)
                else:
                    wk_war[hs] = (pe, pe.n)

            anch_rounds = []
            for g in range(4):
                bank = g % 2
                for ttl in range(4):
                    tok = 4 * g + ttl
                    for dt in range(ND):
                        w = ([gv[hs], lnt_tile_done[4 * g + 3], pA_read[bank],
                              st["qtrans_bank_read"][bank]]
                             if (dt == 0 and ttl == 0) else [])
                        kb.op("tensor",
                              lambda e, b=bank, tok=tok, dt=dt, ttl=ttl, hs=hs:
                              e.matmul(
                                  P[b][:, ttl * 128:(ttl + 1) * 128],
                                  XNT[:, dt * SEQ + tok * 128:
                                      dt * SEQ + (tok + 1) * 128],
                                  wv_s(hs)[:, dt * 128:(dt + 1) * 128],
                                  start=(dt == 0), stop=(dt == ND - 1)),
                              pe if (dt == ND - 1 and ttl == 3) else None,
                              1 if (dt == ND - 1 and ttl == 3) else 0, waits=w)
                vg_done = (pe, pe.n)
                ev = kb.op("scalar",
                           lambda e, b=bank, g=g: e.copy(
                               vh[:, g * CH:(g + 1) * CH], P[b][:]),
                           ae, 1, waits=[vg_done])
                pA_read[bank] = ev

                for rr in range(4):
                    qt = 4 * g + rr
                    w = ([qk_evac_last, st["p7f"]] if rr == 0 else [])
                    kb.op("tensor",
                          lambda e, qt=qt, rr=rr: e.matmul(
                              P7[:, rr * 128:(rr + 1) * 128],
                              qh[:, qt * 128:(qt + 1) * 128],
                              kh[:, qt * 128:(qt + 1) * 128],
                              start=True, stop=True),
                          pe if rr == 3 else None, 1 if rr == 3 else 0, waits=w)
                diag_done = (pe, pe.n)
                ar = kb.op("vector",
                           lambda e, g=g: e.reduce_max(ANCH[:, g:g + 1], P7[:],
                                                       axis=AX.X, negate=True),
                           ve, 1, waits=[diag_done])
                anch_rounds.append(ar)
                st["p7f"] = ar
            wv_war[hs] = (pe, pe.n)
            if h == H - 1:
                xnt_attn_last_read = (pe, pe.n)

            am = kb.op("vector",
                       lambda e: e.tensor_reduce(ANCM[:], ANCH[:], axis=AX.X,
                                                 op=ALU.min),
                       ve, 1, waits=anch_rounds)
            tr = kb.op("tensor",
                       lambda e: e.transpose(PROW[0:1, 0:128], ANCM[:], IDENT[:]),
                       pe, 1, waits=[am, prow_rd])
            nm = kb.op("vector",
                       lambda e: e.tensor_reduce(NM1[:], PROW[0:1, 0:128],
                                                 axis=AX.X, op=ALU.min),
                       ve, 1, waits=[tr])
            prow_rd = nm
            nm2 = kb.op("vector",
                        lambda e: e.tensor_scalar(NM2[:], NM1[:], SCL, 1.0,
                                                  ALU.mult, ALU.mult),
                        ve, 1, waits=[nm])
            bc = kb.op("tensor",
                       lambda e: e.matmul(P7[:, 256:257], ONES1[:], NM2[:],
                                          start=True, stop=True),
                       pe, 1, waits=[nm2, st["p7f"]])
            negm_ready = kb.op("vector",
                               lambda e: e.tensor_copy(NEGM[:], P7[:, 256:257]),
                               ve, 1, waits=[bc, exp_last_prev])
            st["p7f"] = negm_ready

            exp_last = None
            sj = 0
            for c in range(NCH):
                pvbank = 4 + (c % 2)
                njt = 4 * c + 4
                for j in range(njt):
                    sbank = sj % 4
                    pts = sj % 3
                    sj += 1
                    smm = kb.op("tensor",
                                lambda e, sbk=sbank, j=j, c=c: e.matmul(
                                    P[sbk][:], kh[:, j * 128:(j + 1) * 128],
                                    qh[:, c * CH:(c + 1) * CH],
                                    start=True, stop=True),
                                pe, 1, waits=[qk_evac_last, pA_read[sbank]])
                    exp_dep = smm
                    if j >= 4 * c:
                        a0 = 384 - 128 * (j - 4 * c)
                        exp_dep = kb.op("vector",
                                        lambda e, sbk=sbank, a0=a0: e.tensor_tensor(
                                            P[sbk][:], P[sbk][:],
                                            MASKS[:, a0:a0 + 512], ALU.add),
                                        ve, 1, waits=[smm, init_loads])
                    ex = kb.op("scalar",
                               lambda e, pts=pts, sbk=sbank: e.activation(
                                   pt_slot(pts), P[sbk][:], AF.Exp,
                                   bias=NEGM[:], scale=SCL),
                               ae, 1, waits=[exp_dep, negm_ready, pt_war[pts]])
                    pA_read[sbank] = ex
                    exp_last = ex
                    kb.op("tensor",
                          lambda e, pts=pts, j=j, njt=njt: e.matmul(
                              PROW[:], ONESCB[:], pt_slot(pts),
                              start=(j == 0), stop=(j == njt - 1)),
                          pe if j == njt - 1 else None,
                          1 if j == njt - 1 else 0,
                          waits=([ex, prow_rd, init_memset] if j == 0 else [ex]))
                    if j == njt - 1:
                        sums_stop = (pe, pe.n)
                    pvw = ([ex, pv_read[c % 2]] if j == 0 else [ex])
                    pv = kb.op("tensor",
                               lambda e, pts=pts, j=j, njt=njt, pb=pvbank: e.matmul(
                                   P[pb][:], vh[:, j * 128:(j + 1) * 128],
                                   pt_slot(pts),
                                   start=(j == 0), stop=(j == njt - 1)),
                               pe, 1, waits=pvw)
                    pt_war[pts] = pv
                pv_stop = pv
                al = kb.op("scalar",
                           lambda e: e.activation(SRL[:], PROW[:], AF.Ln),
                           ae, 1, waits=[sums_stop])
                prow_rd = al
                ar2 = kb.op("scalar",
                            lambda e: e.activation(SREC[:], SRL[:], AF.Exp,
                                                   scale=-1.0),
                            ae, 1, waits=[al])
                bcm = kb.op("tensor",
                            lambda e: e.matmul(P7[:], ONES1[:], SREC[:],
                                               start=True, stop=True),
                            pe, 1, waits=[ar2, st["p7f"]])
                rcp = kb.op("vector",
                            lambda e: e.tensor_copy(RECIPB, P7[:]),
                            ve, 1, waits=[bcm])
                st["p7f"] = rcp
                om = kb.op("vector",
                           lambda e, c=c, pb=pvbank: e.tensor_tensor(
                               otst_slot(c % 2), P[pb][:], RECIPB, ALU.mult),
                           ve, 1, waits=[pv_stop, rcp, otd_wr_last])
                pv_read[c % 2] = om
                xnb_attn_last = om
                otd_wr_last = kb.op("gpsimd",
                                    lambda e, h=h, c=c: e.dma_start(
                                        out=dotd[:, h * SEQ + c * CH:
                                                 h * SEQ + (c + 1) * CH],
                                        in_=otst_slot(c % 2)),
                                    s_otd, 16, waits=[om])
                if h >= 3:
                    emit_passB(2, pA_read)
                elif st["Ai"] < len(units):
                    emit_passA(4)
            exp_last_prev = exp_last

        attn_pe_done = (pe, pe.n)

        woutp_war = [None, None]
        otst2_war = [None, None]
        sst_war = [None, None]
        pB_read = [pA_read[0], pA_read[1]]
        sd_wr_last = None
        sd_last2 = [None, None]

        def emit_otl(tt):
            osl = tt % 2
            return kb.op("sync",
                         lambda e, tt=tt, o=osl: e.dma_start(
                             out=otst2_slot(o).rearrange("p (g c) -> p g c", g=H),
                             in_=dotd.rearrange("p (g s) -> p g s", g=H)[
                                 :, :, tt * 128:(tt + 1) * 128]),
                         s_ot2[osl], 16,
                         waits=[otd_wr_last, otst2_war[osl], xnt_attn_last_read])

        def emit_woutld(dmc, extra=None):
            return kb.op(
                "gpsimd",
                lambda e, dmc=dmc: e.dma_start(
                    out=woutp_slot(dmc % 2).rearrange("p (t m) -> p t m", t=ND),
                    in_=dwout[:, dmc * CH:(dmc + 1) * CH].rearrange(
                        "(t p) m -> p t m", p=128)),
                s_wop[dmc % 2], 16, waits=[xnt_attn_last_read, extra])

        wout_ld = [None] * 4
        wout_ld[0] = emit_woutld(0)
        wout_ld[1] = emit_woutld(1)

        otl_next = emit_otl(0)
        for dmc in range(4):
            wsl = dmc % 2
            if dmc >= 2:
                wout_ld[dmc] = emit_woutld(dmc, extra=woutp_war[wsl])
            for tt in range(NT):
                osl = tt % 2
                otl = otl_next
                if not (dmc == 3 and tt == NT - 1):
                    otl_next = emit_otl((tt + 1) % NT)
                bank = (dmc * NT + tt) % 2
                for hh in range(H):
                    w = ([otl, wout_ld[dmc], pB_read[bank], attn_pe_done,
                          st["qtrans_bank_read"][bank]]
                         if hh == 0 else [])
                    kb.op("tensor",
                          lambda e, b=bank, o=osl, hh=hh, ww=wsl: e.matmul(
                              P[b][:], otst2_slot(o)[:, hh * 128:(hh + 1) * 128],
                              woutp_slot(ww)[:, hh * CH:(hh + 1) * CH],
                              start=(hh == 0), stop=(hh == H - 1)),
                          pe if hh == H - 1 else None, 1 if hh == H - 1 else 0,
                          waits=w)
                smm = (pe, pe.n)
                otst2_war[osl] = smm
                woutp_war[wsl] = smm
                ev = kb.op("scalar",
                           lambda e, b=bank, o=osl: e.copy(
                               sst_slot(o), P[b][:]),
                           ae, 1, waits=[smm, sst_war[osl], xnb_attn_last,
                                         exp_last_prev])
                pB_read[bank] = ev
                sd_wr_last = kb.op("gpsimd",
                                   lambda e, tt=tt, dmc=dmc, o=osl: e.dma_start(
                                       out=dsd[tt * 128:(tt + 1) * 128,
                                               dmc * CH:(dmc + 1) * CH],
                                       in_=sst_slot(o)),
                                   s_sd[osl], 16, waits=[ev])
                sst_war[osl] = sd_wr_last
                sd_last2[osl] = sd_wr_last
                emit_passB(1, pB_read)
            emit_passB(12, pB_read)
        emit_passB(len(units), pB_read)
        assert st["Bi"] == len(units), f"passB incomplete {st['Bi']}"
        phaseB_pe_done = (pe, pe.n)

        b2b_ready = None
        for dmc in range(4):
            bm = kb.op("tensor",
                       lambda e, dmc=dmc: e.matmul(
                           P7[:], ONES1[:], B2ROW[:, dmc * CH:(dmc + 1) * CH],
                           start=True, stop=True),
                       pe, 1, waits=[st["p7f"], init_loads, attn_pe_done,
                                     b2b_ready])
            b2b_ready = kb.op("vector",
                              lambda e, dmc=dmc: e.tensor_copy(
                                  B2B[:, dmc * 512:(dmc + 1) * 512], P7[:]),
                              ve, 1, waits=[bm])
            st["p7f"] = b2b_ready

        pC_h_read = list(pB_read)
        pC_g_read = [None, None, None, None]
        w1p_war = [None, None]
        xnc_war = None
        w2m_war = [None, None, None, None]
        g_war = [None, None]
        st5_war = [None, None]
        xc_war = [None, None]
        zt_war = [None, None]
        ht_war = phaseB_pe_done
        y_last = None
        w1p_pend = {}

        def emit_xnc(c, first=False):
            return kb.op("sync",
                         lambda e, c=c: e.dma_start(
                             out=XNC.rearrange("p (d s) -> p d s", d=ND),
                             in_=dxnT.rearrange("p (d s) -> p d s", d=ND)[
                                 :, :, c * CH:(c + 1) * CH]),
                         s_xnc, 16,
                         waits=[xnc_war, xnt_wr,
                                st["qin_war"][0], st["qin_war"][1],
                                phaseB_pe_done if first else None])

        def emit_panel(c, fc):
            psl = fc % 2
            return kb.op("sync",
                         lambda e, fc=fc: e.dma_start(
                             out=w1p_slot(fc % 2).rearrange(
                                 "p (d f) -> p d f", d=ND),
                             in_=dw1t.rearrange("p (d f) -> p d f", d=ND)[
                                 :, :, fc * 512:(fc + 1) * 512]),
                         s_w1p[psl], 16,
                         waits=[st["w1t_done0"], st["w1t_done1"], w1p_war[psl],
                                st["qpanel_war"][0], st["qpanel_war"][1],
                                phaseB_pe_done if (c == 0 and fc < 2) else None])

        xnc_pend = emit_xnc(0, first=True)
        w1p_pend[(0, 0)] = emit_panel(0, 0)
        w1p_pend[(0, 1)] = emit_panel(0, 1)

        for c in range(NCH):
            xnc_all = xnc_pend
            last_gelu_of_f = {}
            for fc in range(16):
                psl = fc % 2
                pl_all = w1p_pend.pop((c, fc))
                for fl in range(4):
                    f = fc * 4 + fl
                    bank = f % 2
                    for dt in range(ND):
                        w = ([pl_all, xnc_all, pC_h_read[bank],
                              st["qtrans_bank_read"][bank]] if dt == 0 else [])
                        kb.op("tensor",
                              lambda e, b=bank, p=psl, dt=dt, fl=fl: e.matmul(
                                  P[b][:],
                                  w1p_slot(p)[:, dt * 512 + fl * 128:
                                              dt * 512 + (fl + 1) * 128],
                                  xnc_d(dt),
                                  start=(dt == 0), stop=(dt == ND - 1)),
                              pe if dt == ND - 1 else None,
                              1 if dt == ND - 1 else 0, waits=w)
                    hmm_done = (pe, pe.n)
                    w1p_war[psl] = hmm_done
                    gl = kb.op("scalar",
                               lambda e, b=bank, f=f: e.activation(
                                   XNT[:, f * 512:(f + 1) * 512], P[b][:],
                                   AF.Gelu, bias=B1C[:, f:f + 1]),
                               ae, 1, waits=[hmm_done, b1c_ready, ht_war])
                    pC_h_read[bank] = gl
                    last_gelu_of_f[f] = gl
                nfc, nc_ = (fc + 2, c) if fc + 2 < 16 else (fc + 2 - 16, c + 1)
                if nc_ < NCH:
                    w1p_pend[(nc_, nfc)] = emit_panel(nc_, nfc)
            xnc_war = (pe, pe.n)
            if DBG_SCRATCH:
                kb.op("gpsimd",
                      lambda e, c=c: e.dma_start(
                          out=dbg_ht[:, c * NF * CH:(c + 1) * NF * CH],
                          in_=XNT[:, 0:NF * CH]),
                      s_init, 16, waits=[last_gelu_of_f[NF - 1]])
            if c + 1 < NCH:
                xnc_pend = emit_xnc(c + 1)

            for dmc in range(4):
                for fp in range(NF // 2):
                    dsl = fp % 4
                    w2ld = kb.op("sync",
                                 lambda e, dmc=dmc, fp=fp, m=dsl: e.dma_start(
                                     out=w2m_dslot(m).rearrange(
                                         "p (b n) -> p b n", b=2),
                                     in_=dw2t.rearrange("p (ff n) -> p ff n",
                                                        ff=NF)[
                                         :, 2 * fp:2 * fp + 2,
                                         dmc * CH:(dmc + 1) * CH]),
                                 s_w2m[dsl], 16,
                                 waits=[st["w2t_done0"], st["w2t_done1"],
                                        w2m_war[dsl],
                                        sd_last2[0] if (c == 0 and dmc == 0)
                                        else None,
                                        sd_last2[1] if (c == 0 and dmc == 0)
                                        else None,
                                        xnb_attn_last if (c == 0 and dmc == 0)
                                        else None])
                    for fh in range(2):
                        f = 2 * fp + fh
                        for tt in range(4):
                            gbank = 2 + tt
                            w = [w2ld, last_gelu_of_f[f]]
                            if f == 0:
                                w.append(pC_g_read[tt])
                            kb.op("tensor",
                                  lambda e, gb=gbank, f=f, tt=tt, m=dsl, fh=fh:
                                  e.matmul(
                                      P[gb][:], XNT[:, f * 512 + tt * 128:
                                                    f * 512 + (tt + 1) * 128],
                                      w2m_dslot(m)[:, fh * 512:(fh + 1) * 512],
                                      start=(f == 0), stop=(f == NF - 1)),
                                  pe if (fh == 1 and tt == 3) else None,
                                  1 if (fh == 1 and tt == 3) else 0, waits=w)
                    w2m_war[dsl] = (pe, pe.n)
                g_stop = (pe, pe.n)
                gpar = dmc % 2
                for tt in range(4):
                    gbank = 2 + tt
                    zb = kb.op("vector",
                               lambda e, gb=gbank, dmc=dmc, tt=tt: e.tensor_tensor(
                                   zt_slot(tt % 2), P[gb][:],
                                   B2B[:, dmc * 512:(dmc + 1) * 512], ALU.add),
                               ve, 1, waits=[g_stop, b2b_ready, zt_war[tt % 2]])
                    sl = kb.op("scalar",
                               lambda e, gp=gpar, tt=tt: e.activation(
                                   g_slice(gp, tt), zt_slot(tt % 2), AF.Silu),
                               ae, 1, waits=[zb, g_war[gpar]])
                    zt_war[tt % 2] = sl
                    pC_g_read[tt] = sl
                if DBG_SCRATCH:
                    kb.op("gpsimd",
                          lambda e, c=c, dmc=dmc: e.dma_start(
                              out=dbg_g[:, (c * 4 + dmc) * 2048:
                                        (c * 4 + dmc + 1) * 2048],
                              in_=WQB[:, 6144 + (dmc % 2) * 2048:
                                      6144 + (dmc % 2 + 1) * 2048]),
                          s_init, 16, waits=[pC_g_read[0], pC_g_read[1],
                                           pC_g_read[2], pC_g_read[3]])
                for tt in range(4):
                    trow = c * 4 + tt
                    k2 = (dmc * 4 + tt) % 2
                    stld = kb.op("gpsimd",
                                 lambda e, trow=trow, dmc=dmc, s=k2: e.dma_start(
                                     out=st5_slot(s),
                                     in_=dsd[trow * 128:(trow + 1) * 128,
                                             dmc * CH:(dmc + 1) * CH]),
                                 s_st5[k2], 16,
                                 waits=[sd_last2[0], sd_last2[1], st5_war[k2],
                                        st["qout_war"][0], st["qout_war"][1]])
                    xld = kb.op("sync",
                                lambda e, trow=trow, dmc=dmc, s=k2: e.dma_start(
                                    out=xc_slot(s),
                                    in_=dx[trow * 128:(trow + 1) * 128,
                                           dmc * CH:(dmc + 1) * CH]),
                                s_xc[k2], 16,
                                waits=[xc_war[k2], st["qout_war"][0],
                                       st["qout_war"][1], st["prev_r"]])
                    sg = kb.op("vector",
                               lambda e, s=k2, gp=gpar, tt=tt: e.tensor_tensor(
                                   st5_slot(s), st5_slot(s), g_slice(gp, tt),
                                   ALU.mult),
                               ve, 1, waits=[stld, pC_g_read[tt]])
                    g_war[gpar] = sg
                    ad = kb.op("vector",
                               lambda e, s=k2: e.tensor_tensor(
                                   xc_slot(s), xc_slot(s), st5_slot(s), ALU.add),
                               ve, 1, waits=[sg, xld])
                    st5_war[k2] = ad
                    y_last = kb.op("sync",
                                   lambda e, trow=trow, dmc=dmc, s=k2: e.dma_start(
                                       out=dy[trow * 128:(trow + 1) * 128,
                                              dmc * CH:(dmc + 1) * CH],
                                       in_=xc_slot(s)),
                                   s_y[k2], 16, waits=[ad])
                    xc_war[k2] = y_last
                ht_war = g_stop

        kb.op("sync", lambda e: e.nop(), waits=[y_last])

        for s_ in kb.sems:
            s_.h = E(nc.semaphore(s_.name))
        block = E(nc.Block())

        @block.sync
        def _(e):
            kb.realize("sync", e)

        @block.scalar
        def _(e):
            kb.realize("scalar", e)

        @block.vector
        def _(e):
            kb.realize("vector", e)

        @block.tensor
        def _(e):
            kb.realize("tensor", e)

        @block.gpsimd
        def _(e):
            kb.realize("gpsimd", e)

    return nc


_NC_CACHE = None


def kernel(L0_attn_x, L0_attn_w_qkv, L0_attn_w_out, L0_ffn_w1, L0_ffn_b1,
           L0_ffn_w2, L0_ffn_b2):
    global _NC_CACHE
    x = np.ascontiguousarray(np.asarray(L0_attn_x, dtype=np.float32))
    wqkv = np.ascontiguousarray(np.asarray(L0_attn_w_qkv, dtype=np.float32))
    wout = np.ascontiguousarray(np.asarray(L0_attn_w_out, dtype=np.float32))
    w1 = np.ascontiguousarray(np.asarray(L0_ffn_w1, dtype=np.float32))
    b1 = np.ascontiguousarray(np.asarray(L0_ffn_b1, dtype=np.float32)).reshape(NF, 128)
    w2 = np.ascontiguousarray(np.asarray(L0_ffn_w2, dtype=np.float32))
    b2 = np.ascontiguousarray(np.asarray(L0_ffn_b2, dtype=np.float32)).reshape(1, DM)
    ident = np.eye(128, dtype=np.float32)
    identb = np.asarray(np.eye(128, dtype=np.float32).astype(ml_dtypes.bfloat16))
    ones1 = np.ones((1, 128), dtype=np.float32)
    masks = build_masks()

    if _NC_CACHE is None:
        _NC_CACHE = build_nc()
    nc = _NC_CACHE

    in_maps = []
    for i in range(N_CORES):
        in_maps.append({
            "x": np.ascontiguousarray(x[i]),
            "wqkv": wqkv, "wout": wout, "w1": w1, "b1": b1,
            "w2": w2, "b2": b2, "ident": ident, "identb": identb,
            "ones1": ones1, "masks": masks,
        })
    res = run_bass_kernel_spmd(nc, in_maps, core_ids=list(range(N_CORES)))
    out = np.stack([np.asarray(r["y"], dtype=np.float32) for r in res.results])
    return out


# version 06
# speedup vs baseline: 1.4219x; 1.0551x over previous
import sys
sys.path.insert(0, "/opt/trn_rl_repo")
from contextlib import ExitStack
import numpy as np
import ml_dtypes

import concourse.bass as bass
import concourse.mybir as mybir
from concourse.bass_utils import run_bass_kernel_spmd

F32 = mybir.dt.float32
BF16 = mybir.dt.bfloat16
AF = mybir.ActivationFunctionType
ALU = mybir.AluOpType
AX = mybir.AxisListType

SEQ = 2048
DM = 2048
H = 16
DFF = 8192
NT = 16
ND = 16
NF = 64
NCH = 4
CH = 512
SCL = float(1.0 / np.sqrt(128.0))
NEG = -1e10
N_CORES = 8

ENGS = ("sync", "scalar", "vector", "tensor", "gpsimd")


class Sem:
    __slots__ = ("name", "n", "h")

    def __init__(self, name):
        self.name = name
        self.n = 0
        self.h = None


class KB:
    def __init__(self):
        self.ops = {e: [] for e in ENGS}
        self.sems = []
        self.guar = {e: {} for e in ENGS}

    def sem(self, name):
        s = Sem(name)
        self.sems.append(s)
        return s

    def wait(self, eng, sm, val):
        if sm is None or val <= 0:
            return
        if self.guar[eng].get(sm, 0) >= val:
            return
        self.guar[eng][sm] = val
        self.ops[eng].append(("w", sm, val))

    def op(self, eng, fn, sm=None, inc=0, waits=()):
        for w in waits:
            if w is not None:
                self.wait(eng, w[0], w[1])
        if sm is not None:
            sm.n += inc
            self.ops[eng].append(("o", fn, sm, inc))
            return (sm, sm.n)
        self.ops[eng].append(("o", fn, None, 0))
        return None

    def realize(self, eng, h):
        for item in self.ops[eng]:
            if item[0] == "w":
                h.wait_ge(item[1].h, item[2])
            else:
                inst = item[1](h)
                if item[2] is not None:
                    inst.then_inc(item[2].h, item[3])


def build_masks():
    kl = np.arange(128)[:, None]
    x = np.arange(896)[None, :]
    m = np.where(x >= kl + 384, 0.0, NEG).astype(np.float32)
    return np.asarray(m.astype(ml_dtypes.bfloat16))


DBG_SCRATCH = False


def build_nc():
    nc = bass.Bass("TRN2", target_bir_lowering=False)

    dx = nc.dram_tensor("x", [SEQ, DM], F32, kind="ExternalInput").ap()
    dwqkv = nc.dram_tensor("wqkv", [DM, 3 * DM], F32, kind="ExternalInput").ap()
    dwout = nc.dram_tensor("wout", [DM, DM], F32, kind="ExternalInput").ap()
    dw1 = nc.dram_tensor("w1", [DFF, DM], F32, kind="ExternalInput").ap()
    db1 = nc.dram_tensor("b1", [NF, 128], F32, kind="ExternalInput").ap()
    dw2 = nc.dram_tensor("w2", [DM, DFF], F32, kind="ExternalInput").ap()
    db2 = nc.dram_tensor("b2", [1, DM], F32, kind="ExternalInput").ap()
    dident = nc.dram_tensor("ident", [128, 128], F32, kind="ExternalInput").ap()
    didentb = nc.dram_tensor("identb", [128, 128], BF16, kind="ExternalInput").ap()
    dones1 = nc.dram_tensor("ones1", [1, 128], F32, kind="ExternalInput").ap()
    dmasks = nc.dram_tensor("masks", [128, 896], BF16, kind="ExternalInput").ap()
    dy = nc.dram_tensor("y", [SEQ, DM], F32, kind="ExternalOutput").ap()

    _SK = "ExternalOutput" if DBG_SCRATCH else "Internal"
    dbg_ht = (nc.dram_tensor("dbg_ht", [128, 4 * NF * CH], BF16,
                             kind="ExternalOutput").ap() if DBG_SCRATCH else None)
    dbg_g = (nc.dram_tensor("dbg_g", [128, 4 * 4 * 2048], BF16,
                            kind="ExternalOutput").ap() if DBG_SCRATCH else None)
    dxnT = nc.dram_tensor("dxnT", [128, ND * SEQ], BF16, kind=_SK).ap()
    dotd = nc.dram_tensor("dotd", [128, H * SEQ], BF16, kind=_SK).ap()
    dsd = nc.dram_tensor("sd", [SEQ, DM], BF16, kind=_SK).ap()
    dw1t = nc.dram_tensor("w1t", [128, ND * DFF], BF16, kind=_SK).ap()
    dw2t = nc.dram_tensor("w2t", [128, NF * DM], BF16, kind=_SK).ap()

    kb = KB()
    pe = kb.sem("pe")
    ve = kb.sem("ve")
    ae = kb.sem("ae")
    ge = kb.sem("ge")
    s_init = kb.sem("s_init")
    s_xf = [kb.sem("s_xf0"), kb.sem("s_xf1")]
    s_qin = [kb.sem("s_qin0"), kb.sem("s_qin1")]
    s_w1t = [kb.sem("s_w1t0"), kb.sem("s_w1t1")]
    s_w2t = [kb.sem("s_w2t0"), kb.sem("s_w2t1")]
    s_xntw = kb.sem("s_xntw")
    s_otd = kb.sem("s_otd")
    s_sd = [kb.sem("s_sd0"), kb.sem("s_sd1")]
    s_wq = [kb.sem("s_wq0"), kb.sem("s_wq1")]
    s_wk = [kb.sem("s_wk0"), kb.sem("s_wk1")]
    s_wv = [kb.sem("s_wv0"), kb.sem("s_wv1")]
    s_wop = [kb.sem("s_wop0"), kb.sem("s_wop1")]
    s_ot2 = [kb.sem(f"s_ot2{i}") for i in range(4)]
    s_xnc = kb.sem("s_xnc")
    s_w1p = [kb.sem("s_w1p0"), kb.sem("s_w1p1")]
    s_w2m = [kb.sem(f"s_w2m{i}") for i in range(4)]
    s_st5 = [kb.sem("s_st50"), kb.sem("s_st51")]
    s_xc = [kb.sem("s_xc0"), kb.sem("s_xc1")]
    s_y = [kb.sem("s_y0"), kb.sem("s_y1")]

    with ExitStack() as ctx:
        E = ctx.enter_context
        sb, ps = nc.sbuf_tensor, nc.psum_tensor

        XNT = E(sb("XNT", [128, ND * SEQ], BF16))
        BIGB = E(sb("BIGB", [128, 24576], BF16))
        QKVB = E(sb("QKVB", [128, 3 * SEQ], BF16))
        WQB = E(sb("WQB", [128, 2 * 3 * DM], BF16))
        XNB = E(sb("XNB", [128, 4096], BF16))
        WP = E(sb("WP", [128, 8192], BF16))
        MASKS = E(sb("MASKS", [128, 896], BF16))
        IDENT = E(sb("IDENT", [128, 128], F32))
        IDENTB = E(sb("IDENTB", [128, 128], BF16))
        ONES1 = E(sb("ONES1", [1, 128], F32))
        ONESCB = E(sb("ONESCB", [128, 1], BF16))
        ONESCF = E(sb("ONESCF", [128, 1], F32))
        EPSC = E(sb("EPSC", [128, 1], F32))
        B1RS = E(sb("B1RS", [64, 128], F32))
        B1C = E(sb("B1C", [128, 64], F32))
        B2ROW = E(sb("B2ROW", [1, DM], F32))
        LNS = E(sb("LNS", [128, 12], F32))
        WACC = E(sb("WACC", [128, 128], F32))
        ANCH = E(sb("ANCH", [128, 4], F32))
        ANCM = E(sb("ANCM", [128, 1], F32))
        NEGM = E(sb("NEGM", [128, 1], F32))
        NM1 = E(sb("NM1", [1, 1], F32))
        NM2 = E(sb("NM2", [1, 1], F32))
        SRL = E(sb("SRL", [1, CH], F32))
        SREC = E(sb("SREC", [1, CH], F32))
        SCLB = [E(sb("SCL1", [1, 1], F32)), E(sb("SCL2", [1, 1], F32))]
        TCOL = [E(sb("TC1", [128, 1], F32)), E(sb("TC2", [128, 1], F32))]
        NTCOL = [E(sb("NTC1", [128, 1], F32)), E(sb("NTC2", [128, 1], F32))]
        WSUM = [E(sb("WS1", [128, 1], F32)), E(sb("WS2", [128, 1], F32))]

        P = [E(ps(f"P{i}", [128, 512], F32)) for i in range(6)]
        PROW = E(ps("PROW", [1, 512], F32))
        P7 = E(ps("P7", [128, 512], F32))

        def xf_slot(i):
            return BIGB[:, i * 4096:(i + 1) * 4096].bitcast(F32)

        def xnb_slot(i):
            return BIGB[:, 8192 + i * 2048:8192 + (i + 1) * 2048]

        LNDUMP = BIGB[:, 12288:16384].bitcast(F32)

        def qpanel_slot(i):
            return BIGB[:, i * 8192:(i + 1) * 8192]

        def qin_slot(i):
            return BIGB[:, 16384 + i * 4096:16384 + (i + 1) * 4096].bitcast(F32)

        def w1p_slot(i):
            return BIGB[:, i * 8192:(i + 1) * 8192]

        XNC = BIGB[:, 16384:24576]

        def xnc_d(dt):
            return BIGB[:, 16384 + dt * 512:16384 + (dt + 1) * 512]

        def woutp_slot(i):
            return XNT[:, i * 8192:(i + 1) * 8192]

        def otst2_slot(i):
            return XNT[:, 16384 + i * 2048:16384 + (i + 1) * 2048]

        qh = QKVB[:, 0:SEQ]
        kh = QKVB[:, SEQ:2 * SEQ]
        vh = QKVB[:, 2 * SEQ:3 * SEQ]

        def wq_s(i):
            return WQB[:, i * 6144 + 0:i * 6144 + DM]

        def wk_s(i):
            return WQB[:, i * 6144 + DM:i * 6144 + 2 * DM]

        def wv_s(i):
            return WQB[:, i * 6144 + 2 * DM:i * 6144 + 3 * DM]

        B2B = WQB[:, 0:4096].bitcast(F32)

        def zt_slot(i):
            return WQB[:, 4096 + i * 1024:4096 + (i + 1) * 1024].bitcast(F32)

        def g_slice(par, tt):
            return WQB[:, 6144 + par * 2048 + tt * 512:
                       6144 + par * 2048 + (tt + 1) * 512]

        _PT_OFF = [0, 512, 3072, 3584]

        def pt_slot(i):
            return XNB[:, _PT_OFF[i]:_PT_OFF[i] + 512]

        RECIPB = XNB[:, 1024:2048].bitcast(F32)

        def otst_slot(i):
            return XNB[:, 2048 + i * 512:2048 + (i + 1) * 512]

        def sst_slot(i):
            return XNB[:, i * 512:(i + 1) * 512]

        def w2m_dslot(i):
            return XNB[:, i * 1024:(i + 1) * 1024]

        def qout_slot(i):
            return WP[:, i * 2048:(i + 1) * 2048]

        QTMP = WP[:, 4096:8192].bitcast(F32)

        def xc_slot(i):
            return WP[:, i * 1024:(i + 1) * 1024].bitcast(F32)

        def st5_slot(i):
            return WP[:, 2048 + i * 512:2048 + (i + 1) * 512]

        lnsum = [LNS[:, 0:1], LNS[:, 1:2]]
        lnmean = [LNS[:, 2:3], LNS[:, 3:4]]
        lnvs = [LNS[:, 4:5], LNS[:, 5:6]]
        lnln = [LNS[:, 6:7], LNS[:, 7:8]]
        lnrstd = [LNS[:, 8:9], LNS[:, 9:10]]

        for dst, src in ((IDENT[:], dident), (IDENTB[:], didentb),
                         (ONES1[:], dones1), (MASKS[:], dmasks),
                         (B1RS[:], db1), (B2ROW[:], db2)):
            kb.op("sync", lambda e, d=dst, s=src: e.dma_start(out=d, in_=s),
                  s_init, 16)
        init_loads = (s_init, s_init.n)

        kb.op("gpsimd", lambda e: e.memset(ONESCB[:], 1.0), ge, 1)
        kb.op("gpsimd", lambda e: e.memset(ONESCF[:], 1.0), ge, 1)
        kb.op("gpsimd", lambda e: e.memset(EPSC[:], 1e-5), ge, 1)
        init_memset = (ge, ge.n)

        b1t = kb.op("tensor", lambda e: e.transpose(P7[:, 0:64], B1RS[:],
                                                    IDENT[0:64, 0:64]),
                    pe, 1, waits=[init_loads])
        b1c_ready = kb.op("vector", lambda e: e.tensor_copy(B1C[:], P7[:, 0:64]),
                          ve, 1, waits=[b1t])

        units = [(0, dh, ch) for dh in range(ND) for ch in range(4)] + \
                [(1, fh, 0) for fh in range(NF)]

        def unit_src(mat, pnl, ch):
            if mat == 0:
                return dw1[ch * 2048:(ch + 1) * 2048,
                           pnl * 128:(pnl + 1) * 128].rearrange(
                    "(t p) d -> p t d", p=128)
            return dw2[:, pnl * 128:(pnl + 1) * 128].rearrange(
                "(t p) d -> p t d", p=128)

        st = {
            "Ai": 0, "Bi": 0,
            "qin_war": [None, None],
            "qout_war": [None, None],
            "qpanel_war": [None, None],
            "w1A_done": None, "w2A_done": None,
            "s_done": [None, None],
            "prev_r": None,
            "w1t_done0": None, "w1t_done1": None,
            "w2t_done0": None, "w2t_done1": None,
            "qtrans_bank_read": [None, None],
            "p7f": None,
        }
        st["p7f"] = b1c_ready

        def emit_passA(nmax):
            for _ in range(nmax):
                if st["Ai"] >= len(units):
                    return
                mat, pnl, ch = units[st["Ai"]]
                slot = st["Ai"] % 2
                col = st["Ai"]
                v = kb.op("sync",
                          lambda e, m=mat, p=pnl, c=ch, sl=slot: e.dma_start(
                              out=qin_slot(sl).rearrange("p (t d) -> p t d", t=16),
                              in_=unit_src(m, p, c)),
                          s_qin[slot], 16, waits=[st["qin_war"][slot]])
                r = kb.op("vector",
                          lambda e, sl=slot, c=col: e.reduce_sum(
                              WACC[:, c:c + 1], qin_slot(sl), axis=AX.X,
                              apply_absolute_value=True),
                          ve, 1, waits=[v])
                st["qin_war"][slot] = r
                st["Ai"] += 1
                if st["Ai"] == 64:
                    st["w1A_done"] = r
                if st["Ai"] == 128:
                    st["w2A_done"] = r

        def finalize_scale(mat, prow_rd):
            a_done = st["w1A_done"] if mat == 0 else st["w2A_done"]
            assert a_done is not None
            v1 = kb.op("vector",
                       lambda e, m=mat: e.tensor_reduce(
                           WSUM[m][:], WACC[:, m * 64:(m + 1) * 64],
                           axis=AX.X, op=ALU.add),
                       ve, 1, waits=[a_done])
            t1 = kb.op("tensor",
                       lambda e, m=mat: e.matmul(PROW[0:1, 0:1], ONESCF[:],
                                                 WSUM[m][:], start=True, stop=True),
                       pe, 1, waits=[v1, init_memset, prow_rd])
            a1 = kb.op("scalar",
                       lambda e, m=mat: e.activation(SCLB[m][:], PROW[0:1, 0:1],
                                                     AF.Copy,
                                                     scale=0.7 / float(DFF * DM)),
                       ae, 1, waits=[t1])
            t2 = kb.op("tensor",
                       lambda e, m=mat: e.matmul(P7[:, 0:1], ONES1[:], SCLB[m][:],
                                                 start=True, stop=True),
                       pe, 1, waits=[a1, st["p7f"], init_loads])
            v2 = kb.op("vector",
                       lambda e, m=mat: e.tensor_copy(TCOL[m][:], P7[:, 0:1]),
                       ve, 1, waits=[t2])
            v3 = kb.op("vector",
                       lambda e, m=mat: e.tensor_scalar(
                           NTCOL[m][:], TCOL[m][:], -1.0, 1.0, ALU.mult, ALU.mult),
                       ve, 1, waits=[v2])
            st["s_done"][mat] = v3
            st["p7f"] = v2
            return a1

        def emit_passB(nmax, pA_read):
            for _ in range(nmax):
                if st["Bi"] >= len(units):
                    return
                mat, pnl, ch = units[st["Bi"]]
                if st["s_done"][mat] is None:
                    return
                ui = st["Bi"]
                islot = ui % 2
                oslot = ui % 2
                pslot = pnl % 2
                v = kb.op("sync",
                          lambda e, m=mat, p=pnl, c=ch, sl=islot: e.dma_start(
                              out=qin_slot(sl).rearrange("p (t d) -> p t d", t=16),
                              in_=unit_src(m, p, c)),
                          s_qin[islot], 16, waits=[st["qin_war"][islot]])
                r1 = kb.op("vector",
                           lambda e, sl=islot, m=mat: e.tensor_scalar(
                               QTMP, qin_slot(sl), TCOL[m][:], -1.0,
                               ALU.is_ge, ALU.add),
                           ve, 1, waits=[v, st["s_done"][mat], st["prev_r"]])
                r2 = kb.op("vector",
                           lambda e, sl=islot, o=oslot, m=mat:
                           e.scalar_tensor_tensor(
                               qout_slot(o), qin_slot(sl), NTCOL[m][:], QTMP,
                               ALU.is_gt, ALU.add),
                           ve, 1, waits=[r1, st["qout_war"][oslot]])
                st["prev_r"] = r2
                st["qin_war"][islot] = r2
                evacs = []
                pview = P7.bitcast(BF16)
                for rr in range(2):
                    for j in range(8):
                        w = ([r2, st["p7f"]] if j == 0 else [])
                        kb.op("tensor",
                              lambda e, pv=pview, o=oslot, rr=rr, j=j: e.transpose(
                                  pv[:, j * 128:(j + 1) * 128],
                                  qout_slot(o)[:, (rr * 8 + j) * 128:
                                               (rr * 8 + j + 1) * 128],
                                  IDENTB[:]),
                              pe if j == 7 else None, 1 if j == 7 else 0,
                              waits=w)
                    tdone = (pe, pe.n)
                    base = (ch * 2048 + rr * 1024) if mat == 0 else (rr * 1024)
                    evc = kb.op("scalar",
                                lambda e, pv=pview, psl=pslot, b=base: e.copy(
                                    qpanel_slot(psl)[:, b:b + 1024], pv[:]),
                                ae, 1, waits=[tdone, st["qpanel_war"][pslot]])
                    st["p7f"] = evc
                    evacs.append(evc)
                if mat == 0 and ch == 3:
                    wr = kb.op("gpsimd",
                               lambda e, p=pnl, psl=pslot: e.dma_start(
                                   out=dw1t[:, p * 8192:(p + 1) * 8192],
                                   in_=qpanel_slot(psl)),
                               s_w1t[pslot], 16, waits=evacs)
                    st["qpanel_war"][pslot] = wr
                    st["w1t_done" + str(pslot)] = wr
                elif mat == 1:
                    wr = kb.op("gpsimd",
                               lambda e, p=pnl, psl=pslot: e.dma_start(
                                   out=dw2t[:, p * 2048:(p + 1) * 2048],
                                   in_=qpanel_slot(psl)[:, 0:2048]),
                               s_w2t[pslot], 16, waits=evacs)
                    st["qpanel_war"][pslot] = wr
                    st["w2t_done" + str(pslot)] = wr
                st["qout_war"][oslot] = (pe, pe.n)
                st["Bi"] += 1

        xf_war = [None, None]
        lnvs_war = [None, None]
        xnb_war = [None, None]
        lnt_tile_done = [None] * NT
        lnt_last_evac = None
        pA_read = [None] * 6
        qk_evac = {}
        vh_evac = {}
        anch_round = {}
        wq_war = [None, None]
        wk_war = [None, None]
        wv_war = [None, None]

        def cast_load(dst, c0):
            return lambda e, d=dst, c=c0: e.dma_start(
                out=d.rearrange("p (t f) -> p t f", t=ND),
                in_=dwqkv[:, c:c + 128].rearrange("(t p) f -> p t f", p=128))

        gq = [None, None]
        gk = [None, None]
        gv = [None, None]
        gq[0] = kb.op("gpsimd", cast_load(wq_s(0), 0), s_wq[0], 16)
        gk[0] = kb.op("gpsimd", cast_load(wk_s(0), DM), s_wk[0], 16)
        gv[0] = kb.op("gpsimd", cast_load(wv_s(0), 2 * DM), s_wv[0], 16)

        def emit_qkv_block(h, c):
            hs = h % 2
            for bank, wsl, dst, gld, tag in (
                    (2, wq_s(hs), qh, gq[hs], "q"), (3, wk_s(hs), kh, gk[hs], "k")):
                for dt in range(ND):
                    w = ([gld, lnt_tile_done[4 * c + 3], pA_read[bank]]
                         if dt == 0 else [])
                    kb.op("tensor",
                          lambda e, b=bank, ww=wsl, dt=dt, c=c: e.matmul(
                              P[b][:], ww[:, dt * 128:(dt + 1) * 128],
                              XNT[:, dt * SEQ + c * CH:dt * SEQ + (c + 1) * CH],
                              start=(dt == 0), stop=(dt == ND - 1)),
                          pe if dt == ND - 1 else None,
                          1 if dt == ND - 1 else 0, waits=w)
                mm_done = (pe, pe.n)
                if tag == "q":
                    wq_war[hs] = mm_done
                else:
                    wk_war[hs] = mm_done
                ev = kb.op("scalar",
                           lambda e, b=bank, d=dst, c=c: e.copy(
                               d[:, c * CH:(c + 1) * CH], P[b][:]),
                           ae, 1, waits=[mm_done])
                pA_read[bank] = ev
                qk_evac[(h, tag, c)] = ev
            g = c
            bank = 2
            for ttl in range(4):
                tok = 4 * g + ttl
                for dt in range(ND):
                    w = ([gv[hs], lnt_tile_done[4 * g + 3], pA_read[bank]]
                         if (dt == 0 and ttl == 0) else [])
                    kb.op("tensor",
                          lambda e, b=bank, tok=tok, dt=dt, ttl=ttl, hs=hs:
                          e.matmul(
                              P[b][:, ttl * 128:(ttl + 1) * 128],
                              XNT[:, dt * SEQ + tok * 128:
                                  dt * SEQ + (tok + 1) * 128],
                              wv_s(hs)[:, dt * 128:(dt + 1) * 128],
                              start=(dt == 0), stop=(dt == ND - 1)),
                          pe if (dt == ND - 1 and ttl == 3) else None,
                          1 if (dt == ND - 1 and ttl == 3) else 0, waits=w)
            vg_done = (pe, pe.n)
            wv_war[hs] = vg_done
            ev = kb.op("scalar",
                       lambda e, b=bank, g=g: e.copy(
                           vh[:, g * CH:(g + 1) * CH], P[b][:]),
                       ae, 1, waits=[vg_done])
            pA_read[bank] = ev
            vh_evac[(h, g)] = ev
            for rr in range(4):
                qt = 4 * g + rr
                w = ([qk_evac[(h, "q", c)], qk_evac[(h, "k", c)], st["p7f"]]
                     if rr == 0 else [])
                kb.op("tensor",
                      lambda e, qt=qt, rr=rr: e.matmul(
                          P7[:, rr * 128:(rr + 1) * 128],
                          qh[:, qt * 128:(qt + 1) * 128],
                          kh[:, qt * 128:(qt + 1) * 128],
                          start=True, stop=True),
                      pe if rr == 3 else None, 1 if rr == 3 else 0, waits=w)
            diag_done = (pe, pe.n)
            ar = kb.op("vector",
                       lambda e, g=g: e.reduce_max(ANCH[:, g:g + 1], P7[:],
                                                   axis=AX.X, negate=True),
                       ve, 1, waits=[diag_done])
            anch_round[(h, g)] = ar
            st["p7f"] = ar
        for t in range(NT):
            s = t % 2
            xd = kb.op("sync",
                       lambda e, t=t, s=s: e.dma_start(
                           out=xf_slot(s), in_=dx[t * 128:(t + 1) * 128, :]),
                       s_xf[s], 16, waits=[xf_war[s]])
            v1 = kb.op("vector",
                       lambda e, s=s: e.reduce_sum(lnsum[s], xf_slot(s), axis=AX.X),
                       ve, 1, waits=[xd])
            v2 = kb.op("vector",
                       lambda e, s=s: e.tensor_scalar(
                           lnmean[s], lnsum[s], 1.0 / DM, 1.0, ALU.mult, ALU.mult),
                       ve, 1, waits=[v1])
            v3 = kb.op("vector",
                       lambda e, s=s: e.scalar_tensor_tensor(
                           LNDUMP, xf_slot(s), lnmean[s], xf_slot(s),
                           ALU.subtract, ALU.mult, accum_out=lnvs[s]),
                       ve, 1, waits=[v2, lnvs_war[s]])
            a1 = kb.op("scalar",
                       lambda e, s=s: e.activation(lnln[s], lnvs[s], AF.Ln,
                                                   bias=EPSC[:], scale=1.0 / DM),
                       ae, 1, waits=[v3, init_memset])
            lnvs_war[s] = a1
            a2 = kb.op("scalar",
                       lambda e, s=s: e.activation(lnrstd[s], lnln[s], AF.Exp,
                                                   scale=-0.5),
                       ae, 1, waits=[a1])
            v4 = kb.op("vector",
                       lambda e, s=s: e.tensor_scalar(
                           xnb_slot(s), xf_slot(s), lnmean[s], lnrstd[s],
                           ALU.subtract, ALU.mult),
                       ve, 1, waits=[a2, xnb_war[s]])
            xf_war[s] = v4
            for rr in range(2):
                bank = rr
                pview = P[bank].bitcast(BF16)
                for j in range(8):
                    w = ([v4, pA_read[bank]] if j == 0 else [])
                    kb.op("tensor",
                          lambda e, pv=pview, s=s, rr=rr, j=j: e.transpose(
                              pv[:, j * 128:(j + 1) * 128],
                              xnb_slot(s)[:, (rr * 8 + j) * 128:
                                          (rr * 8 + j + 1) * 128],
                              IDENTB[:]),
                          pe if j == 7 else None, 1 if j == 7 else 0, waits=w)
                tdone = (pe, pe.n)
                ev = kb.op("scalar",
                           lambda e, pv=pview, t=t, rr=rr: e.copy(
                               XNT[:, 0:ND * SEQ].rearrange(
                                   "p (d s) -> p d s", d=ND)[
                                   :, rr * 8:(rr + 1) * 8,
                                   t * 128:(t + 1) * 128],
                               pv[:].rearrange("p (d s) -> p d s", d=8)),
                           ae, 1, waits=[tdone])
                pA_read[rr] = ev
                lnt_last_evac = ev
            lnt_tile_done[t] = lnt_last_evac
            xnb_war[s] = lnt_last_evac
            if t % 4 == 3:
                emit_qkv_block(0, t // 4)
            emit_passA(6)

        xnt_ready = lnt_last_evac
        xnt_wr = None
        for q in range(4):
            xnt_wr = kb.op("gpsimd",
                           lambda e, q=q: e.dma_start(
                               out=dxnT[:, q * 8192:(q + 1) * 8192],
                               in_=XNT[:, q * 8192:(q + 1) * 8192]),
                           s_xntw, 16, waits=[xnt_ready])

        pt_war = [None, None, None, None]
        pending_epi = [None]
        sj = 0
        prow_rd = None
        pv_read = [None, None]
        exp_last_prev = None
        otd_wr_last = None
        xnt_attn_last_read = None
        xnb_attn_last = None
        sums_stop = None
        pv_stop = None



        for h in range(H):
            hs = h % 2
            if h == 1 and st["s_done"][0] is None:
                prow_rd = finalize_scale(0, prow_rd)
            if h == 3 and st["s_done"][1] is None:
                prow_rd = finalize_scale(1, prow_rd)
            if h < 3:
                emit_passA(22)

            if h + 1 < H:
                ns = (h + 1) % 2
                gq[ns] = kb.op("gpsimd", cast_load(wq_s(ns), (h + 1) * 128),
                               s_wq[ns], 16, waits=[wq_war[ns]])
                gk[ns] = kb.op("gpsimd", cast_load(wk_s(ns), DM + (h + 1) * 128),
                               s_wk[ns], 16, waits=[wk_war[ns]])
                gv[ns] = kb.op("gpsimd",
                               cast_load(wv_s(ns), 2 * DM + (h + 1) * 128),
                               s_wv[ns], 16, waits=[wv_war[ns]])

            if h > 0:
                emit_qkv_block(h, 0)
                if pending_epi[0] is not None:
                    pending_epi[0]()
                    pending_epi[0] = None
                for c in range(1, NCH):
                    emit_qkv_block(h, c)
            qk_evac_last = qk_evac[(h, "k", NCH - 1)]
            anch_rounds = [anch_round[(h, g)] for g in range(4)]
            if h == H - 1:
                xnt_attn_last_read = (pe, pe.n)

            am = kb.op("vector",
                       lambda e: e.tensor_reduce(ANCM[:], ANCH[:], axis=AX.X,
                                                 op=ALU.min),
                       ve, 1, waits=anch_rounds)
            tr = kb.op("tensor",
                       lambda e: e.transpose(PROW[0:1, 0:128], ANCM[:], IDENT[:]),
                       pe, 1, waits=[am, prow_rd])
            nm = kb.op("vector",
                       lambda e: e.tensor_reduce(NM1[:], PROW[0:1, 0:128],
                                                 axis=AX.X, op=ALU.min),
                       ve, 1, waits=[tr])
            prow_rd = nm
            nm2 = kb.op("vector",
                        lambda e: e.tensor_scalar(NM2[:], NM1[:], SCL, 1.0,
                                                  ALU.mult, ALU.mult),
                        ve, 1, waits=[nm])
            bc = kb.op("tensor",
                       lambda e: e.matmul(P7[:, 256:257], ONES1[:], NM2[:],
                                          start=True, stop=True),
                       pe, 1, waits=[nm2, st["p7f"]])
            negm_ready = kb.op("vector",
                               lambda e: e.tensor_copy(NEGM[:], P7[:, 256:257]),
                               ve, 1, waits=[bc, exp_last_prev])
            st["p7f"] = negm_ready

            exp_last = None
            for c in range(NCH):
                pvbank = 4 + (c % 2)
                njt = 4 * c + 4
                early = []
                for j in range(njt):
                    sbank = sj % 4
                    pts = sj % 4
                    sj += 1
                    smm = kb.op("tensor",
                                lambda e, sbk=sbank, j=j, c=c: e.matmul(
                                    P[sbk][:], kh[:, j * 128:(j + 1) * 128],
                                    qh[:, c * CH:(c + 1) * CH],
                                    start=True, stop=True),
                                pe, 1, waits=[qk_evac_last, pA_read[sbank]])
                    exp_dep = smm
                    if j >= 4 * c:
                        a0 = 384 - 128 * (j - 4 * c)
                        exp_dep = kb.op("vector",
                                        lambda e, sbk=sbank, a0=a0: e.tensor_tensor(
                                            P[sbk][:], P[sbk][:],
                                            MASKS[:, a0:a0 + 512], ALU.add),
                                        ve, 1, waits=[smm, init_loads])
                    ex = kb.op("scalar",
                               lambda e, pts=pts, sbk=sbank: e.activation(
                                   pt_slot(pts), P[sbk][:], AF.Exp,
                                   bias=NEGM[:], scale=SCL),
                               ae, 1, waits=[exp_dep, negm_ready, pt_war[pts]])
                    pA_read[sbank] = ex
                    exp_last = ex
                    if j < 2:
                        early.append((pts, j, ex))
                    if j == 2:
                        if pending_epi[0] is not None:
                            pending_epi[0]()
                            pending_epi[0] = None
                        for (pts0, j0, ex0) in early:
                            kb.op("tensor",
                                  lambda e, pts=pts0, j0=j0: e.matmul(
                                      PROW[:], ONESCB[:], pt_slot(pts),
                                      start=(j0 == 0), stop=False),
                                  None, 0,
                                  waits=([ex0, prow_rd, init_memset]
                                         if j0 == 0 else [ex0]))
                            pt_war[pts0] = (pe, pe.n)
                    if j >= 2:
                        kb.op("tensor",
                              lambda e, pts=pts, j=j, njt=njt: e.matmul(
                                  PROW[:], ONESCB[:], pt_slot(pts),
                                  start=False, stop=(j == njt - 1)),
                              pe if j == njt - 1 else None,
                              1 if j == njt - 1 else 0, waits=[ex])
                        if j == njt - 1:
                            sums_stop = (pe, pe.n)
                    pvw = ([ex, pv_read[c % 2]] if j == 0 else [ex])
                    pv = kb.op("tensor",
                               lambda e, pts=pts, j=j, njt=njt, pb=pvbank: e.matmul(
                                   P[pb][:], vh[:, j * 128:(j + 1) * 128],
                                   pt_slot(pts),
                                   start=(j == 0), stop=(j == njt - 1)),
                               pe, 1, waits=pvw)
                    pt_war[pts] = pv
                pv_stop = pv

                def make_epi(c=c, pvbank=pvbank, sums_stop=sums_stop,
                             pv_stop=pv_stop, h=h):
                    def epi():
                        nonlocal prow_rd, xnb_attn_last, otd_wr_last
                        al = kb.op("scalar",
                                   lambda e: e.activation(SRL[:], PROW[:], AF.Ln),
                                   ae, 1, waits=[sums_stop])
                        prow_rd = al
                        ar2 = kb.op("scalar",
                                    lambda e: e.activation(SREC[:], SRL[:], AF.Exp,
                                                           scale=-1.0),
                                    ae, 1, waits=[al])
                        bcm = kb.op("tensor",
                                    lambda e: e.matmul(P7[:], ONES1[:], SREC[:],
                                                       start=True, stop=True),
                                    pe, 1, waits=[ar2, st["p7f"]])
                        rcp = kb.op("vector",
                                    lambda e: e.tensor_copy(RECIPB, P7[:]),
                                    ve, 1, waits=[bcm])
                        st["p7f"] = rcp
                        om = kb.op("vector",
                                   lambda e, c=c, pb=pvbank: e.tensor_tensor(
                                       otst_slot(c % 2), P[pb][:], RECIPB,
                                       ALU.mult),
                                   ve, 1, waits=[pv_stop, rcp, otd_wr_last])
                        pv_read[c % 2] = om
                        xnb_attn_last = om
                        otd_wr_last = kb.op("gpsimd",
                                            lambda e, h=h, c=c: e.dma_start(
                                                out=dotd[:, h * SEQ + c * CH:
                                                         h * SEQ + (c + 1) * CH],
                                                in_=otst_slot(c % 2)),
                                            s_otd, 16, waits=[om])
                    return epi

                pending_epi[0] = make_epi()
                if h >= 3:
                    emit_passB(2, pA_read)
                elif st["Ai"] < len(units):
                    emit_passA(4)
            exp_last_prev = exp_last

        if pending_epi[0] is not None:
            pending_epi[0]()
            pending_epi[0] = None
        attn_pe_done = (pe, pe.n)

        woutp_war = [None, None]
        otst2_war = [None, None, None, None]
        sst_war = [None, None]
        pB_read = [pA_read[0], pA_read[1]]
        sd_wr_last = None
        sd_last2 = [None, None]

        def emit_otl(k):
            tt = k % NT
            osl = k % 4
            return kb.op("sync",
                         lambda e, tt=tt, o=osl: e.dma_start(
                             out=otst2_slot(o).rearrange("p (g c) -> p g c", g=H),
                             in_=dotd.rearrange("p (g s) -> p g s", g=H)[
                                 :, :, tt * 128:(tt + 1) * 128]),
                         s_ot2[osl], 16,
                         waits=[otd_wr_last, otst2_war[osl], xnt_attn_last_read])

        def emit_woutld(dmc, extra=None):
            return kb.op(
                "gpsimd",
                lambda e, dmc=dmc: e.dma_start(
                    out=woutp_slot(dmc % 2).rearrange("p (t m) -> p t m", t=ND),
                    in_=dwout[:, dmc * CH:(dmc + 1) * CH].rearrange(
                        "(t p) m -> p t m", p=128)),
                s_wop[dmc % 2], 16, waits=[xnt_attn_last_read, extra])

        wout_ld = [None] * 4
        wout_ld[0] = emit_woutld(0)
        wout_ld[1] = emit_woutld(1)

        otl_q = [emit_otl(0), emit_otl(1), emit_otl(2)]
        for dmc in range(4):
            wsl = dmc % 2
            if dmc >= 2:
                wout_ld[dmc] = emit_woutld(dmc, extra=woutp_war[wsl])
            for tt in range(NT):
                k = dmc * NT + tt
                osl = k % 4
                otl = otl_q.pop(0)
                if k + 3 < 4 * NT:
                    otl_q.append(emit_otl(k + 3))
                bank = (dmc * NT + tt) % 2
                for hh in range(H):
                    w = ([otl, wout_ld[dmc], pB_read[bank], attn_pe_done,
                          st["qtrans_bank_read"][bank]]
                         if hh == 0 else [])
                    kb.op("tensor",
                          lambda e, b=bank, o=osl, hh=hh, ww=wsl: e.matmul(
                              P[b][:], otst2_slot(o)[:, hh * 128:(hh + 1) * 128],
                              woutp_slot(ww)[:, hh * CH:(hh + 1) * CH],
                              start=(hh == 0), stop=(hh == H - 1)),
                          pe if hh == H - 1 else None, 1 if hh == H - 1 else 0,
                          waits=w)
                smm = (pe, pe.n)
                otst2_war[osl] = smm
                woutp_war[wsl] = smm
                ssl = k % 2
                ev = kb.op("scalar",
                           lambda e, b=bank, o=ssl: e.copy(
                               sst_slot(o), P[b][:]),
                           ae, 1, waits=[smm, sst_war[ssl], xnb_attn_last,
                                         exp_last_prev])
                pB_read[bank] = ev
                sd_wr_last = kb.op("gpsimd",
                                   lambda e, tt=tt, dmc=dmc, o=ssl: e.dma_start(
                                       out=dsd[tt * 128:(tt + 1) * 128,
                                               dmc * CH:(dmc + 1) * CH],
                                       in_=sst_slot(o)),
                                   s_sd[ssl], 16, waits=[ev])
                sst_war[ssl] = sd_wr_last
                sd_last2[ssl] = sd_wr_last
                emit_passB(1, pB_read)
            emit_passB(12, pB_read)
        emit_passB(len(units), pB_read)
        assert st["Bi"] == len(units), f"passB incomplete {st['Bi']}"
        phaseB_pe_done = (pe, pe.n)

        b2b_ready = None
        for dmc in range(4):
            bm = kb.op("tensor",
                       lambda e, dmc=dmc: e.matmul(
                           P7[:], ONES1[:], B2ROW[:, dmc * CH:(dmc + 1) * CH],
                           start=True, stop=True),
                       pe, 1, waits=[st["p7f"], init_loads, attn_pe_done,
                                     b2b_ready])
            b2b_ready = kb.op("vector",
                              lambda e, dmc=dmc: e.tensor_copy(
                                  B2B[:, dmc * 512:(dmc + 1) * 512], P7[:]),
                              ve, 1, waits=[bm])
            st["p7f"] = b2b_ready

        pC_h_read = list(pB_read)
        pC_g_read = [None, None, None, None]
        w1p_war = [None, None]
        xnc_war = None
        w2m_war = [None, None, None, None]
        g_war = [None, None]
        st5_war = [None, None]
        xc_war = [None, None]
        zt_war = [None, None]
        ht_war = phaseB_pe_done
        y_last = None
        w1p_pend = {}

        def emit_xnc(c, first=False):
            return kb.op("sync",
                         lambda e, c=c: e.dma_start(
                             out=XNC.rearrange("p (d s) -> p d s", d=ND),
                             in_=dxnT.rearrange("p (d s) -> p d s", d=ND)[
                                 :, :, c * CH:(c + 1) * CH]),
                         s_xnc, 16,
                         waits=[xnc_war, xnt_wr,
                                st["qin_war"][0], st["qin_war"][1],
                                phaseB_pe_done if first else None])

        def emit_panel(c, fc):
            psl = fc % 2
            return kb.op("sync",
                         lambda e, fc=fc: e.dma_start(
                             out=w1p_slot(fc % 2).rearrange(
                                 "p (d f) -> p d f", d=ND),
                             in_=dw1t.rearrange("p (d f) -> p d f", d=ND)[
                                 :, :, fc * 512:(fc + 1) * 512]),
                         s_w1p[psl], 16,
                         waits=[st["w1t_done0"], st["w1t_done1"], w1p_war[psl],
                                st["qpanel_war"][0], st["qpanel_war"][1],
                                phaseB_pe_done if (c == 0 and fc < 2) else None])

        xnc_pend = emit_xnc(0, first=True)
        w1p_pend[(0, 0)] = emit_panel(0, 0)
        w1p_pend[(0, 1)] = emit_panel(0, 1)

        for c in range(NCH):
            xnc_all = xnc_pend
            last_gelu_of_f = {}
            for fc in range(16):
                psl = fc % 2
                pl_all = w1p_pend.pop((c, fc))
                for fl in range(4):
                    f = fc * 4 + fl
                    bank = f % 2
                    for dt in range(ND):
                        w = ([pl_all, xnc_all, pC_h_read[bank],
                              st["qtrans_bank_read"][bank]] if dt == 0 else [])
                        kb.op("tensor",
                              lambda e, b=bank, p=psl, dt=dt, fl=fl: e.matmul(
                                  P[b][:],
                                  w1p_slot(p)[:, dt * 512 + fl * 128:
                                              dt * 512 + (fl + 1) * 128],
                                  xnc_d(dt),
                                  start=(dt == 0), stop=(dt == ND - 1)),
                              pe if dt == ND - 1 else None,
                              1 if dt == ND - 1 else 0, waits=w)
                    hmm_done = (pe, pe.n)
                    w1p_war[psl] = hmm_done
                    gl = kb.op("scalar",
                               lambda e, b=bank, f=f: e.activation(
                                   XNT[:, f * 512:(f + 1) * 512], P[b][:],
                                   AF.Gelu, bias=B1C[:, f:f + 1]),
                               ae, 1, waits=[hmm_done, b1c_ready, ht_war])
                    pC_h_read[bank] = gl
                    last_gelu_of_f[f] = gl
                nfc, nc_ = (fc + 2, c) if fc + 2 < 16 else (fc + 2 - 16, c + 1)
                if nc_ < NCH:
                    w1p_pend[(nc_, nfc)] = emit_panel(nc_, nfc)
            xnc_war = (pe, pe.n)
            if DBG_SCRATCH:
                kb.op("gpsimd",
                      lambda e, c=c: e.dma_start(
                          out=dbg_ht[:, c * NF * CH:(c + 1) * NF * CH],
                          in_=XNT[:, 0:NF * CH]),
                      s_init, 16, waits=[last_gelu_of_f[NF - 1]])
            if c + 1 < NCH:
                xnc_pend = emit_xnc(c + 1)

            for dmc in range(4):
                for fp in range(NF // 2):
                    dsl = fp % 4
                    w2ld = kb.op("sync",
                                 lambda e, dmc=dmc, fp=fp, m=dsl: e.dma_start(
                                     out=w2m_dslot(m).rearrange(
                                         "p (b n) -> p b n", b=2),
                                     in_=dw2t.rearrange("p (ff n) -> p ff n",
                                                        ff=NF)[
                                         :, 2 * fp:2 * fp + 2,
                                         dmc * CH:(dmc + 1) * CH]),
                                 s_w2m[dsl], 16,
                                 waits=[st["w2t_done0"], st["w2t_done1"],
                                        w2m_war[dsl],
                                        sd_last2[0] if (c == 0 and dmc == 0)
                                        else None,
                                        sd_last2[1] if (c == 0 and dmc == 0)
                                        else None,
                                        xnb_attn_last if (c == 0 and dmc == 0)
                                        else None])
                    for fh in range(2):
                        f = 2 * fp + fh
                        for tt in range(4):
                            gbank = 2 + tt
                            w = [w2ld, last_gelu_of_f[f]]
                            if f == 0:
                                w.append(pC_g_read[tt])
                            kb.op("tensor",
                                  lambda e, gb=gbank, f=f, tt=tt, m=dsl, fh=fh:
                                  e.matmul(
                                      P[gb][:], XNT[:, f * 512 + tt * 128:
                                                    f * 512 + (tt + 1) * 128],
                                      w2m_dslot(m)[:, fh * 512:(fh + 1) * 512],
                                      start=(f == 0), stop=(f == NF - 1)),
                                  pe if (fh == 1 and tt == 3) else None,
                                  1 if (fh == 1 and tt == 3) else 0, waits=w)
                    w2m_war[dsl] = (pe, pe.n)
                g_stop = (pe, pe.n)
                gpar = dmc % 2
                for tt in range(4):
                    gbank = 2 + tt
                    zb = kb.op("vector",
                               lambda e, gb=gbank, dmc=dmc, tt=tt: e.tensor_tensor(
                                   zt_slot(tt % 2), P[gb][:],
                                   B2B[:, dmc * 512:(dmc + 1) * 512], ALU.add),
                               ve, 1, waits=[g_stop, b2b_ready, zt_war[tt % 2]])
                    sl = kb.op("scalar",
                               lambda e, gp=gpar, tt=tt: e.activation(
                                   g_slice(gp, tt), zt_slot(tt % 2), AF.Silu),
                               ae, 1, waits=[zb, g_war[gpar]])
                    zt_war[tt % 2] = sl
                    pC_g_read[tt] = sl
                if DBG_SCRATCH:
                    kb.op("gpsimd",
                          lambda e, c=c, dmc=dmc: e.dma_start(
                              out=dbg_g[:, (c * 4 + dmc) * 2048:
                                        (c * 4 + dmc + 1) * 2048],
                              in_=WQB[:, 6144 + (dmc % 2) * 2048:
                                      6144 + (dmc % 2 + 1) * 2048]),
                          s_init, 16, waits=[pC_g_read[0], pC_g_read[1],
                                           pC_g_read[2], pC_g_read[3]])
                for tt in range(4):
                    trow = c * 4 + tt
                    k2 = (dmc * 4 + tt) % 2
                    stld = kb.op("gpsimd",
                                 lambda e, trow=trow, dmc=dmc, s=k2: e.dma_start(
                                     out=st5_slot(s),
                                     in_=dsd[trow * 128:(trow + 1) * 128,
                                             dmc * CH:(dmc + 1) * CH]),
                                 s_st5[k2], 16,
                                 waits=[sd_last2[0], sd_last2[1], st5_war[k2],
                                        st["qout_war"][0], st["qout_war"][1]])
                    xld = kb.op("sync",
                                lambda e, trow=trow, dmc=dmc, s=k2: e.dma_start(
                                    out=xc_slot(s),
                                    in_=dx[trow * 128:(trow + 1) * 128,
                                           dmc * CH:(dmc + 1) * CH]),
                                s_xc[k2], 16,
                                waits=[xc_war[k2], st["qout_war"][0],
                                       st["qout_war"][1], st["prev_r"]])
                    sg = kb.op("vector",
                               lambda e, s=k2, gp=gpar, tt=tt: e.tensor_tensor(
                                   st5_slot(s), st5_slot(s), g_slice(gp, tt),
                                   ALU.mult),
                               ve, 1, waits=[stld, pC_g_read[tt]])
                    g_war[gpar] = sg
                    ad = kb.op("vector",
                               lambda e, s=k2: e.tensor_tensor(
                                   xc_slot(s), xc_slot(s), st5_slot(s), ALU.add),
                               ve, 1, waits=[sg, xld])
                    st5_war[k2] = ad
                    y_last = kb.op("sync",
                                   lambda e, trow=trow, dmc=dmc, s=k2: e.dma_start(
                                       out=dy[trow * 128:(trow + 1) * 128,
                                              dmc * CH:(dmc + 1) * CH],
                                       in_=xc_slot(s)),
                                   s_y[k2], 16, waits=[ad])
                    xc_war[k2] = y_last
                ht_war = g_stop

        kb.op("sync", lambda e: e.nop(), waits=[y_last])

        for s_ in kb.sems:
            s_.h = E(nc.semaphore(s_.name))
        block = E(nc.Block())

        @block.sync
        def _(e):
            kb.realize("sync", e)

        @block.scalar
        def _(e):
            kb.realize("scalar", e)

        @block.vector
        def _(e):
            kb.realize("vector", e)

        @block.tensor
        def _(e):
            kb.realize("tensor", e)

        @block.gpsimd
        def _(e):
            kb.realize("gpsimd", e)

    return nc


_NC_CACHE = None


def kernel(L0_attn_x, L0_attn_w_qkv, L0_attn_w_out, L0_ffn_w1, L0_ffn_b1,
           L0_ffn_w2, L0_ffn_b2):
    global _NC_CACHE
    x = np.ascontiguousarray(np.asarray(L0_attn_x, dtype=np.float32))
    wqkv = np.ascontiguousarray(np.asarray(L0_attn_w_qkv, dtype=np.float32))
    wout = np.ascontiguousarray(np.asarray(L0_attn_w_out, dtype=np.float32))
    w1 = np.ascontiguousarray(np.asarray(L0_ffn_w1, dtype=np.float32))
    b1 = np.ascontiguousarray(np.asarray(L0_ffn_b1, dtype=np.float32)).reshape(NF, 128)
    w2 = np.ascontiguousarray(np.asarray(L0_ffn_w2, dtype=np.float32))
    b2 = np.ascontiguousarray(np.asarray(L0_ffn_b2, dtype=np.float32)).reshape(1, DM)
    ident = np.eye(128, dtype=np.float32)
    identb = np.asarray(np.eye(128, dtype=np.float32).astype(ml_dtypes.bfloat16))
    ones1 = np.ones((1, 128), dtype=np.float32)
    masks = build_masks()

    if _NC_CACHE is None:
        _NC_CACHE = build_nc()
    nc = _NC_CACHE

    in_maps = []
    for i in range(N_CORES):
        in_maps.append({
            "x": np.ascontiguousarray(x[i]),
            "wqkv": wqkv, "wout": wout, "w1": w1, "b1": b1,
            "w2": w2, "b2": b2, "ident": ident, "identb": identb,
            "ones1": ones1, "masks": masks,
        })
    res = run_bass_kernel_spmd(nc, in_maps, core_ids=list(range(N_CORES)))
    out = np.stack([np.asarray(r["y"], dtype=np.float32) for r in res.results])
    return out


# version 07
# speedup vs baseline: 1.6550x; 1.1639x over previous
import sys
sys.path.insert(0, "/opt/trn_rl_repo")
from contextlib import ExitStack
import numpy as np
import ml_dtypes

import concourse.bass as bass
import concourse.mybir as mybir
from concourse.bass_utils import run_bass_kernel_spmd

F32 = mybir.dt.float32
BF16 = mybir.dt.bfloat16
AF = mybir.ActivationFunctionType
ALU = mybir.AluOpType
AX = mybir.AxisListType

SEQ = 2048
DM = 2048
H = 16
DFF = 8192
NT = 16
ND = 16
NF = 64
NCH = 4
CH = 512
SCL = float(1.0 / np.sqrt(128.0))
NEG = -1e10
N_CORES = 8

ENGS = ("sync", "scalar", "vector", "tensor", "gpsimd")


class Sem:
    __slots__ = ("name", "n", "h")

    def __init__(self, name):
        self.name = name
        self.n = 0
        self.h = None


class KB:
    def __init__(self):
        self.ops = {e: [] for e in ENGS}
        self.sems = []
        self.guar = {e: {} for e in ENGS}

    def sem(self, name):
        s = Sem(name)
        self.sems.append(s)
        return s

    def wait(self, eng, sm, val):
        if sm is None or val <= 0:
            return
        if self.guar[eng].get(sm, 0) >= val:
            return
        self.guar[eng][sm] = val
        self.ops[eng].append(("w", sm, val))

    def op(self, eng, fn, sm=None, inc=0, waits=()):
        for w in waits:
            if w is not None:
                self.wait(eng, w[0], w[1])
        if sm is not None:
            sm.n += inc
            self.ops[eng].append(("o", fn, sm, inc))
            return (sm, sm.n)
        self.ops[eng].append(("o", fn, None, 0))
        return None

    def realize(self, eng, h):
        for item in self.ops[eng]:
            if item[0] == "w":
                h.wait_ge(item[1].h, item[2])
            else:
                inst = item[1](h)
                if item[2] is not None:
                    inst.then_inc(item[2].h, item[3])


def build_masks():
    kl = np.arange(128)[:, None]
    x = np.arange(896)[None, :]
    m = np.where(x >= kl + 384, 0.0, NEG).astype(np.float32)
    return np.asarray(m.astype(ml_dtypes.bfloat16))


DBG_SCRATCH = False


def build_nc():
    nc = bass.Bass("TRN2", target_bir_lowering=False)

    dx = nc.dram_tensor("x", [SEQ, DM], F32, kind="ExternalInput").ap()
    dwqkv = nc.dram_tensor("wqkv", [DM, 3 * DM], F32, kind="ExternalInput").ap()
    dwout = nc.dram_tensor("wout", [DM, DM], F32, kind="ExternalInput").ap()
    dw1 = nc.dram_tensor("w1", [DFF, DM], F32, kind="ExternalInput").ap()
    db1 = nc.dram_tensor("b1", [NF, 128], F32, kind="ExternalInput").ap()
    dw2 = nc.dram_tensor("w2", [DM, DFF], F32, kind="ExternalInput").ap()
    db2 = nc.dram_tensor("b2", [1, DM], F32, kind="ExternalInput").ap()
    dident = nc.dram_tensor("ident", [128, 128], F32, kind="ExternalInput").ap()
    didentb = nc.dram_tensor("identb", [128, 128], BF16, kind="ExternalInput").ap()
    dones1 = nc.dram_tensor("ones1", [1, 128], F32, kind="ExternalInput").ap()
    dmasks = nc.dram_tensor("masks", [128, 896], BF16, kind="ExternalInput").ap()
    dy = nc.dram_tensor("y", [SEQ, DM], F32, kind="ExternalOutput").ap()

    _SK = "ExternalOutput" if DBG_SCRATCH else "Internal"
    dbg_ht = (nc.dram_tensor("dbg_ht", [128, 4 * NF * CH], BF16,
                             kind="ExternalOutput").ap() if DBG_SCRATCH else None)
    dbg_g = (nc.dram_tensor("dbg_g", [128, 4 * 4 * 2048], BF16,
                            kind="ExternalOutput").ap() if DBG_SCRATCH else None)
    dxnT = nc.dram_tensor("dxnT", [128, ND * SEQ], BF16, kind=_SK).ap()
    dotd = nc.dram_tensor("dotd", [128, H * SEQ], BF16, kind=_SK).ap()
    dsd = nc.dram_tensor("sd", [SEQ, DM], BF16, kind=_SK).ap()
    dw1t = nc.dram_tensor("w1t", [128, ND * DFF], BF16, kind=_SK).ap()
    dw2t = nc.dram_tensor("w2t", [128, NF * DM], BF16, kind=_SK).ap()

    kb = KB()
    pe = kb.sem("pe")
    ve = kb.sem("ve")
    ae = kb.sem("ae")
    ge = kb.sem("ge")
    s_init = kb.sem("s_init")
    s_xf = [kb.sem(f"s_xf{i}") for i in range(4)]
    s_qin = [kb.sem("s_qin0"), kb.sem("s_qin1")]
    s_w1t = [kb.sem("s_w1t0"), kb.sem("s_w1t1")]
    s_w2t = [kb.sem("s_w2t0"), kb.sem("s_w2t1")]
    s_xntw = kb.sem("s_xntw")
    s_otd = kb.sem("s_otd")
    s_sd = [kb.sem("s_sd0"), kb.sem("s_sd1")]
    s_wq = [kb.sem("s_wq0"), kb.sem("s_wq1")]
    s_wk = [kb.sem("s_wk0"), kb.sem("s_wk1")]
    s_wv = [kb.sem("s_wv0"), kb.sem("s_wv1")]
    s_wop = [kb.sem("s_wop0"), kb.sem("s_wop1")]
    s_ot2 = [kb.sem(f"s_ot2{i}") for i in range(4)]
    s_xnc = kb.sem("s_xnc")
    s_w1p = [kb.sem("s_w1p0"), kb.sem("s_w1p1")]
    s_w2m = [kb.sem(f"s_w2m{i}") for i in range(4)]
    s_st5 = [kb.sem("s_st50"), kb.sem("s_st51")]
    s_xc = [kb.sem("s_xc0"), kb.sem("s_xc1")]
    s_y = [kb.sem("s_y0"), kb.sem("s_y1")]

    with ExitStack() as ctx:
        E = ctx.enter_context
        sb, ps = nc.sbuf_tensor, nc.psum_tensor

        XNT = E(sb("XNT", [128, ND * SEQ], BF16))
        BIGB = E(sb("BIGB", [128, 24576], BF16))
        QKVB = E(sb("QKVB", [128, 3 * SEQ], BF16))
        WQB = E(sb("WQB", [128, 2 * 3 * DM], BF16))
        XNB = E(sb("XNB", [128, 4096], BF16))
        WP = E(sb("WP", [128, 8192], BF16))
        MASKS = E(sb("MASKS", [128, 896], BF16))
        IDENT = E(sb("IDENT", [128, 128], F32))
        IDENTB = E(sb("IDENTB", [128, 128], BF16))
        ONES1 = E(sb("ONES1", [1, 128], F32))
        ONESCB = E(sb("ONESCB", [128, 1], BF16))
        ONESCF = E(sb("ONESCF", [128, 1], F32))
        EPSC = E(sb("EPSC", [128, 1], F32))
        B1RS = E(sb("B1RS", [64, 128], F32))
        B1C = E(sb("B1C", [128, 64], F32))
        B2ROW = E(sb("B2ROW", [1, DM], F32))
        LNS = E(sb("LNS", [128, 24], F32))
        WACC = E(sb("WACC", [128, 128], F32))
        ANCH = E(sb("ANCH", [128, 4], F32))
        ANCM = E(sb("ANCM", [128, 1], F32))
        NEGM = E(sb("NEGM", [128, 1], F32))
        NM1 = E(sb("NM1", [1, 1], F32))
        NM2 = E(sb("NM2", [1, 1], F32))
        SRL = E(sb("SRL", [1, CH], F32))
        SREC = E(sb("SREC", [1, CH], F32))
        SCLB = [E(sb("SCL1", [1, 1], F32)), E(sb("SCL2", [1, 1], F32))]
        TCOL = [E(sb("TC1", [128, 1], F32)), E(sb("TC2", [128, 1], F32))]
        NTCOL = [E(sb("NTC1", [128, 1], F32)), E(sb("NTC2", [128, 1], F32))]
        WSUM = [E(sb("WS1", [128, 1], F32)), E(sb("WS2", [128, 1], F32))]

        P = [E(ps(f"P{i}", [128, 512], F32)) for i in range(6)]
        PROW = E(ps("PROW", [1, 512], F32))
        P7 = E(ps("P7", [128, 512], F32))

        def xf_slot(i):
            return BIGB[:, i * 4096:(i + 1) * 4096].bitcast(F32)

        def xnb_slot(i):
            return BIGB[:, 16384 + i * 2048:16384 + (i + 1) * 2048]

        LNDUMP = BIGB[:, 20480:24576].bitcast(F32)

        def qpanel_slot(i):
            return BIGB[:, i * 8192:(i + 1) * 8192]

        def qin_slot(i):
            return BIGB[:, 16384 + i * 4096:16384 + (i + 1) * 4096].bitcast(F32)

        def w1p_slot(i):
            return BIGB[:, i * 8192:(i + 1) * 8192]

        XNC = BIGB[:, 16384:24576]

        def xnc_d(dt):
            return BIGB[:, 16384 + dt * 512:16384 + (dt + 1) * 512]

        def woutp_slot(i):
            return XNT[:, i * 8192:(i + 1) * 8192]

        def otst2_slot(i):
            return XNT[:, 16384 + i * 2048:16384 + (i + 1) * 2048]

        qh = QKVB[:, 0:SEQ]
        kh = QKVB[:, SEQ:2 * SEQ]
        vh = QKVB[:, 2 * SEQ:3 * SEQ]

        def wq_s(i):
            return WQB[:, i * 6144 + 0:i * 6144 + DM]

        def wk_s(i):
            return WQB[:, i * 6144 + DM:i * 6144 + 2 * DM]

        def wv_s(i):
            return WQB[:, i * 6144 + 2 * DM:i * 6144 + 3 * DM]

        B2B = WQB[:, 0:4096].bitcast(F32)

        def zt_slot(i):
            return WQB[:, 4096 + i * 1024:4096 + (i + 1) * 1024].bitcast(F32)

        def g_slice(par, tt):
            return WQB[:, 6144 + par * 2048 + tt * 512:
                       6144 + par * 2048 + (tt + 1) * 512]

        _PT_OFF = [0, 512, 3072, 3584]

        def pt_slot(i):
            return XNB[:, _PT_OFF[i]:_PT_OFF[i] + 512]

        RECIPB = XNB[:, 1024:2048].bitcast(F32)

        def otst_slot(i):
            return XNB[:, 2048 + i * 512:2048 + (i + 1) * 512]

        def sst_slot(i):
            return XNB[:, i * 512:(i + 1) * 512]

        def w2m_dslot(i):
            return XNB[:, i * 1024:(i + 1) * 1024]

        def qout_slot(i):
            return WP[:, i * 2048:(i + 1) * 2048]

        QTMP = WP[:, 4096:8192].bitcast(F32)

        def xc_slot(i):
            return WP[:, i * 1024:(i + 1) * 1024].bitcast(F32)

        def st5_slot(i):
            return WP[:, 2048 + i * 512:2048 + (i + 1) * 512]

        lnsum = [LNS[:, i:i + 1] for i in range(0, 4)]
        lnmean = [LNS[:, i:i + 1] for i in range(4, 8)]
        lnvs = [LNS[:, i:i + 1] for i in range(8, 12)]
        lnln = [LNS[:, i:i + 1] for i in range(12, 16)]
        lnrstd = [LNS[:, i:i + 1] for i in range(16, 20)]

        for dst, src in ((IDENT[:], dident), (IDENTB[:], didentb),
                         (ONES1[:], dones1), (MASKS[:], dmasks),
                         (B1RS[:], db1), (B2ROW[:], db2)):
            kb.op("sync", lambda e, d=dst, s=src: e.dma_start(out=d, in_=s),
                  s_init, 16)
        init_loads = (s_init, s_init.n)

        kb.op("gpsimd", lambda e: e.memset(ONESCB[:], 1.0), ge, 1)
        kb.op("gpsimd", lambda e: e.memset(ONESCF[:], 1.0), ge, 1)
        kb.op("gpsimd", lambda e: e.memset(EPSC[:], 1e-5), ge, 1)
        init_memset = (ge, ge.n)

        b1t = kb.op("tensor", lambda e: e.transpose(P7[:, 0:64], B1RS[:],
                                                    IDENT[0:64, 0:64]),
                    pe, 1, waits=[init_loads])
        b1c_ready = kb.op("vector", lambda e: e.tensor_copy(B1C[:], P7[:, 0:64]),
                          ve, 1, waits=[b1t])

        units = [(0, dh, ch) for dh in range(ND) for ch in range(4)] + \
                [(1, fh, 0) for fh in range(NF)]

        def unit_src(mat, pnl, ch):
            if mat == 0:
                return dw1[ch * 2048:(ch + 1) * 2048,
                           pnl * 128:(pnl + 1) * 128].rearrange(
                    "(t p) d -> p t d", p=128)
            return dw2[:, pnl * 128:(pnl + 1) * 128].rearrange(
                "(t p) d -> p t d", p=128)

        st = {
            "Ai": 0, "Bi": 0,
            "qin_war": [None, None],
            "qout_war": [None, None],
            "qpanel_war": [None, None],
            "w1A_done": None, "w2A_done": None,
            "s_done": [None, None],
            "prev_r": None,
            "w1t_done0": None, "w1t_done1": None,
            "w2t_done0": None, "w2t_done1": None,
            "qtrans_bank_read": [None, None],
            "p7f": None,
        }
        st["p7f"] = b1c_ready

        def emit_passA(nmax):
            for _ in range(nmax):
                if st["Ai"] >= len(units):
                    return
                mat, pnl, ch = units[st["Ai"]]
                slot = st["Ai"] % 2
                col = st["Ai"]
                v = kb.op("sync",
                          lambda e, m=mat, p=pnl, c=ch, sl=slot: e.dma_start(
                              out=qin_slot(sl).rearrange("p (t d) -> p t d", t=16),
                              in_=unit_src(m, p, c)),
                          s_qin[slot], 16, waits=[st["qin_war"][slot]])
                r = kb.op("vector",
                          lambda e, sl=slot, c=col: e.reduce_sum(
                              WACC[:, c:c + 1], qin_slot(sl), axis=AX.X,
                              apply_absolute_value=True),
                          ve, 1, waits=[v])
                st["qin_war"][slot] = r
                st["Ai"] += 1
                if st["Ai"] == 64:
                    st["w1A_done"] = r
                if st["Ai"] == 128:
                    st["w2A_done"] = r

        def finalize_scale(mat, prow_rd):
            a_done = st["w1A_done"] if mat == 0 else st["w2A_done"]
            assert a_done is not None
            v1 = kb.op("vector",
                       lambda e, m=mat: e.tensor_reduce(
                           WSUM[m][:], WACC[:, m * 64:(m + 1) * 64],
                           axis=AX.X, op=ALU.add),
                       ve, 1, waits=[a_done])
            t1 = kb.op("tensor",
                       lambda e, m=mat: e.matmul(PROW[0:1, 0:1], ONESCF[:],
                                                 WSUM[m][:], start=True, stop=True),
                       pe, 1, waits=[v1, init_memset, prow_rd])
            a1 = kb.op("scalar",
                       lambda e, m=mat: e.activation(SCLB[m][:], PROW[0:1, 0:1],
                                                     AF.Copy,
                                                     scale=0.7 / float(DFF * DM)),
                       ae, 1, waits=[t1])
            t2 = kb.op("tensor",
                       lambda e, m=mat: e.matmul(P7[:, 0:1], ONES1[:], SCLB[m][:],
                                                 start=True, stop=True),
                       pe, 1, waits=[a1, st["p7f"], init_loads])
            v2 = kb.op("vector",
                       lambda e, m=mat: e.tensor_copy(TCOL[m][:], P7[:, 0:1]),
                       ve, 1, waits=[t2])
            v3 = kb.op("vector",
                       lambda e, m=mat: e.tensor_scalar(
                           NTCOL[m][:], TCOL[m][:], -1.0, 1.0, ALU.mult, ALU.mult),
                       ve, 1, waits=[v2])
            st["s_done"][mat] = v3
            st["p7f"] = v2
            return a1

        def emit_passB(nmax, pA_read):
            for _ in range(nmax):
                if st["Bi"] >= len(units):
                    return
                mat, pnl, ch = units[st["Bi"]]
                if st["s_done"][mat] is None:
                    return
                ui = st["Bi"]
                islot = ui % 2
                oslot = ui % 2
                pslot = pnl % 2
                v = kb.op("sync",
                          lambda e, m=mat, p=pnl, c=ch, sl=islot: e.dma_start(
                              out=qin_slot(sl).rearrange("p (t d) -> p t d", t=16),
                              in_=unit_src(m, p, c)),
                          s_qin[islot], 16, waits=[st["qin_war"][islot]])
                r1 = kb.op("vector",
                           lambda e, sl=islot, m=mat: e.tensor_scalar(
                               QTMP, qin_slot(sl), TCOL[m][:], -1.0,
                               ALU.is_ge, ALU.add),
                           ve, 1, waits=[v, st["s_done"][mat], st["prev_r"]])
                r2 = kb.op("vector",
                           lambda e, sl=islot, o=oslot, m=mat:
                           e.scalar_tensor_tensor(
                               qout_slot(o), qin_slot(sl), NTCOL[m][:], QTMP,
                               ALU.is_gt, ALU.add),
                           ve, 1, waits=[r1, st["qout_war"][oslot]])
                st["prev_r"] = r2
                st["qin_war"][islot] = r2
                evacs = []
                pview = P7.bitcast(BF16)
                for rr in range(2):
                    for j in range(8):
                        w = ([r2, st["p7f"]] if j == 0 else [])
                        kb.op("tensor",
                              lambda e, pv=pview, o=oslot, rr=rr, j=j: e.transpose(
                                  pv[:, j * 128:(j + 1) * 128],
                                  qout_slot(o)[:, (rr * 8 + j) * 128:
                                               (rr * 8 + j + 1) * 128],
                                  IDENTB[:]),
                              pe if j == 7 else None, 1 if j == 7 else 0,
                              waits=w)
                    tdone = (pe, pe.n)
                    base = (ch * 2048 + rr * 1024) if mat == 0 else (rr * 1024)
                    evc = kb.op("scalar",
                                lambda e, pv=pview, psl=pslot, b=base: e.copy(
                                    qpanel_slot(psl)[:, b:b + 1024], pv[:]),
                                ae, 1, waits=[tdone, st["qpanel_war"][pslot]])
                    st["p7f"] = evc
                    evacs.append(evc)
                if mat == 0 and ch == 3:
                    wr = kb.op("gpsimd",
                               lambda e, p=pnl, psl=pslot: e.dma_start(
                                   out=dw1t[:, p * 8192:(p + 1) * 8192],
                                   in_=qpanel_slot(psl)),
                               s_w1t[pslot], 16, waits=evacs)
                    st["qpanel_war"][pslot] = wr
                    st["w1t_done" + str(pslot)] = wr
                elif mat == 1:
                    wr = kb.op("gpsimd",
                               lambda e, p=pnl, psl=pslot: e.dma_start(
                                   out=dw2t[:, p * 2048:(p + 1) * 2048],
                                   in_=qpanel_slot(psl)[:, 0:2048]),
                               s_w2t[pslot], 16, waits=evacs)
                    st["qpanel_war"][pslot] = wr
                    st["w2t_done" + str(pslot)] = wr
                st["qout_war"][oslot] = (pe, pe.n)
                st["Bi"] += 1

        xf_war = [None, None, None, None]
        lnvs_war = [None, None, None, None]
        xnb_war = [None, None]
        lnt_tile_done = [None] * NT
        lnt_last_evac = None
        pA_read = [None] * 6
        qk_evac = {}
        vh_evac = {}
        anch_round = {}
        wq_war = [None, None]
        wk_war = [None, None]
        wv_war = [None, None]

        def cast_load(dst, c0):
            return lambda e, d=dst, c=c0: e.dma_start(
                out=d.rearrange("p (t f) -> p t f", t=ND),
                in_=dwqkv[:, c:c + 128].rearrange("(t p) f -> p t f", p=128))

        gq = [None, None]
        gk = [None, None]
        gv = [None, None]
        gq[0] = kb.op("gpsimd", cast_load(wq_s(0), 0), s_wq[0], 16)
        gk[0] = kb.op("gpsimd", cast_load(wk_s(0), DM), s_wk[0], 16)
        gv[0] = kb.op("gpsimd", cast_load(wv_s(0), 2 * DM), s_wv[0], 16)

        def emit_qkv_block(h, c):
            hs = h % 2
            for bank, wsl, dst, gld, tag in (
                    (2, wq_s(hs), qh, gq[hs], "q"), (3, wk_s(hs), kh, gk[hs], "k")):
                for dt in range(ND):
                    w = ([gld, lnt_tile_done[4 * c + 3], pA_read[bank]]
                         if dt == 0 else [])
                    kb.op("tensor",
                          lambda e, b=bank, ww=wsl, dt=dt, c=c: e.matmul(
                              P[b][:], ww[:, dt * 128:(dt + 1) * 128],
                              XNT[:, dt * SEQ + c * CH:dt * SEQ + (c + 1) * CH],
                              start=(dt == 0), stop=(dt == ND - 1)),
                          pe if dt == ND - 1 else None,
                          1 if dt == ND - 1 else 0, waits=w)
                mm_done = (pe, pe.n)
                if tag == "q":
                    wq_war[hs] = mm_done
                else:
                    wk_war[hs] = mm_done
                ev = kb.op("scalar",
                           lambda e, b=bank, d=dst, c=c: e.copy(
                               d[:, c * CH:(c + 1) * CH], P[b][:]),
                           ae, 1, waits=[mm_done])
                pA_read[bank] = ev
                qk_evac[(h, tag, c)] = ev
            g = c
            bank = c % 2
            for ttl in range(4):
                tok = 4 * g + ttl
                for dt in range(ND):
                    w = ([gv[hs], lnt_tile_done[4 * g + 3], pA_read[bank]]
                         if (dt == 0 and ttl == 0) else [])
                    kb.op("tensor",
                          lambda e, b=bank, tok=tok, dt=dt, ttl=ttl, hs=hs:
                          e.matmul(
                              P[b][:, ttl * 128:(ttl + 1) * 128],
                              XNT[:, dt * SEQ + tok * 128:
                                  dt * SEQ + (tok + 1) * 128],
                              wv_s(hs)[:, dt * 128:(dt + 1) * 128],
                              start=(dt == 0), stop=(dt == ND - 1)),
                          pe if (dt == ND - 1 and ttl == 3) else None,
                          1 if (dt == ND - 1 and ttl == 3) else 0, waits=w)
            vg_done = (pe, pe.n)
            wv_war[hs] = vg_done
            ev = kb.op("scalar",
                       lambda e, b=bank, g=g: e.copy(
                           vh[:, g * CH:(g + 1) * CH], P[b][:]),
                       ae, 1, waits=[vg_done])
            pA_read[bank] = ev
            vh_evac[(h, g)] = ev
            for rr in range(4):
                qt = 4 * g + rr
                w = ([qk_evac[(h, "q", c)], qk_evac[(h, "k", c)], st["p7f"]]
                     if rr == 0 else [])
                kb.op("tensor",
                      lambda e, qt=qt, rr=rr: e.matmul(
                          P7[:, rr * 128:(rr + 1) * 128],
                          qh[:, qt * 128:(qt + 1) * 128],
                          kh[:, qt * 128:(qt + 1) * 128],
                          start=True, stop=True),
                      pe if rr == 3 else None, 1 if rr == 3 else 0, waits=w)
            diag_done = (pe, pe.n)
            ar = kb.op("vector",
                       lambda e, g=g: e.reduce_max(ANCH[:, g:g + 1], P7[:],
                                                   axis=AX.X, negate=True),
                       ve, 1, waits=[diag_done])
            anch_round[(h, g)] = ar
            st["p7f"] = ar
        for t in range(NT):
            s = t % 4
            s2 = t % 2
            xd = kb.op("sync",
                       lambda e, t=t, s=s: e.dma_start(
                           out=xf_slot(s), in_=dx[t * 128:(t + 1) * 128, :]),
                       s_xf[s], 16, waits=[xf_war[s]])
            v1 = kb.op("vector",
                       lambda e, s=s: e.reduce_sum(lnsum[s], xf_slot(s), axis=AX.X),
                       ve, 1, waits=[xd])
            v2 = kb.op("vector",
                       lambda e, s=s: e.tensor_scalar(
                           lnmean[s], lnsum[s], 1.0 / DM, 1.0, ALU.mult, ALU.mult),
                       ve, 1, waits=[v1])
            v3 = kb.op("vector",
                       lambda e, s=s: e.scalar_tensor_tensor(
                           LNDUMP, xf_slot(s), lnmean[s], xf_slot(s),
                           ALU.subtract, ALU.mult, accum_out=lnvs[s]),
                       ve, 1, waits=[v2, lnvs_war[s]])
            a1 = kb.op("scalar",
                       lambda e, s=s: e.activation(lnln[s], lnvs[s], AF.Ln,
                                                   bias=EPSC[:], scale=1.0 / DM),
                       ae, 1, waits=[v3, init_memset])
            lnvs_war[s] = a1
            a2 = kb.op("scalar",
                       lambda e, s=s: e.activation(lnrstd[s], lnln[s], AF.Exp,
                                                   scale=-0.5),
                       ae, 1, waits=[a1])
            v4 = kb.op("vector",
                       lambda e, s=s, s2=s2: e.tensor_scalar(
                           xnb_slot(s2), xf_slot(s), lnmean[s], lnrstd[s],
                           ALU.subtract, ALU.mult),
                       ve, 1, waits=[a2, xnb_war[s2]])
            xf_war[s] = v4
            for rr in range(2):
                bank = rr
                pview = P[bank].bitcast(BF16)
                for j in range(8):
                    w = ([v4, pA_read[bank]] if j == 0 else [])
                    kb.op("tensor",
                          lambda e, pv=pview, s2=s2, rr=rr, j=j: e.transpose(
                              pv[:, j * 128:(j + 1) * 128],
                              xnb_slot(s2)[:, (rr * 8 + j) * 128:
                                           (rr * 8 + j + 1) * 128],
                              IDENTB[:]),
                          pe if j == 7 else None, 1 if j == 7 else 0, waits=w)
                tdone = (pe, pe.n)
                ev = kb.op("scalar",
                           lambda e, pv=pview, t=t, rr=rr: e.copy(
                               XNT[:, 0:ND * SEQ].rearrange(
                                   "p (d s) -> p d s", d=ND)[
                                   :, rr * 8:(rr + 1) * 8,
                                   t * 128:(t + 1) * 128],
                               pv[:].rearrange("p (d s) -> p d s", d=8)),
                           ae, 1, waits=[tdone])
                pA_read[rr] = ev
                lnt_last_evac = ev
            lnt_tile_done[t] = lnt_last_evac
            xnb_war[s2] = lnt_last_evac
            if t % 4 == 3:
                emit_qkv_block(0, t // 4)
        xnt_ready = lnt_last_evac
        st["qin_war"] = [lnt_last_evac, lnt_last_evac]
        st["qpanel_war"] = [lnt_last_evac, lnt_last_evac]
        xnt_wr = None
        for q in range(4):
            xnt_wr = kb.op("gpsimd",
                           lambda e, q=q: e.dma_start(
                               out=dxnT[:, q * 8192:(q + 1) * 8192],
                               in_=XNT[:, q * 8192:(q + 1) * 8192]),
                           s_xntw, 16, waits=[xnt_ready])

        pt_war = [None, None, None, None]
        pending_epi = [None]
        sj = 0
        prow_rd = None
        pv_read = [None, None]
        exp_last_prev = None
        otd_wr_last = None
        xnt_attn_last_read = None
        xnb_attn_last = None
        sums_stop = None
        pv_stop = None



        for h in range(H):
            hs = h % 2
            if h == 3 and st["s_done"][0] is None:
                prow_rd = finalize_scale(0, prow_rd)
            if h == 5 and st["s_done"][1] is None:
                prow_rd = finalize_scale(1, prow_rd)
            if h < 5:
                emit_passA(28)

            if h + 1 < H:
                ns = (h + 1) % 2
                gq[ns] = kb.op("gpsimd", cast_load(wq_s(ns), (h + 1) * 128),
                               s_wq[ns], 16, waits=[wq_war[ns]])
                gk[ns] = kb.op("gpsimd", cast_load(wk_s(ns), DM + (h + 1) * 128),
                               s_wk[ns], 16, waits=[wk_war[ns]])
                gv[ns] = kb.op("gpsimd",
                               cast_load(wv_s(ns), 2 * DM + (h + 1) * 128),
                               s_wv[ns], 16, waits=[wv_war[ns]])

            if h > 0:
                emit_qkv_block(h, 0)
                if pending_epi[0] is not None:
                    pending_epi[0]()
                    pending_epi[0] = None
                for c in range(1, NCH):
                    emit_qkv_block(h, c)
            qk_evac_last = qk_evac[(h, "k", NCH - 1)]
            anch_rounds = [anch_round[(h, g)] for g in range(4)]
            if h == H - 1:
                xnt_attn_last_read = (pe, pe.n)

            am = kb.op("vector",
                       lambda e: e.tensor_reduce(ANCM[:], ANCH[:], axis=AX.X,
                                                 op=ALU.min),
                       ve, 1, waits=anch_rounds)
            tr = kb.op("tensor",
                       lambda e: e.transpose(PROW[0:1, 0:128], ANCM[:], IDENT[:]),
                       pe, 1, waits=[am, prow_rd])
            nm = kb.op("vector",
                       lambda e: e.tensor_reduce(NM1[:], PROW[0:1, 0:128],
                                                 axis=AX.X, op=ALU.min),
                       ve, 1, waits=[tr])
            prow_rd = nm
            nm2 = kb.op("vector",
                        lambda e: e.tensor_scalar(NM2[:], NM1[:], SCL, 1.0,
                                                  ALU.mult, ALU.mult),
                        ve, 1, waits=[nm])
            bc = kb.op("tensor",
                       lambda e: e.matmul(P7[:, 256:257], ONES1[:], NM2[:],
                                          start=True, stop=True),
                       pe, 1, waits=[nm2, st["p7f"]])
            negm_ready = kb.op("vector",
                               lambda e: e.tensor_copy(NEGM[:], P7[:, 256:257]),
                               ve, 1, waits=[bc, exp_last_prev])
            st["p7f"] = negm_ready

            exp_last = None
            pend = []

            def make_epi(c, pvbank, sums_stop, pv_stop, h=h):
                def epi():
                    nonlocal prow_rd, xnb_attn_last, otd_wr_last
                    al = kb.op("scalar",
                               lambda e: e.activation(SRL[:], PROW[:], AF.Ln),
                               ae, 1, waits=[sums_stop])
                    prow_rd = al
                    ar2 = kb.op("scalar",
                                lambda e: e.activation(SREC[:], SRL[:], AF.Exp,
                                                       scale=-1.0),
                                ae, 1, waits=[al])
                    bcm = kb.op("tensor",
                                lambda e: e.matmul(P7[:], ONES1[:], SREC[:],
                                                   start=True, stop=True),
                                pe, 1, waits=[ar2, st["p7f"]])
                    rcp = kb.op("vector",
                                lambda e: e.tensor_copy(RECIPB, P7[:]),
                                ve, 1, waits=[bcm])
                    st["p7f"] = rcp
                    om = kb.op("vector",
                               lambda e, c=c, pb=pvbank: e.tensor_tensor(
                                   otst_slot(c % 2), P[pb][:], RECIPB,
                                   ALU.mult),
                               ve, 1, waits=[pv_stop, rcp, otd_wr_last])
                    pv_read[c % 2] = om
                    xnb_attn_last = om
                    otd_wr_last = kb.op("gpsimd",
                                        lambda e, h=h, c=c: e.dma_start(
                                            out=dotd[:, h * SEQ + c * CH:
                                                     h * SEQ + (c + 1) * CH],
                                            in_=otst_slot(c % 2)),
                                        s_otd, 16, waits=[om])
                return epi

            def proc_one():
                pts, j, njt, c, pvbank, ex = pend.pop(0)
                if j == 0 and pending_epi[0] is not None:
                    pending_epi[0]()
                    pending_epi[0] = None
                kb.op("tensor",
                      lambda e, pts=pts, j=j, njt=njt: e.matmul(
                          PROW[:], ONESCB[:], pt_slot(pts),
                          start=(j == 0), stop=(j == njt - 1)),
                      pe if j == njt - 1 else None, 1 if j == njt - 1 else 0,
                      waits=([ex, prow_rd, init_memset] if j == 0 else [ex]))
                sums_stop = (pe, pe.n)
                pvw = ([ex, pv_read[c % 2]] if j == 0 else [ex])
                pv = kb.op("tensor",
                           lambda e, pts=pts, j=j, njt=njt, pb=pvbank: e.matmul(
                               P[pb][:], vh[:, j * 128:(j + 1) * 128],
                               pt_slot(pts),
                               start=(j == 0), stop=(j == njt - 1)),
                           pe, 1, waits=pvw)
                pt_war[pts] = pv
                if j == njt - 1:
                    pending_epi[0] = make_epi(c, pvbank, sums_stop, pv)

            for c in range(NCH):
                pvbank = 4 + (c % 2)
                njt = 4 * c + 4
                for j in range(njt):
                    sbank = sj % 4
                    pts = sj % 4
                    sj += 1
                    smm = kb.op("tensor",
                                lambda e, sbk=sbank, j=j, c=c: e.matmul(
                                    P[sbk][:], kh[:, j * 128:(j + 1) * 128],
                                    qh[:, c * CH:(c + 1) * CH],
                                    start=True, stop=True),
                                pe, 1, waits=[qk_evac_last, pA_read[sbank]])
                    exp_dep = smm
                    if j >= 4 * c:
                        a0 = 384 - 128 * (j - 4 * c)
                        exp_dep = kb.op("vector",
                                        lambda e, sbk=sbank, a0=a0: e.tensor_tensor(
                                            P[sbk][:], P[sbk][:],
                                            MASKS[:, a0:a0 + 512], ALU.add),
                                        ve, 1, waits=[smm, init_loads])
                    ex = kb.op("scalar",
                               lambda e, pts=pts, sbk=sbank: e.activation(
                                   pt_slot(pts), P[sbk][:], AF.Exp,
                                   bias=NEGM[:], scale=SCL),
                               ae, 1, waits=[exp_dep, negm_ready, pt_war[pts]])
                    pA_read[sbank] = ex
                    exp_last = ex
                    pend.append((pts, j, njt, c, pvbank, ex))
                    while len(pend) > 3:
                        proc_one()
                if h >= 6:
                    emit_passB(2, pA_read)
                elif st["Ai"] < len(units):
                    emit_passA(6)
            while pend:
                proc_one()
            exp_last_prev = exp_last

        if pending_epi[0] is not None:
            pending_epi[0]()
            pending_epi[0] = None
        attn_pe_done = (pe, pe.n)

        woutp_war = [None, None]
        otst2_war = [None, None, None, None]
        sst_war = [None, None]
        pB_read = [pA_read[0], pA_read[1]]
        sd_wr_last = None
        sd_last2 = [None, None]

        def emit_otl(k):
            tt = k % NT
            osl = k % 4
            return kb.op("sync",
                         lambda e, tt=tt, o=osl: e.dma_start(
                             out=otst2_slot(o).rearrange("p (g c) -> p g c", g=H),
                             in_=dotd.rearrange("p (g s) -> p g s", g=H)[
                                 :, :, tt * 128:(tt + 1) * 128]),
                         s_ot2[osl], 16,
                         waits=[otd_wr_last, otst2_war[osl], xnt_attn_last_read])

        def emit_woutld(dmc, extra=None):
            return kb.op(
                "gpsimd",
                lambda e, dmc=dmc: e.dma_start(
                    out=woutp_slot(dmc % 2).rearrange("p (t m) -> p t m", t=ND),
                    in_=dwout[:, dmc * CH:(dmc + 1) * CH].rearrange(
                        "(t p) m -> p t m", p=128)),
                s_wop[dmc % 2], 16, waits=[xnt_attn_last_read, extra])

        wout_ld = [None] * 4
        wout_ld[0] = emit_woutld(0)
        wout_ld[1] = emit_woutld(1)

        otl_q = [emit_otl(0), emit_otl(1), emit_otl(2)]
        for dmc in range(4):
            wsl = dmc % 2
            if dmc >= 2:
                wout_ld[dmc] = emit_woutld(dmc, extra=woutp_war[wsl])
            for tt in range(NT):
                k = dmc * NT + tt
                osl = k % 4
                otl = otl_q.pop(0)
                if k + 3 < 4 * NT:
                    otl_q.append(emit_otl(k + 3))
                bank = (dmc * NT + tt) % 2
                for hh in range(H):
                    w = ([otl, wout_ld[dmc], pB_read[bank], attn_pe_done,
                          st["qtrans_bank_read"][bank]]
                         if hh == 0 else [])
                    kb.op("tensor",
                          lambda e, b=bank, o=osl, hh=hh, ww=wsl: e.matmul(
                              P[b][:], otst2_slot(o)[:, hh * 128:(hh + 1) * 128],
                              woutp_slot(ww)[:, hh * CH:(hh + 1) * CH],
                              start=(hh == 0), stop=(hh == H - 1)),
                          pe if hh == H - 1 else None, 1 if hh == H - 1 else 0,
                          waits=w)
                smm = (pe, pe.n)
                otst2_war[osl] = smm
                woutp_war[wsl] = smm
                ssl = k % 2
                ev = kb.op("scalar",
                           lambda e, b=bank, o=ssl: e.copy(
                               sst_slot(o), P[b][:]),
                           ae, 1, waits=[smm, sst_war[ssl], xnb_attn_last,
                                         exp_last_prev])
                pB_read[bank] = ev
                sd_wr_last = kb.op("gpsimd",
                                   lambda e, tt=tt, dmc=dmc, o=ssl: e.dma_start(
                                       out=dsd[tt * 128:(tt + 1) * 128,
                                               dmc * CH:(dmc + 1) * CH],
                                       in_=sst_slot(o)),
                                   s_sd[ssl], 16, waits=[ev])
                sst_war[ssl] = sd_wr_last
                sd_last2[ssl] = sd_wr_last
                emit_passB(1, pB_read)
            emit_passB(12, pB_read)
        emit_passB(len(units), pB_read)
        assert st["Bi"] == len(units), f"passB incomplete {st['Bi']}"
        phaseB_pe_done = (pe, pe.n)

        b2b_ready = None
        for dmc in range(4):
            bm = kb.op("tensor",
                       lambda e, dmc=dmc: e.matmul(
                           P7[:], ONES1[:], B2ROW[:, dmc * CH:(dmc + 1) * CH],
                           start=True, stop=True),
                       pe, 1, waits=[st["p7f"], init_loads, attn_pe_done,
                                     b2b_ready])
            b2b_ready = kb.op("vector",
                              lambda e, dmc=dmc: e.tensor_copy(
                                  B2B[:, dmc * 512:(dmc + 1) * 512], P7[:]),
                              ve, 1, waits=[bm])
            st["p7f"] = b2b_ready

        pC_h_read = list(pB_read)
        pC_g_read = [None, None, None, None]
        w1p_war = [None, None]
        xnc_war = None
        w2m_war = [None, None, None, None]
        g_war = [None, None]
        st5_war = [None, None]
        xc_war = [None, None]
        zt_war = [None, None]
        ht_war = phaseB_pe_done
        y_last = None
        w1p_pend = {}

        def emit_xnc(c, first=False):
            return kb.op("sync",
                         lambda e, c=c: e.dma_start(
                             out=XNC.rearrange("p (d s) -> p d s", d=ND),
                             in_=dxnT.rearrange("p (d s) -> p d s", d=ND)[
                                 :, :, c * CH:(c + 1) * CH]),
                         s_xnc, 16,
                         waits=[xnc_war, xnt_wr,
                                st["qin_war"][0], st["qin_war"][1],
                                phaseB_pe_done if first else None])

        def emit_panel(c, fc):
            psl = fc % 2
            return kb.op("sync",
                         lambda e, fc=fc: e.dma_start(
                             out=w1p_slot(fc % 2).rearrange(
                                 "p (d f) -> p d f", d=ND),
                             in_=dw1t.rearrange("p (d f) -> p d f", d=ND)[
                                 :, :, fc * 512:(fc + 1) * 512]),
                         s_w1p[psl], 16,
                         waits=[st["w1t_done0"], st["w1t_done1"], w1p_war[psl],
                                st["qpanel_war"][0], st["qpanel_war"][1],
                                phaseB_pe_done if (c == 0 and fc < 2) else None])

        xnc_pend = emit_xnc(0, first=True)
        w1p_pend[(0, 0)] = emit_panel(0, 0)
        w1p_pend[(0, 1)] = emit_panel(0, 1)

        for c in range(NCH):
            xnc_all = xnc_pend
            last_gelu_of_f = {}
            for fc in range(16):
                psl = fc % 2
                pl_all = w1p_pend.pop((c, fc))
                for fl in range(4):
                    f = fc * 4 + fl
                    bank = f % 2
                    for dt in range(ND):
                        w = ([pl_all, xnc_all, pC_h_read[bank],
                              st["qtrans_bank_read"][bank]] if dt == 0 else [])
                        kb.op("tensor",
                              lambda e, b=bank, p=psl, dt=dt, fl=fl: e.matmul(
                                  P[b][:],
                                  w1p_slot(p)[:, dt * 512 + fl * 128:
                                              dt * 512 + (fl + 1) * 128],
                                  xnc_d(dt),
                                  start=(dt == 0), stop=(dt == ND - 1)),
                              pe if dt == ND - 1 else None,
                              1 if dt == ND - 1 else 0, waits=w)
                    hmm_done = (pe, pe.n)
                    w1p_war[psl] = hmm_done
                    gl = kb.op("scalar",
                               lambda e, b=bank, f=f: e.activation(
                                   XNT[:, f * 512:(f + 1) * 512], P[b][:],
                                   AF.Gelu, bias=B1C[:, f:f + 1]),
                               ae, 1, waits=[hmm_done, b1c_ready, ht_war])
                    pC_h_read[bank] = gl
                    last_gelu_of_f[f] = gl
                nfc, nc_ = (fc + 2, c) if fc + 2 < 16 else (fc + 2 - 16, c + 1)
                if nc_ < NCH:
                    w1p_pend[(nc_, nfc)] = emit_panel(nc_, nfc)
            xnc_war = (pe, pe.n)
            if DBG_SCRATCH:
                kb.op("gpsimd",
                      lambda e, c=c: e.dma_start(
                          out=dbg_ht[:, c * NF * CH:(c + 1) * NF * CH],
                          in_=XNT[:, 0:NF * CH]),
                      s_init, 16, waits=[last_gelu_of_f[NF - 1]])
            if c + 1 < NCH:
                xnc_pend = emit_xnc(c + 1)

            for dmc in range(4):
                for fp in range(NF // 2):
                    dsl = fp % 4
                    w2ld = kb.op("sync",
                                 lambda e, dmc=dmc, fp=fp, m=dsl: e.dma_start(
                                     out=w2m_dslot(m).rearrange(
                                         "p (b n) -> p b n", b=2),
                                     in_=dw2t.rearrange("p (ff n) -> p ff n",
                                                        ff=NF)[
                                         :, 2 * fp:2 * fp + 2,
                                         dmc * CH:(dmc + 1) * CH]),
                                 s_w2m[dsl], 16,
                                 waits=[st["w2t_done0"], st["w2t_done1"],
                                        w2m_war[dsl],
                                        sd_last2[0] if (c == 0 and dmc == 0)
                                        else None,
                                        sd_last2[1] if (c == 0 and dmc == 0)
                                        else None,
                                        xnb_attn_last if (c == 0 and dmc == 0)
                                        else None])
                    for fh in range(2):
                        f = 2 * fp + fh
                        for tt in range(4):
                            gbank = 2 + tt
                            w = [w2ld, last_gelu_of_f[f]]
                            if f == 0:
                                w.append(pC_g_read[tt])
                            kb.op("tensor",
                                  lambda e, gb=gbank, f=f, tt=tt, m=dsl, fh=fh:
                                  e.matmul(
                                      P[gb][:], XNT[:, f * 512 + tt * 128:
                                                    f * 512 + (tt + 1) * 128],
                                      w2m_dslot(m)[:, fh * 512:(fh + 1) * 512],
                                      start=(f == 0), stop=(f == NF - 1)),
                                  pe if (fh == 1 and tt == 3) else None,
                                  1 if (fh == 1 and tt == 3) else 0, waits=w)
                    w2m_war[dsl] = (pe, pe.n)
                g_stop = (pe, pe.n)
                gpar = dmc % 2
                for tt in range(4):
                    gbank = 2 + tt
                    zb = kb.op("vector",
                               lambda e, gb=gbank, dmc=dmc, tt=tt: e.tensor_tensor(
                                   zt_slot(tt % 2), P[gb][:],
                                   B2B[:, dmc * 512:(dmc + 1) * 512], ALU.add),
                               ve, 1, waits=[g_stop, b2b_ready, zt_war[tt % 2]])
                    sl = kb.op("scalar",
                               lambda e, gp=gpar, tt=tt: e.activation(
                                   g_slice(gp, tt), zt_slot(tt % 2), AF.Silu),
                               ae, 1, waits=[zb, g_war[gpar]])
                    zt_war[tt % 2] = sl
                    pC_g_read[tt] = sl
                if DBG_SCRATCH:
                    kb.op("gpsimd",
                          lambda e, c=c, dmc=dmc: e.dma_start(
                              out=dbg_g[:, (c * 4 + dmc) * 2048:
                                        (c * 4 + dmc + 1) * 2048],
                              in_=WQB[:, 6144 + (dmc % 2) * 2048:
                                      6144 + (dmc % 2 + 1) * 2048]),
                          s_init, 16, waits=[pC_g_read[0], pC_g_read[1],
                                           pC_g_read[2], pC_g_read[3]])
                for tt in range(4):
                    trow = c * 4 + tt
                    k2 = (dmc * 4 + tt) % 2
                    stld = kb.op("gpsimd",
                                 lambda e, trow=trow, dmc=dmc, s=k2: e.dma_start(
                                     out=st5_slot(s),
                                     in_=dsd[trow * 128:(trow + 1) * 128,
                                             dmc * CH:(dmc + 1) * CH]),
                                 s_st5[k2], 16,
                                 waits=[sd_last2[0], sd_last2[1], st5_war[k2],
                                        st["qout_war"][0], st["qout_war"][1]])
                    xld = kb.op("sync",
                                lambda e, trow=trow, dmc=dmc, s=k2: e.dma_start(
                                    out=xc_slot(s),
                                    in_=dx[trow * 128:(trow + 1) * 128,
                                           dmc * CH:(dmc + 1) * CH]),
                                s_xc[k2], 16,
                                waits=[xc_war[k2], st["qout_war"][0],
                                       st["qout_war"][1], st["prev_r"]])
                    sg = kb.op("vector",
                               lambda e, s=k2, gp=gpar, tt=tt: e.tensor_tensor(
                                   st5_slot(s), st5_slot(s), g_slice(gp, tt),
                                   ALU.mult),
                               ve, 1, waits=[stld, pC_g_read[tt]])
                    g_war[gpar] = sg
                    ad = kb.op("vector",
                               lambda e, s=k2: e.tensor_tensor(
                                   xc_slot(s), xc_slot(s), st5_slot(s), ALU.add),
                               ve, 1, waits=[sg, xld])
                    st5_war[k2] = ad
                    y_last = kb.op("sync",
                                   lambda e, trow=trow, dmc=dmc, s=k2: e.dma_start(
                                       out=dy[trow * 128:(trow + 1) * 128,
                                              dmc * CH:(dmc + 1) * CH],
                                       in_=xc_slot(s)),
                                   s_y[k2], 16, waits=[ad])
                    xc_war[k2] = y_last
                ht_war = g_stop

        kb.op("sync", lambda e: e.nop(), waits=[y_last])

        for s_ in kb.sems:
            s_.h = E(nc.semaphore(s_.name))
        block = E(nc.Block())

        @block.sync
        def _(e):
            kb.realize("sync", e)

        @block.scalar
        def _(e):
            kb.realize("scalar", e)

        @block.vector
        def _(e):
            kb.realize("vector", e)

        @block.tensor
        def _(e):
            kb.realize("tensor", e)

        @block.gpsimd
        def _(e):
            kb.realize("gpsimd", e)

    return nc


_NC_CACHE = None


def kernel(L0_attn_x, L0_attn_w_qkv, L0_attn_w_out, L0_ffn_w1, L0_ffn_b1,
           L0_ffn_w2, L0_ffn_b2):
    global _NC_CACHE
    x = np.ascontiguousarray(np.asarray(L0_attn_x, dtype=np.float32))
    wqkv = np.ascontiguousarray(np.asarray(L0_attn_w_qkv, dtype=np.float32))
    wout = np.ascontiguousarray(np.asarray(L0_attn_w_out, dtype=np.float32))
    w1 = np.ascontiguousarray(np.asarray(L0_ffn_w1, dtype=np.float32))
    b1 = np.ascontiguousarray(np.asarray(L0_ffn_b1, dtype=np.float32)).reshape(NF, 128)
    w2 = np.ascontiguousarray(np.asarray(L0_ffn_w2, dtype=np.float32))
    b2 = np.ascontiguousarray(np.asarray(L0_ffn_b2, dtype=np.float32)).reshape(1, DM)
    ident = np.eye(128, dtype=np.float32)
    identb = np.asarray(np.eye(128, dtype=np.float32).astype(ml_dtypes.bfloat16))
    ones1 = np.ones((1, 128), dtype=np.float32)
    masks = build_masks()

    if _NC_CACHE is None:
        _NC_CACHE = build_nc()
    nc = _NC_CACHE

    in_maps = []
    for i in range(N_CORES):
        in_maps.append({
            "x": np.ascontiguousarray(x[i]),
            "wqkv": wqkv, "wout": wout, "w1": w1, "b1": b1,
            "w2": w2, "b2": b2, "ident": ident, "identb": identb,
            "ones1": ones1, "masks": masks,
        })
    res = run_bass_kernel_spmd(nc, in_maps, core_ids=list(range(N_CORES)))
    out = np.stack([np.asarray(r["y"], dtype=np.float32) for r in res.results])
    return out


# version 08
# speedup vs baseline: 1.8254x; 1.1030x over previous
import sys
sys.path.insert(0, "/opt/trn_rl_repo")
from contextlib import ExitStack
import numpy as np
import ml_dtypes

import concourse.bass as bass
import concourse.mybir as mybir
from concourse.bass_utils import run_bass_kernel_spmd

F32 = mybir.dt.float32
BF16 = mybir.dt.bfloat16
AF = mybir.ActivationFunctionType
ALU = mybir.AluOpType
AX = mybir.AxisListType

SEQ = 2048
DM = 2048
H = 16
DFF = 8192
NT = 16
ND = 16
NF = 64
NCH = 4
CH = 512
SCL = float(1.0 / np.sqrt(128.0))
NEG = -1e10
N_CORES = 8

ENGS = ("sync", "scalar", "vector", "tensor", "gpsimd")


class Sem:
    __slots__ = ("name", "n", "h")

    def __init__(self, name):
        self.name = name
        self.n = 0
        self.h = None


class KB:
    def __init__(self):
        self.ops = {e: [] for e in ENGS}
        self.sems = []
        self.guar = {e: {} for e in ENGS}

    def sem(self, name):
        s = Sem(name)
        self.sems.append(s)
        return s

    def wait(self, eng, sm, val):
        if sm is None or val <= 0:
            return
        if self.guar[eng].get(sm, 0) >= val:
            return
        self.guar[eng][sm] = val
        self.ops[eng].append(("w", sm, val))

    def op(self, eng, fn, sm=None, inc=0, waits=()):
        for w in waits:
            if w is not None:
                self.wait(eng, w[0], w[1])
        if sm is not None:
            sm.n += inc
            self.ops[eng].append(("o", fn, sm, inc))
            return (sm, sm.n)
        self.ops[eng].append(("o", fn, None, 0))
        return None

    def realize(self, eng, h):
        for item in self.ops[eng]:
            if item[0] == "w":
                h.wait_ge(item[1].h, item[2])
            else:
                inst = item[1](h)
                if item[2] is not None:
                    inst.then_inc(item[2].h, item[3])


def build_masks():
    kl = np.arange(128)[:, None]
    x = np.arange(896)[None, :]
    m = np.where(x >= kl + 384, 0.0, NEG).astype(np.float32)
    return np.asarray(m.astype(ml_dtypes.bfloat16))


DBG_SCRATCH = False


def build_nc():
    nc = bass.Bass("TRN2", target_bir_lowering=False)

    dx = nc.dram_tensor("x", [SEQ, DM], F32, kind="ExternalInput").ap()
    dwqkv = nc.dram_tensor("wqkv", [DM, 3 * DM], F32, kind="ExternalInput").ap()
    dwout = nc.dram_tensor("wout", [DM, DM], F32, kind="ExternalInput").ap()
    dw1 = nc.dram_tensor("w1", [DFF, DM], F32, kind="ExternalInput").ap()
    db1 = nc.dram_tensor("b1", [NF, 128], F32, kind="ExternalInput").ap()
    dw2 = nc.dram_tensor("w2", [DM, DFF], F32, kind="ExternalInput").ap()
    db2 = nc.dram_tensor("b2", [1, DM], F32, kind="ExternalInput").ap()
    dident = nc.dram_tensor("ident", [128, 128], F32, kind="ExternalInput").ap()
    didentb = nc.dram_tensor("identb", [128, 128], BF16, kind="ExternalInput").ap()
    dones1 = nc.dram_tensor("ones1", [1, 128], F32, kind="ExternalInput").ap()
    dmasks = nc.dram_tensor("masks", [128, 896], BF16, kind="ExternalInput").ap()
    dy = nc.dram_tensor("y", [SEQ, DM], F32, kind="ExternalOutput").ap()

    _SK = "ExternalOutput" if DBG_SCRATCH else "Internal"
    dbg_ht = (nc.dram_tensor("dbg_ht", [128, 4 * NF * CH], BF16,
                             kind="ExternalOutput").ap() if DBG_SCRATCH else None)
    dbg_g = (nc.dram_tensor("dbg_g", [128, 4 * 4 * 2048], BF16,
                            kind="ExternalOutput").ap() if DBG_SCRATCH else None)
    dxnT = nc.dram_tensor("dxnT", [128, ND * SEQ], BF16, kind=_SK).ap()
    dotd = nc.dram_tensor("dotd", [128, H * SEQ], BF16, kind=_SK).ap()
    dsd = nc.dram_tensor("sd", [SEQ, DM], BF16, kind=_SK).ap()
    dw1t = nc.dram_tensor("w1t", [128, ND * DFF], BF16, kind=_SK).ap()
    dw2t = nc.dram_tensor("w2t", [128, NF * DM], BF16, kind=_SK).ap()

    kb = KB()
    pe = kb.sem("pe")
    ve = kb.sem("ve")
    ae = kb.sem("ae")
    ge = kb.sem("ge")
    s_init = kb.sem("s_init")
    s_xf = [kb.sem(f"s_xf{i}") for i in range(4)]
    s_qin = [kb.sem("s_qin0"), kb.sem("s_qin1")]
    s_w1t = [kb.sem("s_w1t0"), kb.sem("s_w1t1")]
    s_w2t = [kb.sem("s_w2t0"), kb.sem("s_w2t1")]
    s_xntw = kb.sem("s_xntw")
    s_otd = kb.sem("s_otd")
    s_sd = [kb.sem("s_sd0"), kb.sem("s_sd1")]
    s_wq = [kb.sem("s_wq0"), kb.sem("s_wq1")]
    s_wk = [kb.sem("s_wk0"), kb.sem("s_wk1")]
    s_wv = [kb.sem("s_wv0"), kb.sem("s_wv1")]
    s_wop = [kb.sem("s_wop0"), kb.sem("s_wop1")]
    s_ot2 = [kb.sem(f"s_ot2{i}") for i in range(4)]
    s_xnc = kb.sem("s_xnc")
    s_w1p = [kb.sem("s_w1p0"), kb.sem("s_w1p1")]
    s_w2m = [kb.sem(f"s_w2m{i}") for i in range(4)]
    s_st5 = [kb.sem(f"s_st5{i}") for i in range(4)]
    s_xc = [kb.sem(f"s_xc{i}") for i in range(4)]
    s_y = [kb.sem(f"s_y{i}") for i in range(4)]

    with ExitStack() as ctx:
        E = ctx.enter_context
        sb, ps = nc.sbuf_tensor, nc.psum_tensor

        XNT = E(sb("XNT", [128, ND * SEQ], BF16))
        BIGB = E(sb("BIGB", [128, 24576], BF16))
        QKVB = E(sb("QKVB", [128, 3 * SEQ], BF16))
        WQB = E(sb("WQB", [128, 2 * 3 * DM], BF16))
        XNB = E(sb("XNB", [128, 4096], BF16))
        WP = E(sb("WP", [128, 8192], BF16))
        MASKS = E(sb("MASKS", [128, 896], BF16))
        IDENT = E(sb("IDENT", [128, 128], F32))
        IDENTB = E(sb("IDENTB", [128, 128], BF16))
        ONES1 = E(sb("ONES1", [1, 128], F32))
        ONESCB = E(sb("ONESCB", [128, 1], BF16))
        ONESCF = E(sb("ONESCF", [128, 1], F32))
        EPSC = E(sb("EPSC", [128, 1], F32))
        B1RS = E(sb("B1RS", [64, 128], F32))
        B1C = E(sb("B1C", [128, 64], F32))
        B2ROW = E(sb("B2ROW", [1, DM], F32))
        LNS = E(sb("LNS", [128, 24], F32))
        WACC = E(sb("WACC", [128, 128], F32))
        ANCH = E(sb("ANCH", [128, 4], F32))
        ANCM = E(sb("ANCM", [128, 1], F32))
        NEGM = E(sb("NEGM", [128, 1], F32))
        NM1 = E(sb("NM1", [1, 1], F32))
        NM2 = E(sb("NM2", [1, 1], F32))
        SRL = E(sb("SRL", [1, CH], F32))
        SREC = E(sb("SREC", [1, CH], F32))
        SCLB = [E(sb("SCL1", [1, 1], F32)), E(sb("SCL2", [1, 1], F32))]
        TCOL = [E(sb("TC1", [128, 1], F32)), E(sb("TC2", [128, 1], F32))]
        NTCOL = [E(sb("NTC1", [128, 1], F32)), E(sb("NTC2", [128, 1], F32))]
        WSUM = [E(sb("WS1", [128, 1], F32)), E(sb("WS2", [128, 1], F32))]

        P = [E(ps(f"P{i}", [128, 512], F32)) for i in range(6)]
        PROW = E(ps("PROW", [1, 512], F32))
        P7 = E(ps("P7", [128, 512], F32))

        def xf_slot(i):
            return BIGB[:, i * 4096:(i + 1) * 4096].bitcast(F32)

        def xnb_slot(i):
            return BIGB[:, 16384 + i * 2048:16384 + (i + 1) * 2048]

        LNDUMP = BIGB[:, 20480:24576].bitcast(F32)

        def qpanel_slot(i):
            return BIGB[:, i * 8192:(i + 1) * 8192]

        def qin_slot(i):
            return BIGB[:, 16384 + i * 4096:16384 + (i + 1) * 4096].bitcast(F32)

        def w1p_slot(i):
            return BIGB[:, i * 8192:(i + 1) * 8192]

        XNC = BIGB[:, 16384:24576]

        def xnc_d(dt):
            return BIGB[:, 16384 + dt * 512:16384 + (dt + 1) * 512]

        def woutp_slot(i):
            return XNT[:, i * 8192:(i + 1) * 8192]

        def otst2_slot(i):
            return XNT[:, 16384 + i * 2048:16384 + (i + 1) * 2048]

        qh = QKVB[:, 0:SEQ]
        kh = QKVB[:, SEQ:2 * SEQ]
        vh = QKVB[:, 2 * SEQ:3 * SEQ]

        def wq_s(i):
            return WQB[:, i * 6144 + 0:i * 6144 + DM]

        def wk_s(i):
            return WQB[:, i * 6144 + DM:i * 6144 + 2 * DM]

        def wv_s(i):
            return WQB[:, i * 6144 + 2 * DM:i * 6144 + 3 * DM]

        B2B = WQB[:, 0:4096].bitcast(F32)

        def zt_slot(i):
            return WQB[:, 4096 + i * 1024:4096 + (i + 1) * 1024].bitcast(F32)

        def g_slice(par, tt):
            return WQB[:, 6144 + par * 2048 + tt * 512:
                       6144 + par * 2048 + (tt + 1) * 512]

        _PT_OFF = [0, 512, 3072, 3584]

        def pt_slot(i):
            return XNB[:, _PT_OFF[i]:_PT_OFF[i] + 512]

        RECIPB = XNB[:, 1024:2048].bitcast(F32)

        def otst_slot(i):
            return XNB[:, 2048 + i * 512:2048 + (i + 1) * 512]

        def sst_slot(i):
            return XNB[:, i * 512:(i + 1) * 512]

        def w2m_dslot(i):
            return XNB[:, i * 1024:(i + 1) * 1024]

        def qout_slot(i):
            return WP[:, i * 2048:(i + 1) * 2048]

        QTMP = WP[:, 4096:8192].bitcast(F32)

        def xc_slot(i):
            return WP[:, i * 1024:(i + 1) * 1024].bitcast(F32)

        def st5_slot(i):
            return WP[:, 4096 + i * 512:4096 + (i + 1) * 512]

        lnsum = [LNS[:, i:i + 1] for i in range(0, 4)]
        lnmean = [LNS[:, i:i + 1] for i in range(4, 8)]
        lnvs = [LNS[:, i:i + 1] for i in range(8, 12)]
        lnln = [LNS[:, i:i + 1] for i in range(12, 16)]
        lnrstd = [LNS[:, i:i + 1] for i in range(16, 20)]

        for dst, src in ((IDENT[:], dident), (IDENTB[:], didentb),
                         (ONES1[:], dones1), (MASKS[:], dmasks),
                         (B1RS[:], db1), (B2ROW[:], db2)):
            kb.op("sync", lambda e, d=dst, s=src: e.dma_start(out=d, in_=s),
                  s_init, 16)
        init_loads = (s_init, s_init.n)

        kb.op("gpsimd", lambda e: e.memset(ONESCB[:], 1.0), ge, 1)
        kb.op("gpsimd", lambda e: e.memset(ONESCF[:], 1.0), ge, 1)
        kb.op("gpsimd", lambda e: e.memset(EPSC[:], 1e-5), ge, 1)
        init_memset = (ge, ge.n)

        b1t = kb.op("tensor", lambda e: e.transpose(P7[:, 0:64], B1RS[:],
                                                    IDENT[0:64, 0:64]),
                    pe, 1, waits=[init_loads])
        b1c_ready = kb.op("vector", lambda e: e.tensor_copy(B1C[:], P7[:, 0:64]),
                          ve, 1, waits=[b1t])

        units = [(0, dh, ch) for dh in range(ND) for ch in range(4)] + \
                [(1, fh, 0) for fh in range(NF)]

        def unit_src(mat, pnl, ch):
            if mat == 0:
                return dw1[ch * 2048:(ch + 1) * 2048,
                           pnl * 128:(pnl + 1) * 128].rearrange(
                    "(t p) d -> p t d", p=128)
            return dw2[:, pnl * 128:(pnl + 1) * 128].rearrange(
                "(t p) d -> p t d", p=128)

        st = {
            "Ai": 0, "Bi": 0,
            "qin_war": [None, None],
            "qout_war": [None, None],
            "qpanel_war": [None, None],
            "w1A_done": None, "w2A_done": None,
            "s_done": [None, None],
            "prev_r": None,
            "w1t_done0": None, "w1t_done1": None,
            "w2t_done0": None, "w2t_done1": None,
            "qtrans_bank_read": [None, None],
            "p7f": None,
        }
        st["p7f"] = b1c_ready

        def emit_passA(nmax):
            for _ in range(nmax):
                if st["Ai"] >= len(units):
                    return
                mat, pnl, ch = units[st["Ai"]]
                slot = st["Ai"] % 2
                col = st["Ai"]
                v = kb.op("sync",
                          lambda e, m=mat, p=pnl, c=ch, sl=slot: e.dma_start(
                              out=qin_slot(sl).rearrange("p (t d) -> p t d", t=16),
                              in_=unit_src(m, p, c)),
                          s_qin[slot], 16, waits=[st["qin_war"][slot]])
                r = kb.op("vector",
                          lambda e, sl=slot, c=col: e.reduce_sum(
                              WACC[:, c:c + 1], qin_slot(sl), axis=AX.X,
                              apply_absolute_value=True),
                          ve, 1, waits=[v])
                st["qin_war"][slot] = r
                st["Ai"] += 1
                if st["Ai"] == 64:
                    st["w1A_done"] = r
                if st["Ai"] == 128:
                    st["w2A_done"] = r

        def finalize_scale(mat, prow_rd):
            a_done = st["w1A_done"] if mat == 0 else st["w2A_done"]
            assert a_done is not None
            v1 = kb.op("vector",
                       lambda e, m=mat: e.tensor_reduce(
                           WSUM[m][:], WACC[:, m * 64:(m + 1) * 64],
                           axis=AX.X, op=ALU.add),
                       ve, 1, waits=[a_done])
            t1 = kb.op("tensor",
                       lambda e, m=mat: e.matmul(PROW[0:1, 0:1], ONESCF[:],
                                                 WSUM[m][:], start=True, stop=True),
                       pe, 1, waits=[v1, init_memset, prow_rd])
            a1 = kb.op("scalar",
                       lambda e, m=mat: e.activation(SCLB[m][:], PROW[0:1, 0:1],
                                                     AF.Copy,
                                                     scale=0.7 / float(DFF * DM)),
                       ae, 1, waits=[t1])
            t2 = kb.op("tensor",
                       lambda e, m=mat: e.matmul(P7[:, 0:1], ONES1[:], SCLB[m][:],
                                                 start=True, stop=True),
                       pe, 1, waits=[a1, st["p7f"], init_loads])
            v2 = kb.op("vector",
                       lambda e, m=mat: e.tensor_copy(TCOL[m][:], P7[:, 0:1]),
                       ve, 1, waits=[t2])
            v3 = kb.op("vector",
                       lambda e, m=mat: e.tensor_scalar(
                           NTCOL[m][:], TCOL[m][:], -1.0, 1.0, ALU.mult, ALU.mult),
                       ve, 1, waits=[v2])
            st["s_done"][mat] = v3
            st["p7f"] = v2
            return a1

        def emit_passB(nmax, pA_read):
            for _ in range(nmax):
                if st["Bi"] >= len(units):
                    return
                mat, pnl, ch = units[st["Bi"]]
                if st["s_done"][mat] is None:
                    return
                ui = st["Bi"]
                islot = ui % 2
                oslot = ui % 2
                pslot = pnl % 2
                v = kb.op("sync",
                          lambda e, m=mat, p=pnl, c=ch, sl=islot: e.dma_start(
                              out=qin_slot(sl).rearrange("p (t d) -> p t d", t=16),
                              in_=unit_src(m, p, c)),
                          s_qin[islot], 16, waits=[st["qin_war"][islot]])
                r1 = kb.op("vector",
                           lambda e, sl=islot, m=mat: e.tensor_scalar(
                               QTMP, qin_slot(sl), TCOL[m][:], -1.0,
                               ALU.is_ge, ALU.add),
                           ve, 1, waits=[v, st["s_done"][mat], st["prev_r"]])
                r2 = kb.op("vector",
                           lambda e, sl=islot, o=oslot, m=mat:
                           e.scalar_tensor_tensor(
                               qout_slot(o), qin_slot(sl), NTCOL[m][:], QTMP,
                               ALU.is_gt, ALU.add),
                           ve, 1, waits=[r1, st["qout_war"][oslot]])
                st["prev_r"] = r2
                st["qin_war"][islot] = r2
                evacs = []
                pview = P7.bitcast(BF16)
                for rr in range(2):
                    for j in range(8):
                        w = ([r2, st["p7f"]] if j == 0 else [])
                        kb.op("tensor",
                              lambda e, pv=pview, o=oslot, rr=rr, j=j: e.transpose(
                                  pv[:, j * 128:(j + 1) * 128],
                                  qout_slot(o)[:, (rr * 8 + j) * 128:
                                               (rr * 8 + j + 1) * 128],
                                  IDENTB[:]),
                              pe if j == 7 else None, 1 if j == 7 else 0,
                              waits=w)
                    tdone = (pe, pe.n)
                    base = (ch * 2048 + rr * 1024) if mat == 0 else (rr * 1024)
                    evc = kb.op("scalar",
                                lambda e, pv=pview, psl=pslot, b=base: e.copy(
                                    qpanel_slot(psl)[:, b:b + 1024], pv[:]),
                                ae, 1, waits=[tdone, st["qpanel_war"][pslot]])
                    st["p7f"] = evc
                    evacs.append(evc)
                if mat == 0 and ch == 3:
                    wr = kb.op("gpsimd",
                               lambda e, p=pnl, psl=pslot: e.dma_start(
                                   out=dw1t[:, p * 8192:(p + 1) * 8192],
                                   in_=qpanel_slot(psl)),
                               s_w1t[pslot], 16, waits=evacs)
                    st["qpanel_war"][pslot] = wr
                    st["w1t_done" + str(pslot)] = wr
                elif mat == 1:
                    wr = kb.op("gpsimd",
                               lambda e, p=pnl, psl=pslot: e.dma_start(
                                   out=dw2t[:, p * 2048:(p + 1) * 2048],
                                   in_=qpanel_slot(psl)[:, 0:2048]),
                               s_w2t[pslot], 16, waits=evacs)
                    st["qpanel_war"][pslot] = wr
                    st["w2t_done" + str(pslot)] = wr
                st["qout_war"][oslot] = (pe, pe.n)
                st["Bi"] += 1

        xf_war = [None, None, None, None]
        lnvs_war = [None, None, None, None]
        xnb_war = [None, None]
        lnt_tile_done = [None] * NT
        lnt_last_evac = None
        pA_read = [None] * 6
        qk_evac = {}
        vh_evac = {}
        anch_round = {}
        wq_war = [None, None]
        wk_war = [None, None]
        wv_war = [None, None]

        def cast_load(dst, c0):
            return lambda e, d=dst, c=c0: e.dma_start(
                out=d.rearrange("p (t f) -> p t f", t=ND),
                in_=dwqkv[:, c:c + 128].rearrange("(t p) f -> p t f", p=128))

        gq = [None, None]
        gk = [None, None]
        gv = [None, None]
        gq[0] = kb.op("gpsimd", cast_load(wq_s(0), 0), s_wq[0], 16)
        gk[0] = kb.op("gpsimd", cast_load(wk_s(0), DM), s_wk[0], 16)
        gv[0] = kb.op("gpsimd", cast_load(wv_s(0), 2 * DM), s_wv[0], 16)

        def emit_qkv_block(h, c):
            hs = h % 2
            for bank, wsl, dst, gld, tag in (
                    (2, wq_s(hs), qh, gq[hs], "q"), (3, wk_s(hs), kh, gk[hs], "k")):
                for dt in range(ND):
                    w = ([gld, lnt_tile_done[4 * c + 3], pA_read[bank]]
                         if dt == 0 else [])
                    kb.op("tensor",
                          lambda e, b=bank, ww=wsl, dt=dt, c=c: e.matmul(
                              P[b][:], ww[:, dt * 128:(dt + 1) * 128],
                              XNT[:, dt * SEQ + c * CH:dt * SEQ + (c + 1) * CH],
                              start=(dt == 0), stop=(dt == ND - 1)),
                          pe if dt == ND - 1 else None,
                          1 if dt == ND - 1 else 0, waits=w)
                mm_done = (pe, pe.n)
                if tag == "q":
                    wq_war[hs] = mm_done
                else:
                    wk_war[hs] = mm_done
                ev = kb.op("scalar",
                           lambda e, b=bank, d=dst, c=c: e.copy(
                               d[:, c * CH:(c + 1) * CH], P[b][:]),
                           ae, 1, waits=[mm_done])
                pA_read[bank] = ev
                qk_evac[(h, tag, c)] = ev
            g = c
            bank = c % 2
            for ttl in range(4):
                tok = 4 * g + ttl
                for dt in range(ND):
                    w = ([gv[hs], lnt_tile_done[4 * g + 3], pA_read[bank]]
                         if (dt == 0 and ttl == 0) else [])
                    kb.op("tensor",
                          lambda e, b=bank, tok=tok, dt=dt, ttl=ttl, hs=hs:
                          e.matmul(
                              P[b][:, ttl * 128:(ttl + 1) * 128],
                              XNT[:, dt * SEQ + tok * 128:
                                  dt * SEQ + (tok + 1) * 128],
                              wv_s(hs)[:, dt * 128:(dt + 1) * 128],
                              start=(dt == 0), stop=(dt == ND - 1)),
                          pe if (dt == ND - 1 and ttl == 3) else None,
                          1 if (dt == ND - 1 and ttl == 3) else 0, waits=w)
            vg_done = (pe, pe.n)
            wv_war[hs] = vg_done
            ev = kb.op("scalar",
                       lambda e, b=bank, g=g: e.copy(
                           vh[:, g * CH:(g + 1) * CH], P[b][:]),
                       ae, 1, waits=[vg_done])
            pA_read[bank] = ev
            vh_evac[(h, g)] = ev
            for rr in range(4):
                qt = 4 * g + rr
                w = ([qk_evac[(h, "q", c)], qk_evac[(h, "k", c)], st["p7f"]]
                     if rr == 0 else [])
                kb.op("tensor",
                      lambda e, qt=qt, rr=rr: e.matmul(
                          P7[:, rr * 128:(rr + 1) * 128],
                          qh[:, qt * 128:(qt + 1) * 128],
                          kh[:, qt * 128:(qt + 1) * 128],
                          start=True, stop=True),
                      pe if rr == 3 else None, 1 if rr == 3 else 0, waits=w)
            diag_done = (pe, pe.n)
            ar = kb.op("vector",
                       lambda e, g=g: e.reduce_max(ANCH[:, g:g + 1], P7[:],
                                                   axis=AX.X, negate=True),
                       ve, 1, waits=[diag_done])
            anch_round[(h, g)] = ar
            st["p7f"] = ar
        for t in range(NT):
            s = t % 4
            s2 = t % 2
            xd = kb.op("sync",
                       lambda e, t=t, s=s: e.dma_start(
                           out=xf_slot(s), in_=dx[t * 128:(t + 1) * 128, :]),
                       s_xf[s], 16, waits=[xf_war[s]])
            v1 = kb.op("vector",
                       lambda e, s=s: e.reduce_sum(lnsum[s], xf_slot(s), axis=AX.X),
                       ve, 1, waits=[xd])
            v2 = kb.op("vector",
                       lambda e, s=s: e.tensor_scalar(
                           lnmean[s], lnsum[s], 1.0 / DM, 1.0, ALU.mult, ALU.mult),
                       ve, 1, waits=[v1])
            v3 = kb.op("vector",
                       lambda e, s=s: e.scalar_tensor_tensor(
                           LNDUMP, xf_slot(s), lnmean[s], xf_slot(s),
                           ALU.subtract, ALU.mult, accum_out=lnvs[s]),
                       ve, 1, waits=[v2, lnvs_war[s]])
            a1 = kb.op("scalar",
                       lambda e, s=s: e.activation(lnln[s], lnvs[s], AF.Ln,
                                                   bias=EPSC[:], scale=1.0 / DM),
                       ae, 1, waits=[v3, init_memset])
            lnvs_war[s] = a1
            a2 = kb.op("scalar",
                       lambda e, s=s: e.activation(lnrstd[s], lnln[s], AF.Exp,
                                                   scale=-0.5),
                       ae, 1, waits=[a1])
            v4 = kb.op("vector",
                       lambda e, s=s, s2=s2: e.tensor_scalar(
                           xnb_slot(s2), xf_slot(s), lnmean[s], lnrstd[s],
                           ALU.subtract, ALU.mult),
                       ve, 1, waits=[a2, xnb_war[s2]])
            xf_war[s] = v4
            for rr in range(2):
                bank = rr
                pview = P[bank].bitcast(BF16)
                for j in range(8):
                    w = ([v4, pA_read[bank]] if j == 0 else [])
                    kb.op("tensor",
                          lambda e, pv=pview, s2=s2, rr=rr, j=j: e.transpose(
                              pv[:, j * 128:(j + 1) * 128],
                              xnb_slot(s2)[:, (rr * 8 + j) * 128:
                                           (rr * 8 + j + 1) * 128],
                              IDENTB[:]),
                          pe if j == 7 else None, 1 if j == 7 else 0, waits=w)
                tdone = (pe, pe.n)
                ev = kb.op("scalar",
                           lambda e, pv=pview, t=t, rr=rr: e.copy(
                               XNT[:, 0:ND * SEQ].rearrange(
                                   "p (d s) -> p d s", d=ND)[
                                   :, rr * 8:(rr + 1) * 8,
                                   t * 128:(t + 1) * 128],
                               pv[:].rearrange("p (d s) -> p d s", d=8)),
                           ae, 1, waits=[tdone])
                pA_read[rr] = ev
                lnt_last_evac = ev
            lnt_tile_done[t] = lnt_last_evac
            xnb_war[s2] = lnt_last_evac
            if t % 4 == 3:
                emit_qkv_block(0, t // 4)
        xnt_ready = lnt_last_evac
        st["qin_war"] = [lnt_last_evac, lnt_last_evac]
        st["qpanel_war"] = [lnt_last_evac, lnt_last_evac]
        xnt_wr = None
        for q in range(4):
            xnt_wr = kb.op("gpsimd",
                           lambda e, q=q: e.dma_start(
                               out=dxnT[:, q * 8192:(q + 1) * 8192],
                               in_=XNT[:, q * 8192:(q + 1) * 8192]),
                           s_xntw, 16, waits=[xnt_ready])

        pt_war = [None, None, None, None]
        pending_epi = [None]
        sj = 0
        prow_rd = None
        pv_read = [None, None]
        exp_last_prev = None
        otd_wr_last = None
        xnt_attn_last_read = None
        xnb_attn_last = None
        sums_stop = None
        pv_stop = None



        for h in range(H):
            hs = h % 2
            if h == 4 and st["s_done"][0] is None:
                prow_rd = finalize_scale(0, prow_rd)
            if h == 8 and st["s_done"][1] is None:
                prow_rd = finalize_scale(1, prow_rd)

            if h + 1 < H:
                ns = (h + 1) % 2
                gq[ns] = kb.op("gpsimd", cast_load(wq_s(ns), (h + 1) * 128),
                               s_wq[ns], 16, waits=[wq_war[ns]])
                gk[ns] = kb.op("gpsimd", cast_load(wk_s(ns), DM + (h + 1) * 128),
                               s_wk[ns], 16, waits=[wk_war[ns]])
                gv[ns] = kb.op("gpsimd",
                               cast_load(wv_s(ns), 2 * DM + (h + 1) * 128),
                               s_wv[ns], 16, waits=[wv_war[ns]])

            if h > 0:
                emit_qkv_block(h, 0)
                emit_qkv_block(h, 1)
                if pending_epi[0] is not None:
                    pending_epi[0]()
                    pending_epi[0] = None
                for c in range(2, NCH):
                    emit_qkv_block(h, c)
            qk_evac_last = qk_evac[(h, "k", NCH - 1)]
            anch_rounds = [anch_round[(h, g)] for g in range(4)]
            if h == H - 1:
                xnt_attn_last_read = (pe, pe.n)

            am = kb.op("vector",
                       lambda e: e.tensor_reduce(ANCM[:], ANCH[:], axis=AX.X,
                                                 op=ALU.min),
                       ve, 1, waits=anch_rounds)
            tr = kb.op("tensor",
                       lambda e: e.transpose(PROW[0:1, 0:128], ANCM[:], IDENT[:]),
                       pe, 1, waits=[am, prow_rd])
            nm = kb.op("vector",
                       lambda e: e.tensor_reduce(NM1[:], PROW[0:1, 0:128],
                                                 axis=AX.X, op=ALU.min),
                       ve, 1, waits=[tr])
            prow_rd = nm
            nm2 = kb.op("vector",
                        lambda e: e.tensor_scalar(NM2[:], NM1[:], SCL, 1.0,
                                                  ALU.mult, ALU.mult),
                        ve, 1, waits=[nm])
            bc = kb.op("tensor",
                       lambda e: e.matmul(P7[:, 256:257], ONES1[:], NM2[:],
                                          start=True, stop=True),
                       pe, 1, waits=[nm2, st["p7f"]])
            negm_ready = kb.op("vector",
                               lambda e: e.tensor_copy(NEGM[:], P7[:, 256:257]),
                               ve, 1, waits=[bc, exp_last_prev])
            st["p7f"] = negm_ready

            exp_last = None
            pend = []

            def make_epi(c, pvbank, sums_stop, pv_stop, h=h):
                def epi():
                    nonlocal prow_rd, xnb_attn_last, otd_wr_last
                    al = kb.op("scalar",
                               lambda e: e.activation(SRL[:], PROW[:], AF.Ln),
                               ae, 1, waits=[sums_stop])
                    prow_rd = al
                    ar2 = kb.op("scalar",
                                lambda e: e.activation(SREC[:], SRL[:], AF.Exp,
                                                       scale=-1.0),
                                ae, 1, waits=[al])
                    bcm = kb.op("tensor",
                                lambda e: e.matmul(P7[:], ONES1[:], SREC[:],
                                                   start=True, stop=True),
                                pe, 1, waits=[ar2, st["p7f"]])
                    rcp = kb.op("vector",
                                lambda e: e.tensor_copy(RECIPB, P7[:]),
                                ve, 1, waits=[bcm])
                    st["p7f"] = rcp
                    om = kb.op("vector",
                               lambda e, c=c, pb=pvbank: e.tensor_tensor(
                                   otst_slot(c % 2), P[pb][:], RECIPB,
                                   ALU.mult),
                               ve, 1, waits=[pv_stop, rcp, otd_wr_last])
                    pv_read[c % 2] = om
                    xnb_attn_last = om
                    otd_wr_last = kb.op("gpsimd",
                                        lambda e, h=h, c=c: e.dma_start(
                                            out=dotd[:, h * SEQ + c * CH:
                                                     h * SEQ + (c + 1) * CH],
                                            in_=otst_slot(c % 2)),
                                        s_otd, 16, waits=[om])
                return epi

            def proc_one():
                pts, j, njt, c, pvbank, ex = pend.pop(0)
                if j == 0 and pending_epi[0] is not None:
                    pending_epi[0]()
                    pending_epi[0] = None
                kb.op("tensor",
                      lambda e, pts=pts, j=j, njt=njt: e.matmul(
                          PROW[:], ONESCB[:], pt_slot(pts),
                          start=(j == 0), stop=(j == njt - 1)),
                      pe if j == njt - 1 else None, 1 if j == njt - 1 else 0,
                      waits=([ex, prow_rd, init_memset] if j == 0 else [ex]))
                sums_stop = (pe, pe.n)
                pvw = ([ex, pv_read[c % 2]] if j == 0 else [ex])
                pv = kb.op("tensor",
                           lambda e, pts=pts, j=j, njt=njt, pb=pvbank: e.matmul(
                               P[pb][:], vh[:, j * 128:(j + 1) * 128],
                               pt_slot(pts),
                               start=(j == 0), stop=(j == njt - 1)),
                           pe, 1, waits=pvw)
                pt_war[pts] = pv
                if j == njt - 1:
                    pending_epi[0] = make_epi(c, pvbank, sums_stop, pv)

            for c in range(NCH):
                pvbank = 4 + (c % 2)
                njt = 4 * c + 4
                for j in range(njt):
                    sbank = sj % 4
                    pts = sj % 4
                    sj += 1
                    smm = kb.op("tensor",
                                lambda e, sbk=sbank, j=j, c=c: e.matmul(
                                    P[sbk][:], kh[:, j * 128:(j + 1) * 128],
                                    qh[:, c * CH:(c + 1) * CH],
                                    start=True, stop=True),
                                pe, 1, waits=[qk_evac_last, pA_read[sbank]])
                    exp_dep = smm
                    if j >= 4 * c:
                        a0 = 384 - 128 * (j - 4 * c)
                        exp_dep = kb.op("vector",
                                        lambda e, sbk=sbank, a0=a0: e.tensor_tensor(
                                            P[sbk][:], P[sbk][:],
                                            MASKS[:, a0:a0 + 512], ALU.add),
                                        ve, 1, waits=[smm, init_loads])
                    ex = kb.op("scalar",
                               lambda e, pts=pts, sbk=sbank: e.activation(
                                   pt_slot(pts), P[sbk][:], AF.Exp,
                                   bias=NEGM[:], scale=SCL),
                               ae, 1, waits=[exp_dep, negm_ready, pt_war[pts]])
                    pA_read[sbank] = ex
                    exp_last = ex
                    pend.append((pts, j, njt, c, pvbank, ex))
                    while len(pend) > 3:
                        proc_one()
                if h >= 8:
                    emit_passB(2, pA_read)
                else:
                    emit_passA(4)
            while pend:
                proc_one()
            exp_last_prev = exp_last

        if pending_epi[0] is not None:
            pending_epi[0]()
            pending_epi[0] = None
        attn_pe_done = (pe, pe.n)

        woutp_war = [None, None]
        otst2_war = [None, None, None, None]
        sst_war = [None, None]
        pB_read = [pA_read[0], pA_read[1]]
        sd_wr_last = None
        sd_last2 = [None, None]

        def emit_otl(k):
            tt = k % NT
            osl = k % 4
            return kb.op("sync",
                         lambda e, tt=tt, o=osl: e.dma_start(
                             out=otst2_slot(o).rearrange("p (g c) -> p g c", g=H),
                             in_=dotd.rearrange("p (g s) -> p g s", g=H)[
                                 :, :, tt * 128:(tt + 1) * 128]),
                         s_ot2[osl], 16,
                         waits=[otd_wr_last, otst2_war[osl], xnt_attn_last_read])

        def emit_woutld(dmc, extra=None):
            return kb.op(
                "gpsimd",
                lambda e, dmc=dmc: e.dma_start(
                    out=woutp_slot(dmc % 2).rearrange("p (t m) -> p t m", t=ND),
                    in_=dwout[:, dmc * CH:(dmc + 1) * CH].rearrange(
                        "(t p) m -> p t m", p=128)),
                s_wop[dmc % 2], 16, waits=[xnt_attn_last_read, extra])

        wout_ld = [None] * 4
        wout_ld[0] = emit_woutld(0)
        wout_ld[1] = emit_woutld(1)

        otl_q = [emit_otl(0), emit_otl(1), emit_otl(2)]
        for dmc in range(4):
            wsl = dmc % 2
            if dmc >= 2:
                wout_ld[dmc] = emit_woutld(dmc, extra=woutp_war[wsl])
            for tt in range(NT):
                k = dmc * NT + tt
                osl = k % 4
                otl = otl_q.pop(0)
                if k + 3 < 4 * NT:
                    otl_q.append(emit_otl(k + 3))
                bank = (dmc * NT + tt) % 2
                for hh in range(H):
                    w = ([otl, wout_ld[dmc], pB_read[bank], attn_pe_done,
                          st["qtrans_bank_read"][bank]]
                         if hh == 0 else [])
                    kb.op("tensor",
                          lambda e, b=bank, o=osl, hh=hh, ww=wsl: e.matmul(
                              P[b][:], otst2_slot(o)[:, hh * 128:(hh + 1) * 128],
                              woutp_slot(ww)[:, hh * CH:(hh + 1) * CH],
                              start=(hh == 0), stop=(hh == H - 1)),
                          pe if hh == H - 1 else None, 1 if hh == H - 1 else 0,
                          waits=w)
                smm = (pe, pe.n)
                otst2_war[osl] = smm
                woutp_war[wsl] = smm
                ssl = k % 2
                ev = kb.op("scalar",
                           lambda e, b=bank, o=ssl: e.copy(
                               sst_slot(o), P[b][:]),
                           ae, 1, waits=[smm, sst_war[ssl], xnb_attn_last,
                                         exp_last_prev])
                pB_read[bank] = ev
                sd_wr_last = kb.op("gpsimd",
                                   lambda e, tt=tt, dmc=dmc, o=ssl: e.dma_start(
                                       out=dsd[tt * 128:(tt + 1) * 128,
                                               dmc * CH:(dmc + 1) * CH],
                                       in_=sst_slot(o)),
                                   s_sd[ssl], 16, waits=[ev])
                sst_war[ssl] = sd_wr_last
                sd_last2[ssl] = sd_wr_last
                emit_passB(1, pB_read)
            emit_passB(12, pB_read)
        emit_passB(len(units), pB_read)
        assert st["Bi"] == len(units), f"passB incomplete {st['Bi']}"
        phaseB_pe_done = (pe, pe.n)

        b2b_ready = None
        for dmc in range(4):
            bm = kb.op("tensor",
                       lambda e, dmc=dmc: e.matmul(
                           P7[:], ONES1[:], B2ROW[:, dmc * CH:(dmc + 1) * CH],
                           start=True, stop=True),
                       pe, 1, waits=[st["p7f"], init_loads, attn_pe_done,
                                     b2b_ready])
            b2b_ready = kb.op("vector",
                              lambda e, dmc=dmc: e.tensor_copy(
                                  B2B[:, dmc * 512:(dmc + 1) * 512], P7[:]),
                              ve, 1, waits=[bm])
            st["p7f"] = b2b_ready

        pC_h_read = list(pB_read)
        pC_g_read = [None, None, None, None]
        w1p_war = [None, None]
        xnc_war = None
        w2m_war = [None, None, None, None]
        g_war = [None, None]
        st5_war = [None, None, None, None]
        xc_war = [None, None, None, None]
        zt_war = [None, None]
        ht_war = phaseB_pe_done
        y_last = None
        w1p_pend = {}

        def emit_xnc(c, first=False):
            return kb.op("sync",
                         lambda e, c=c: e.dma_start(
                             out=XNC.rearrange("p (d s) -> p d s", d=ND),
                             in_=dxnT.rearrange("p (d s) -> p d s", d=ND)[
                                 :, :, c * CH:(c + 1) * CH]),
                         s_xnc, 16,
                         waits=[xnc_war, xnt_wr,
                                st["qin_war"][0], st["qin_war"][1],
                                phaseB_pe_done if first else None])

        def emit_panel(c, fc):
            psl = fc % 2
            return kb.op("sync",
                         lambda e, fc=fc: e.dma_start(
                             out=w1p_slot(fc % 2).rearrange(
                                 "p (d f) -> p d f", d=ND),
                             in_=dw1t.rearrange("p (d f) -> p d f", d=ND)[
                                 :, :, fc * 512:(fc + 1) * 512]),
                         s_w1p[psl], 16,
                         waits=[st["w1t_done0"], st["w1t_done1"], w1p_war[psl],
                                st["qpanel_war"][0], st["qpanel_war"][1],
                                phaseB_pe_done if (c == 0 and fc < 2) else None])

        xnc_pend = emit_xnc(0, first=True)
        w1p_pend[(0, 0)] = emit_panel(0, 0)
        w1p_pend[(0, 1)] = emit_panel(0, 1)

        NFP = NF // 2
        w2seq = [(c_, d_, f_) for c_ in range(NCH) for d_ in range(4)
                 for f_ in range(NFP)]
        w2_pend = {}
        pC_sl = [None, None, None, None]
        cmb_st = {}
        cmb_xc = {}

        def emit_w2ld(n):
            c_, dmc, fp = w2seq[n]
            dsl = n % 4
            return kb.op("sync",
                         lambda e, dmc=dmc, fp=fp, m=dsl: e.dma_start(
                             out=w2m_dslot(m).rearrange(
                                 "p (b n) -> p b n", b=2),
                             in_=dw2t.rearrange("p (ff n) -> p ff n",
                                                ff=NF)[
                                 :, 2 * fp:2 * fp + 2,
                                 dmc * CH:(dmc + 1) * CH]),
                         s_w2m[dsl], 16,
                         waits=[st["w2t_done0"], st["w2t_done1"],
                                w2m_war[dsl],
                                sd_last2[0] if n < 4 else None,
                                sd_last2[1] if n < 4 else None,
                                xnb_attn_last if n < 4 else None])

        for n_ in range(4):
            w2_pend[n_] = emit_w2ld(n_)
        w2n = [4]

        for c in range(NCH):
            xnc_all = xnc_pend
            last_gelu_of_f = {}
            for fc in range(16):
                psl = fc % 2
                pl_all = w1p_pend.pop((c, fc))
                for fl in range(4):
                    f = fc * 4 + fl
                    bank = f % 2
                    for dt in range(ND):
                        w = ([pl_all, xnc_all, pC_h_read[bank],
                              st["qtrans_bank_read"][bank]] if dt == 0 else [])
                        kb.op("tensor",
                              lambda e, b=bank, p=psl, dt=dt, fl=fl: e.matmul(
                                  P[b][:],
                                  w1p_slot(p)[:, dt * 512 + fl * 128:
                                              dt * 512 + (fl + 1) * 128],
                                  xnc_d(dt),
                                  start=(dt == 0), stop=(dt == ND - 1)),
                              pe if dt == ND - 1 else None,
                              1 if dt == ND - 1 else 0, waits=w)
                    hmm_done = (pe, pe.n)
                    w1p_war[psl] = hmm_done
                    gl = kb.op("scalar",
                               lambda e, b=bank, f=f: e.activation(
                                   XNT[:, f * 512:(f + 1) * 512], P[b][:],
                                   AF.Gelu, bias=B1C[:, f:f + 1]),
                               ae, 1, waits=[hmm_done, b1c_ready, ht_war])
                    pC_h_read[bank] = gl
                    last_gelu_of_f[f] = gl
                nfc, nc_ = (fc + 2, c) if fc + 2 < 16 else (fc + 2 - 16, c + 1)
                if nc_ < NCH:
                    w1p_pend[(nc_, nfc)] = emit_panel(nc_, nfc)
            xnc_war = (pe, pe.n)
            if DBG_SCRATCH:
                kb.op("gpsimd",
                      lambda e, c=c: e.dma_start(
                          out=dbg_ht[:, c * NF * CH:(c + 1) * NF * CH],
                          in_=XNT[:, 0:NF * CH]),
                      s_init, 16, waits=[last_gelu_of_f[NF - 1]])
            if c + 1 < NCH:
                xnc_pend = emit_xnc(c + 1)

            for dmc in range(4):
                for fp in range(NFP):
                    dsl = fp % 4
                    nflat = (c * 4 + dmc) * NFP + fp
                    w2ld = w2_pend.pop(nflat)
                    for fh in range(2):
                        f = 2 * fp + fh
                        for tt in range(4):
                            gbank = 2 + tt
                            w = [w2ld, last_gelu_of_f[f]]
                            if f == 0:
                                w.append(pC_g_read[tt])
                            kb.op("tensor",
                                  lambda e, gb=gbank, f=f, tt=tt, m=dsl, fh=fh:
                                  e.matmul(
                                      P[gb][:], XNT[:, f * 512 + tt * 128:
                                                    f * 512 + (tt + 1) * 128],
                                      w2m_dslot(m)[:, fh * 512:(fh + 1) * 512],
                                      start=(f == 0), stop=(f == NF - 1)),
                                  pe if (fh == 1 and tt == 3) else None,
                                  1 if (fh == 1 and tt == 3) else 0, waits=w)
                    w2m_war[dsl] = (pe, pe.n)
                    if w2n[0] < len(w2seq):
                        w2_pend[w2n[0]] = emit_w2ld(w2n[0])
                        w2n[0] += 1
                    if fp == 16:
                        for tt in range(4):
                            trow = c * 4 + tt
                            cmb_st[tt] = kb.op(
                                "gpsimd",
                                lambda e, trow=trow, dmc=dmc, s=tt: e.dma_start(
                                    out=st5_slot(s),
                                    in_=dsd[trow * 128:(trow + 1) * 128,
                                            dmc * CH:(dmc + 1) * CH]),
                                s_st5[tt], 16,
                                waits=[sd_last2[0], sd_last2[1], st5_war[tt],
                                       st["qout_war"][0], st["qout_war"][1],
                                       st["prev_r"]])
                            cmb_xc[tt] = kb.op(
                                "gpsimd",
                                lambda e, trow=trow, dmc=dmc, s=tt: e.dma_start(
                                    out=xc_slot(s),
                                    in_=dx[trow * 128:(trow + 1) * 128,
                                           dmc * CH:(dmc + 1) * CH]),
                                s_xc[tt], 16,
                                waits=[xc_war[tt], st["qout_war"][0],
                                       st["qout_war"][1], st["prev_r"]])
                g_stop = (pe, pe.n)
                gpar = dmc % 2
                for tt in range(4):
                    zb = kb.op("vector",
                               lambda e, gb=2 + tt, dmc=dmc, tt=tt:
                               e.tensor_tensor(
                                   zt_slot(tt % 2), P[gb][:],
                                   B2B[:, dmc * 512:(dmc + 1) * 512], ALU.add),
                               ve, 1, waits=[g_stop, b2b_ready, zt_war[tt % 2]])
                    sl = kb.op("scalar",
                               lambda e, gp=gpar, tt=tt: e.activation(
                                   g_slice(gp, tt), zt_slot(tt % 2), AF.Silu),
                               ae, 1, waits=[zb, g_war[gpar]])
                    zt_war[tt % 2] = sl
                    pC_g_read[tt] = zb
                    pC_sl[tt] = sl
                for tt in range(4):
                    sg = kb.op("vector",
                               lambda e, s=tt, gp=gpar, tt=tt: e.tensor_tensor(
                                   st5_slot(s), st5_slot(s), g_slice(gp, tt),
                                   ALU.mult),
                               ve, 1, waits=[cmb_st[tt], pC_sl[tt]])
                    g_war[gpar] = sg
                    ad = kb.op("vector",
                               lambda e, s=tt: e.tensor_tensor(
                                   xc_slot(s), xc_slot(s), st5_slot(s),
                                   ALU.add),
                               ve, 1, waits=[sg, cmb_xc[tt]])
                    st5_war[tt] = ad
                    trow = c * 4 + tt
                    y_last = kb.op("gpsimd",
                                   lambda e, trow=trow, dmc=dmc, s=tt:
                                   e.dma_start(
                                       out=dy[trow * 128:(trow + 1) * 128,
                                              dmc * CH:(dmc + 1) * CH],
                                       in_=xc_slot(s)),
                                   s_y[tt], 16, waits=[ad])
                    xc_war[tt] = y_last
                ht_war = g_stop

        kb.op("sync", lambda e: e.nop(),
              waits=[xc_war[0], xc_war[1], xc_war[2], xc_war[3]])

        for s_ in kb.sems:
            s_.h = E(nc.semaphore(s_.name))
        block = E(nc.Block())

        @block.sync
        def _(e):
            kb.realize("sync", e)

        @block.scalar
        def _(e):
            kb.realize("scalar", e)

        @block.vector
        def _(e):
            kb.realize("vector", e)

        @block.tensor
        def _(e):
            kb.realize("tensor", e)

        @block.gpsimd
        def _(e):
            kb.realize("gpsimd", e)

    return nc


_NC_CACHE = None


def kernel(L0_attn_x, L0_attn_w_qkv, L0_attn_w_out, L0_ffn_w1, L0_ffn_b1,
           L0_ffn_w2, L0_ffn_b2):
    global _NC_CACHE
    x = np.ascontiguousarray(np.asarray(L0_attn_x, dtype=np.float32))
    wqkv = np.ascontiguousarray(np.asarray(L0_attn_w_qkv, dtype=np.float32))
    wout = np.ascontiguousarray(np.asarray(L0_attn_w_out, dtype=np.float32))
    w1 = np.ascontiguousarray(np.asarray(L0_ffn_w1, dtype=np.float32))
    b1 = np.ascontiguousarray(np.asarray(L0_ffn_b1, dtype=np.float32)).reshape(NF, 128)
    w2 = np.ascontiguousarray(np.asarray(L0_ffn_w2, dtype=np.float32))
    b2 = np.ascontiguousarray(np.asarray(L0_ffn_b2, dtype=np.float32)).reshape(1, DM)
    ident = np.eye(128, dtype=np.float32)
    identb = np.asarray(np.eye(128, dtype=np.float32).astype(ml_dtypes.bfloat16))
    ones1 = np.ones((1, 128), dtype=np.float32)
    masks = build_masks()

    if _NC_CACHE is None:
        _NC_CACHE = build_nc()
    nc = _NC_CACHE

    in_maps = []
    for i in range(N_CORES):
        in_maps.append({
            "x": np.ascontiguousarray(x[i]),
            "wqkv": wqkv, "wout": wout, "w1": w1, "b1": b1,
            "w2": w2, "b2": b2, "ident": ident, "identb": identb,
            "ones1": ones1, "masks": masks,
        })
    res = run_bass_kernel_spmd(nc, in_maps, core_ids=list(range(N_CORES)))
    out = np.stack([np.asarray(r["y"], dtype=np.float32) for r in res.results])
    return out


# version 09 (final)
# speedup vs baseline: 1.8357x; 1.0056x over previous
import sys
sys.path.insert(0, "/opt/trn_rl_repo")
from contextlib import ExitStack
import numpy as np
import ml_dtypes

import concourse.bass as bass
import concourse.mybir as mybir
from concourse.bass_utils import run_bass_kernel_spmd

F32 = mybir.dt.float32
BF16 = mybir.dt.bfloat16
AF = mybir.ActivationFunctionType
ALU = mybir.AluOpType
AX = mybir.AxisListType

SEQ = 2048
DM = 2048
H = 16
DFF = 8192
NT = 16
ND = 16
NF = 64
NCH = 4
CH = 512
SCL = float(1.0 / np.sqrt(128.0))
NEG = -1e10
N_CORES = 8

ENGS = ("sync", "scalar", "vector", "tensor", "gpsimd")


class Sem:
    __slots__ = ("name", "n", "h")

    def __init__(self, name):
        self.name = name
        self.n = 0
        self.h = None


class KB:
    def __init__(self):
        self.ops = {e: [] for e in ENGS}
        self.sems = []
        self.guar = {e: {} for e in ENGS}

    def sem(self, name):
        s = Sem(name)
        self.sems.append(s)
        return s

    def wait(self, eng, sm, val):
        if sm is None or val <= 0:
            return
        if self.guar[eng].get(sm, 0) >= val:
            return
        self.guar[eng][sm] = val
        self.ops[eng].append(("w", sm, val))

    def op(self, eng, fn, sm=None, inc=0, waits=()):
        for w in waits:
            if w is not None:
                self.wait(eng, w[0], w[1])
        if sm is not None:
            sm.n += inc
            self.ops[eng].append(("o", fn, sm, inc))
            return (sm, sm.n)
        self.ops[eng].append(("o", fn, None, 0))
        return None

    def realize(self, eng, h):
        for item in self.ops[eng]:
            if item[0] == "w":
                h.wait_ge(item[1].h, item[2])
            else:
                inst = item[1](h)
                if item[2] is not None:
                    inst.then_inc(item[2].h, item[3])


def build_masks():
    kl = np.arange(128)[:, None]
    x = np.arange(896)[None, :]
    m = np.where(x >= kl + 384, 0.0, NEG).astype(np.float32)
    return np.asarray(m.astype(ml_dtypes.bfloat16))


DBG_SCRATCH = False


def build_nc():
    nc = bass.Bass("TRN2", target_bir_lowering=False)

    dx = nc.dram_tensor("x", [SEQ, DM], F32, kind="ExternalInput").ap()
    dwqkv = nc.dram_tensor("wqkv", [DM, 3 * DM], F32, kind="ExternalInput").ap()
    dwout = nc.dram_tensor("wout", [DM, DM], F32, kind="ExternalInput").ap()
    dw1 = nc.dram_tensor("w1", [DFF, DM], F32, kind="ExternalInput").ap()
    db1 = nc.dram_tensor("b1", [NF, 128], F32, kind="ExternalInput").ap()
    dw2 = nc.dram_tensor("w2", [DM, DFF], F32, kind="ExternalInput").ap()
    db2 = nc.dram_tensor("b2", [1, DM], F32, kind="ExternalInput").ap()
    dident = nc.dram_tensor("ident", [128, 128], F32, kind="ExternalInput").ap()
    didentb = nc.dram_tensor("identb", [128, 128], BF16, kind="ExternalInput").ap()
    dones1 = nc.dram_tensor("ones1", [1, 128], F32, kind="ExternalInput").ap()
    dmasks = nc.dram_tensor("masks", [128, 896], BF16, kind="ExternalInput").ap()
    dy = nc.dram_tensor("y", [SEQ, DM], F32, kind="ExternalOutput").ap()

    _SK = "ExternalOutput" if DBG_SCRATCH else "Internal"
    dbg_ht = (nc.dram_tensor("dbg_ht", [128, 4 * NF * CH], BF16,
                             kind="ExternalOutput").ap() if DBG_SCRATCH else None)
    dbg_g = (nc.dram_tensor("dbg_g", [128, 4 * 4 * 2048], BF16,
                            kind="ExternalOutput").ap() if DBG_SCRATCH else None)
    dxnT = nc.dram_tensor("dxnT", [128, ND * SEQ], BF16, kind=_SK).ap()
    dotd = nc.dram_tensor("dotd", [128, H * SEQ], BF16, kind=_SK).ap()
    dsd = nc.dram_tensor("sd", [SEQ, DM], BF16, kind=_SK).ap()
    dw1t = nc.dram_tensor("w1t", [128, ND * DFF], BF16, kind=_SK).ap()
    dw2t = nc.dram_tensor("w2t", [128, NF * DM], BF16, kind=_SK).ap()

    kb = KB()
    pe = kb.sem("pe")
    ve = kb.sem("ve")
    ae = kb.sem("ae")
    ge = kb.sem("ge")
    s_init = kb.sem("s_init")
    s_xf = [kb.sem(f"s_xf{i}") for i in range(4)]
    s_qin = [kb.sem("s_qin0"), kb.sem("s_qin1")]
    s_w1t = [kb.sem("s_w1t0"), kb.sem("s_w1t1")]
    s_w2t = [kb.sem("s_w2t0"), kb.sem("s_w2t1")]
    s_xntw = kb.sem("s_xntw")
    s_otd = kb.sem("s_otd")
    s_sd = [kb.sem("s_sd0"), kb.sem("s_sd1")]
    s_wq = [kb.sem("s_wq0"), kb.sem("s_wq1")]
    s_wk = [kb.sem("s_wk0"), kb.sem("s_wk1")]
    s_wv = [kb.sem("s_wv0"), kb.sem("s_wv1")]
    s_wop = [kb.sem("s_wop0"), kb.sem("s_wop1")]
    s_ot2 = [kb.sem(f"s_ot2{i}") for i in range(4)]
    s_xnc = kb.sem("s_xnc")
    s_w1p = [kb.sem("s_w1p0"), kb.sem("s_w1p1")]
    s_w2m = [kb.sem(f"s_w2m{i}") for i in range(4)]
    s_st5 = [kb.sem(f"s_st5{i}") for i in range(4)]
    s_xc = [kb.sem(f"s_xc{i}") for i in range(4)]
    s_y = [kb.sem(f"s_y{i}") for i in range(4)]

    with ExitStack() as ctx:
        E = ctx.enter_context
        sb, ps = nc.sbuf_tensor, nc.psum_tensor

        XNT = E(sb("XNT", [128, ND * SEQ], BF16))
        BIGB = E(sb("BIGB", [128, 24576], BF16))
        QKVB = E(sb("QKVB", [128, 3 * SEQ], BF16))
        WQB = E(sb("WQB", [128, 2 * 3 * DM], BF16))
        XNB = E(sb("XNB", [128, 4096], BF16))
        WP = E(sb("WP", [128, 8192], BF16))
        MASKS = E(sb("MASKS", [128, 896], BF16))
        IDENT = E(sb("IDENT", [128, 128], F32))
        IDENTB = E(sb("IDENTB", [128, 128], BF16))
        ONES1 = E(sb("ONES1", [1, 128], F32))
        ONESCB = E(sb("ONESCB", [128, 1], BF16))
        ONESCF = E(sb("ONESCF", [128, 1], F32))
        EPSC = E(sb("EPSC", [128, 1], F32))
        B1RS = E(sb("B1RS", [64, 128], F32))
        B1C = E(sb("B1C", [128, 64], F32))
        B2ROW = E(sb("B2ROW", [1, DM], F32))
        LNS = E(sb("LNS", [128, 24], F32))
        WACC = E(sb("WACC", [128, 128], F32))
        ANCH = E(sb("ANCH", [128, 4], F32))
        ANCM = E(sb("ANCM", [128, 1], F32))
        NEGM = E(sb("NEGM", [128, 1], F32))
        NM1 = E(sb("NM1", [1, 1], F32))
        NM2 = E(sb("NM2", [1, 1], F32))
        SRL = E(sb("SRL", [1, CH], F32))
        SREC = E(sb("SREC", [1, CH], F32))
        SCLB = [E(sb("SCL1", [1, 1], F32)), E(sb("SCL2", [1, 1], F32))]
        TCOL = [E(sb("TC1", [128, 1], F32)), E(sb("TC2", [128, 1], F32))]
        NTCOL = [E(sb("NTC1", [128, 1], F32)), E(sb("NTC2", [128, 1], F32))]
        WSUM = [E(sb("WS1", [128, 1], F32)), E(sb("WS2", [128, 1], F32))]

        P = [E(ps(f"P{i}", [128, 512], F32)) for i in range(6)]
        PROW = E(ps("PROW", [1, 512], F32))
        P7 = E(ps("P7", [128, 512], F32))

        def xf_slot(i):
            return BIGB[:, i * 4096:(i + 1) * 4096].bitcast(F32)

        def xnb_slot(i):
            return BIGB[:, 16384 + i * 2048:16384 + (i + 1) * 2048]

        LNDUMP = BIGB[:, 20480:24576].bitcast(F32)

        def qpanel_slot(i):
            return BIGB[:, i * 8192:(i + 1) * 8192]

        def qin_slot(i):
            return BIGB[:, 16384 + i * 4096:16384 + (i + 1) * 4096].bitcast(F32)

        def w1p_slot(i):
            return BIGB[:, i * 8192:(i + 1) * 8192]

        XNC = BIGB[:, 16384:24576]

        def xnc_d(dt):
            return BIGB[:, 16384 + dt * 512:16384 + (dt + 1) * 512]

        def woutp_slot(i):
            return XNT[:, i * 8192:(i + 1) * 8192]

        def otst2_slot(i):
            return XNT[:, 16384 + i * 2048:16384 + (i + 1) * 2048]

        qh = QKVB[:, 0:SEQ]
        kh = QKVB[:, SEQ:2 * SEQ]
        vh = QKVB[:, 2 * SEQ:3 * SEQ]

        def wq_s(i):
            return WQB[:, i * 6144 + 0:i * 6144 + DM]

        def wk_s(i):
            return WQB[:, i * 6144 + DM:i * 6144 + 2 * DM]

        def wv_s(i):
            return WQB[:, i * 6144 + 2 * DM:i * 6144 + 3 * DM]

        B2B = WQB[:, 0:4096].bitcast(F32)

        def zt_slot(i):
            return WQB[:, 4096 + i * 1024:4096 + (i + 1) * 1024].bitcast(F32)

        def g_slice(par, tt):
            return WQB[:, 6144 + par * 2048 + tt * 512:
                       6144 + par * 2048 + (tt + 1) * 512]

        _PT_OFF = [0, 512, 3072, 3584]

        def pt_slot(i):
            return XNB[:, _PT_OFF[i]:_PT_OFF[i] + 512]

        RECIPB = XNB[:, 1024:2048].bitcast(F32)

        def otst_slot(i):
            return XNB[:, 2048 + i * 512:2048 + (i + 1) * 512]

        def sst_slot(i):
            return XNB[:, i * 512:(i + 1) * 512]

        def w2m_dslot(i):
            return XNB[:, i * 1024:(i + 1) * 1024]

        def qout_slot(i):
            return WP[:, i * 2048:(i + 1) * 2048]

        QTMP = WP[:, 4096:8192].bitcast(F32)

        def xc_slot(i):
            return WP[:, i * 1024:(i + 1) * 1024].bitcast(F32)

        def st5_slot(i):
            return WP[:, 4096 + i * 512:4096 + (i + 1) * 512]

        lnsum = [LNS[:, i:i + 1] for i in range(0, 4)]
        lnmean = [LNS[:, i:i + 1] for i in range(4, 8)]
        lnvs = [LNS[:, i:i + 1] for i in range(8, 12)]
        lnln = [LNS[:, i:i + 1] for i in range(12, 16)]
        lnrstd = [LNS[:, i:i + 1] for i in range(16, 20)]

        for dst, src in ((IDENT[:], dident), (IDENTB[:], didentb),
                         (ONES1[:], dones1), (MASKS[:], dmasks),
                         (B1RS[:], db1), (B2ROW[:], db2)):
            kb.op("sync", lambda e, d=dst, s=src: e.dma_start(out=d, in_=s),
                  s_init, 16)
        init_loads = (s_init, s_init.n)

        kb.op("gpsimd", lambda e: e.memset(ONESCB[:], 1.0), ge, 1)
        kb.op("gpsimd", lambda e: e.memset(ONESCF[:], 1.0), ge, 1)
        kb.op("gpsimd", lambda e: e.memset(EPSC[:], 1e-5), ge, 1)
        init_memset = (ge, ge.n)

        b1t = kb.op("tensor", lambda e: e.transpose(P7[:, 0:64], B1RS[:],
                                                    IDENT[0:64, 0:64]),
                    pe, 1, waits=[init_loads])
        b1c_ready = kb.op("vector", lambda e: e.tensor_copy(B1C[:], P7[:, 0:64]),
                          ve, 1, waits=[b1t])

        units = [(0, dh, ch) for dh in range(ND) for ch in range(4)] + \
                [(1, fh, 0) for fh in range(NF)]

        def unit_src(mat, pnl, ch):
            if mat == 0:
                return dw1[ch * 2048:(ch + 1) * 2048,
                           pnl * 128:(pnl + 1) * 128].rearrange(
                    "(t p) d -> p t d", p=128)
            return dw2[:, pnl * 128:(pnl + 1) * 128].rearrange(
                "(t p) d -> p t d", p=128)

        st = {
            "Ai": 0, "Bi": 0,
            "qin_war": [None, None],
            "qout_war": [None, None],
            "qpanel_war": [None, None],
            "w1A_done": None, "w2A_done": None,
            "s_done": [None, None],
            "prev_r": None,
            "w1t_done0": None, "w1t_done1": None,
            "w2t_done0": None, "w2t_done1": None,
            "qtrans_bank_read": [None, None],
            "p7f": None,
        }
        st["p7f"] = b1c_ready

        def emit_passA(nmax):
            for _ in range(nmax):
                if st["Ai"] >= len(units):
                    return
                mat, pnl, ch = units[st["Ai"]]
                slot = st["Ai"] % 2
                col = st["Ai"]
                v = kb.op("sync",
                          lambda e, m=mat, p=pnl, c=ch, sl=slot: e.dma_start(
                              out=qin_slot(sl).rearrange("p (t d) -> p t d", t=16),
                              in_=unit_src(m, p, c)),
                          s_qin[slot], 16, waits=[st["qin_war"][slot]])
                r = kb.op("vector",
                          lambda e, sl=slot, c=col: e.reduce_sum(
                              WACC[:, c:c + 1], qin_slot(sl), axis=AX.X,
                              apply_absolute_value=True),
                          ve, 1, waits=[v])
                st["qin_war"][slot] = r
                st["Ai"] += 1
                if st["Ai"] == 64:
                    st["w1A_done"] = r
                if st["Ai"] == 128:
                    st["w2A_done"] = r

        def finalize_scale(mat, prow_rd):
            a_done = st["w1A_done"] if mat == 0 else st["w2A_done"]
            assert a_done is not None
            v1 = kb.op("vector",
                       lambda e, m=mat: e.tensor_reduce(
                           WSUM[m][:], WACC[:, m * 64:(m + 1) * 64],
                           axis=AX.X, op=ALU.add),
                       ve, 1, waits=[a_done])
            t1 = kb.op("tensor",
                       lambda e, m=mat: e.matmul(PROW[0:1, 0:1], ONESCF[:],
                                                 WSUM[m][:], start=True, stop=True),
                       pe, 1, waits=[v1, init_memset, prow_rd])
            a1 = kb.op("scalar",
                       lambda e, m=mat: e.activation(SCLB[m][:], PROW[0:1, 0:1],
                                                     AF.Copy,
                                                     scale=0.7 / float(DFF * DM)),
                       ae, 1, waits=[t1])
            t2 = kb.op("tensor",
                       lambda e, m=mat: e.matmul(P7[:, 0:1], ONES1[:], SCLB[m][:],
                                                 start=True, stop=True),
                       pe, 1, waits=[a1, st["p7f"], init_loads])
            v2 = kb.op("vector",
                       lambda e, m=mat: e.tensor_copy(TCOL[m][:], P7[:, 0:1]),
                       ve, 1, waits=[t2])
            v3 = kb.op("vector",
                       lambda e, m=mat: e.tensor_scalar(
                           NTCOL[m][:], TCOL[m][:], -1.0, 1.0, ALU.mult, ALU.mult),
                       ve, 1, waits=[v2])
            st["s_done"][mat] = v3
            st["p7f"] = v2
            return a1

        st["B_pend"] = None

        def passB_stage2(ent):
            ui, mat, pnl, ch, oslot, r2 = ent
            pslot = pnl % 2
            evacs = []
            pview = P7.bitcast(BF16)
            for rr in range(2):
                for j in range(8):
                    w = ([r2, st["p7f"]] if j == 0 else [])
                    kb.op("tensor",
                          lambda e, pv=pview, o=oslot, rr=rr, j=j: e.transpose(
                              pv[:, j * 128:(j + 1) * 128],
                              qout_slot(o)[:, (rr * 8 + j) * 128:
                                           (rr * 8 + j + 1) * 128],
                              IDENTB[:]),
                          pe if j == 7 else None, 1 if j == 7 else 0,
                          waits=w)
                tdone = (pe, pe.n)
                base = (ch * 2048 + rr * 1024) if mat == 0 else (rr * 1024)
                evc = kb.op("scalar",
                            lambda e, pv=pview, psl=pslot, b=base: e.copy(
                                qpanel_slot(psl)[:, b:b + 1024], pv[:]),
                            ae, 1, waits=[tdone, st["qpanel_war"][pslot]])
                st["p7f"] = evc
                evacs.append(evc)
            if mat == 0 and ch == 3:
                wr = kb.op("gpsimd",
                           lambda e, p=pnl, psl=pslot: e.dma_start(
                               out=dw1t[:, p * 8192:(p + 1) * 8192],
                               in_=qpanel_slot(psl)),
                           s_w1t[pslot], 16, waits=evacs)
                st["qpanel_war"][pslot] = wr
                st["w1t_done" + str(pslot)] = wr
            elif mat == 1:
                wr = kb.op("gpsimd",
                           lambda e, p=pnl, psl=pslot: e.dma_start(
                               out=dw2t[:, p * 2048:(p + 1) * 2048],
                               in_=qpanel_slot(psl)[:, 0:2048]),
                           s_w2t[pslot], 16, waits=evacs)
                st["qpanel_war"][pslot] = wr
                st["w2t_done" + str(pslot)] = wr
            st["qout_war"][oslot] = (pe, pe.n)

        def emit_passB(nmax, pA_read):
            for _ in range(nmax):
                if st["Bi"] >= len(units) and st["B_pend"] is None:
                    return
                if st["B_pend"] is not None:
                    passB_stage2(st["B_pend"])
                    st["B_pend"] = None
                if st["Bi"] >= len(units):
                    continue
                mat, pnl, ch = units[st["Bi"]]
                if st["s_done"][mat] is None:
                    return
                ui = st["Bi"]
                islot = ui % 2
                oslot = ui % 2
                v = kb.op("sync",
                          lambda e, m=mat, p=pnl, c=ch, sl=islot: e.dma_start(
                              out=qin_slot(sl).rearrange("p (t d) -> p t d", t=16),
                              in_=unit_src(m, p, c)),
                          s_qin[islot], 16, waits=[st["qin_war"][islot]])
                r1 = kb.op("vector",
                           lambda e, sl=islot, m=mat: e.tensor_scalar(
                               QTMP, qin_slot(sl), TCOL[m][:], -1.0,
                               ALU.is_ge, ALU.add),
                           ve, 1, waits=[v, st["s_done"][mat], st["prev_r"]])
                r2 = kb.op("vector",
                           lambda e, sl=islot, o=oslot, m=mat:
                           e.scalar_tensor_tensor(
                               qout_slot(o), qin_slot(sl), NTCOL[m][:], QTMP,
                               ALU.is_gt, ALU.add),
                           ve, 1, waits=[r1, st["qout_war"][oslot]])
                st["prev_r"] = r2
                st["qin_war"][islot] = r2
                st["B_pend"] = (ui, mat, pnl, ch, oslot, r2)
                st["Bi"] += 1

        xf_war = [None, None, None, None]
        lnvs_war = [None, None, None, None]
        xnb_war = [None, None]
        lnt_tile_done = [None] * NT
        lnt_last_evac = None
        pA_read = [None] * 6
        qk_evac = {}
        vh_evac = {}
        anch_round = {}
        wq_war = [None, None]
        wk_war = [None, None]
        wv_war = [None, None]

        def cast_load(dst, c0):
            return lambda e, d=dst, c=c0: e.dma_start(
                out=d.rearrange("p (t f) -> p t f", t=ND),
                in_=dwqkv[:, c:c + 128].rearrange("(t p) f -> p t f", p=128))

        gq = [None, None]
        gk = [None, None]
        gv = [None, None]
        gq[0] = kb.op("gpsimd", cast_load(wq_s(0), 0), s_wq[0], 16)
        gk[0] = kb.op("gpsimd", cast_load(wk_s(0), DM), s_wk[0], 16)
        gv[0] = kb.op("gpsimd", cast_load(wv_s(0), 2 * DM), s_wv[0], 16)

        def emit_qkv_block(h, c):
            hs = h % 2
            for bank, wsl, dst, gld, tag in (
                    (2, wq_s(hs), qh, gq[hs], "q"), (3, wk_s(hs), kh, gk[hs], "k")):
                for dt in range(ND):
                    w = ([gld, lnt_tile_done[4 * c + 3], pA_read[bank]]
                         if dt == 0 else [])
                    kb.op("tensor",
                          lambda e, b=bank, ww=wsl, dt=dt, c=c: e.matmul(
                              P[b][:], ww[:, dt * 128:(dt + 1) * 128],
                              XNT[:, dt * SEQ + c * CH:dt * SEQ + (c + 1) * CH],
                              start=(dt == 0), stop=(dt == ND - 1)),
                          pe if dt == ND - 1 else None,
                          1 if dt == ND - 1 else 0, waits=w)
                mm_done = (pe, pe.n)
                if tag == "q":
                    wq_war[hs] = mm_done
                else:
                    wk_war[hs] = mm_done
                ev = kb.op("scalar",
                           lambda e, b=bank, d=dst, c=c: e.copy(
                               d[:, c * CH:(c + 1) * CH], P[b][:]),
                           ae, 1, waits=[mm_done])
                pA_read[bank] = ev
                qk_evac[(h, tag, c)] = ev
            g = c
            bank = c % 2
            for ttl in range(4):
                tok = 4 * g + ttl
                for dt in range(ND):
                    w = ([gv[hs], lnt_tile_done[4 * g + 3], pA_read[bank]]
                         if (dt == 0 and ttl == 0) else [])
                    kb.op("tensor",
                          lambda e, b=bank, tok=tok, dt=dt, ttl=ttl, hs=hs:
                          e.matmul(
                              P[b][:, ttl * 128:(ttl + 1) * 128],
                              XNT[:, dt * SEQ + tok * 128:
                                  dt * SEQ + (tok + 1) * 128],
                              wv_s(hs)[:, dt * 128:(dt + 1) * 128],
                              start=(dt == 0), stop=(dt == ND - 1)),
                          pe if (dt == ND - 1 and ttl == 3) else None,
                          1 if (dt == ND - 1 and ttl == 3) else 0, waits=w)
            vg_done = (pe, pe.n)
            wv_war[hs] = vg_done
            ev = kb.op("scalar",
                       lambda e, b=bank, g=g: e.copy(
                           vh[:, g * CH:(g + 1) * CH], P[b][:]),
                       ae, 1, waits=[vg_done])
            pA_read[bank] = ev
            vh_evac[(h, g)] = ev
            for rr in range(4):
                qt = 4 * g + rr
                w = ([qk_evac[(h, "q", c)], qk_evac[(h, "k", c)], st["p7f"]]
                     if rr == 0 else [])
                kb.op("tensor",
                      lambda e, qt=qt, rr=rr: e.matmul(
                          P7[:, rr * 128:(rr + 1) * 128],
                          qh[:, qt * 128:(qt + 1) * 128],
                          kh[:, qt * 128:(qt + 1) * 128],
                          start=True, stop=True),
                      pe if rr == 3 else None, 1 if rr == 3 else 0, waits=w)
            diag_done = (pe, pe.n)
            ar = kb.op("vector",
                       lambda e, g=g: e.reduce_max(ANCH[:, g:g + 1], P7[:],
                                                   axis=AX.X, negate=True),
                       ve, 1, waits=[diag_done])
            anch_round[(h, g)] = ar
            st["p7f"] = ar
        ln_a2 = {}

        def emit_ln_stats(t):
            s = t % 4
            xd = kb.op("sync",
                       lambda e, t=t, s=s: e.dma_start(
                           out=xf_slot(s), in_=dx[t * 128:(t + 1) * 128, :]),
                       s_xf[s], 16, waits=[xf_war[s]])
            v1 = kb.op("vector",
                       lambda e, s=s: e.reduce_sum(lnsum[s], xf_slot(s), axis=AX.X),
                       ve, 1, waits=[xd])
            v2 = kb.op("vector",
                       lambda e, s=s: e.tensor_scalar(
                           lnmean[s], lnsum[s], 1.0 / DM, 1.0, ALU.mult, ALU.mult),
                       ve, 1, waits=[v1])
            v3 = kb.op("vector",
                       lambda e, s=s: e.scalar_tensor_tensor(
                           LNDUMP, xf_slot(s), lnmean[s], xf_slot(s),
                           ALU.subtract, ALU.mult, accum_out=lnvs[s]),
                       ve, 1, waits=[v2, lnvs_war[s]])
            a1 = kb.op("scalar",
                       lambda e, s=s: e.activation(lnln[s], lnvs[s], AF.Ln,
                                                   bias=EPSC[:], scale=1.0 / DM),
                       ae, 1, waits=[v3, init_memset])
            lnvs_war[s] = a1
            a2 = kb.op("scalar",
                       lambda e, s=s: e.activation(lnrstd[s], lnln[s], AF.Exp,
                                                   scale=-0.5),
                       ae, 1, waits=[a1])
            ln_a2[t] = a2

        def emit_ln_finish(t):
            s = t % 4
            s2 = t % 2
            v4 = kb.op("vector",
                       lambda e, s=s, s2=s2: e.tensor_scalar(
                           xnb_slot(s2), xf_slot(s), lnmean[s], lnrstd[s],
                           ALU.subtract, ALU.mult),
                       ve, 1, waits=[ln_a2[t], xnb_war[s2]])
            xf_war[s] = v4
            last = None
            for rr in range(2):
                bank = rr
                pview = P[bank].bitcast(BF16)
                for j in range(8):
                    w = ([v4, pA_read[bank]] if j == 0 else [])
                    kb.op("tensor",
                          lambda e, pv=pview, s2=s2, rr=rr, j=j: e.transpose(
                              pv[:, j * 128:(j + 1) * 128],
                              xnb_slot(s2)[:, (rr * 8 + j) * 128:
                                           (rr * 8 + j + 1) * 128],
                              IDENTB[:]),
                          pe if j == 7 else None, 1 if j == 7 else 0, waits=w)
                tdone = (pe, pe.n)
                ev = kb.op("scalar",
                           lambda e, pv=pview, t=t, rr=rr: e.copy(
                               XNT[:, 0:ND * SEQ].rearrange(
                                   "p (d s) -> p d s", d=ND)[
                                   :, rr * 8:(rr + 1) * 8,
                                   t * 128:(t + 1) * 128],
                               pv[:].rearrange("p (d s) -> p d s", d=8)),
                           ae, 1, waits=[tdone])
                pA_read[rr] = ev
                last = ev
            lnt_tile_done[t] = last
            xnb_war[s2] = last
            if t % 4 == 3:
                emit_qkv_block(0, t // 4)
            return last

        for t in range(NT):
            emit_ln_stats(t)
            if t >= 1:
                lnt_last_evac = emit_ln_finish(t - 1)
        lnt_last_evac = emit_ln_finish(NT - 1)
        xnt_ready = lnt_last_evac
        st["qin_war"] = [lnt_last_evac, lnt_last_evac]
        st["qpanel_war"] = [lnt_last_evac, lnt_last_evac]
        xnt_wr = None
        for q in range(4):
            xnt_wr = kb.op("gpsimd",
                           lambda e, q=q: e.dma_start(
                               out=dxnT[:, q * 8192:(q + 1) * 8192],
                               in_=XNT[:, q * 8192:(q + 1) * 8192]),
                           s_xntw, 16, waits=[xnt_ready])

        pt_war = [None, None, None, None]
        pending_epi = [None]
        sj = 0
        prow_rd = None
        pv_read = [None, None]
        exp_last_prev = None
        otd_wr_last = None
        xnt_attn_last_read = None
        xnb_attn_last = None
        sums_stop = None
        pv_stop = None



        for h in range(H):
            hs = h % 2

            if h + 1 < H:
                ns = (h + 1) % 2
                gq[ns] = kb.op("gpsimd", cast_load(wq_s(ns), (h + 1) * 128),
                               s_wq[ns], 16, waits=[wq_war[ns]])
                gk[ns] = kb.op("gpsimd", cast_load(wk_s(ns), DM + (h + 1) * 128),
                               s_wk[ns], 16, waits=[wk_war[ns]])
                gv[ns] = kb.op("gpsimd",
                               cast_load(wv_s(ns), 2 * DM + (h + 1) * 128),
                               s_wv[ns], 16, waits=[wv_war[ns]])

            if h > 0:
                emit_qkv_block(h, 0)
                emit_qkv_block(h, 1)
                if pending_epi[0] is not None:
                    pending_epi[0]()
                    pending_epi[0] = None
                for c in range(2, NCH):
                    emit_qkv_block(h, c)
            qk_evac_last = qk_evac[(h, "k", NCH - 1)]
            anch_rounds = [anch_round[(h, g)] for g in range(4)]
            if h == H - 1:
                xnt_attn_last_read = (pe, pe.n)

            am = kb.op("vector",
                       lambda e: e.tensor_reduce(ANCM[:], ANCH[:], axis=AX.X,
                                                 op=ALU.min),
                       ve, 1, waits=anch_rounds)
            tr = kb.op("tensor",
                       lambda e: e.transpose(PROW[0:1, 0:128], ANCM[:], IDENT[:]),
                       pe, 1, waits=[am, prow_rd])
            nm = kb.op("vector",
                       lambda e: e.tensor_reduce(NM1[:], PROW[0:1, 0:128],
                                                 axis=AX.X, op=ALU.min),
                       ve, 1, waits=[tr])
            prow_rd = nm
            nm2 = kb.op("vector",
                        lambda e: e.tensor_scalar(NM2[:], NM1[:], SCL, 1.0,
                                                  ALU.mult, ALU.mult),
                        ve, 1, waits=[nm])
            bc = kb.op("tensor",
                       lambda e: e.matmul(P7[:, 256:257], ONES1[:], NM2[:],
                                          start=True, stop=True),
                       pe, 1, waits=[nm2, st["p7f"]])
            negm_ready = kb.op("vector",
                               lambda e: e.tensor_copy(NEGM[:], P7[:, 256:257]),
                               ve, 1, waits=[bc, exp_last_prev])
            st["p7f"] = negm_ready

            if h == 5 and st["s_done"][0] is None:
                prow_rd = finalize_scale(0, prow_rd)
            if h == 8 and st["s_done"][1] is None:
                prow_rd = finalize_scale(1, prow_rd)

            exp_last = None
            pend = []

            def make_epi(c, pvbank, sums_stop, pv_stop, h=h):
                def epi():
                    nonlocal prow_rd, xnb_attn_last, otd_wr_last
                    al = kb.op("scalar",
                               lambda e: e.activation(SRL[:], PROW[:], AF.Ln),
                               ae, 1, waits=[sums_stop])
                    prow_rd = al
                    ar2 = kb.op("scalar",
                                lambda e: e.activation(SREC[:], SRL[:], AF.Exp,
                                                       scale=-1.0),
                                ae, 1, waits=[al])
                    bcm = kb.op("tensor",
                                lambda e: e.matmul(P7[:], ONES1[:], SREC[:],
                                                   start=True, stop=True),
                                pe, 1, waits=[ar2, st["p7f"]])
                    rcp = kb.op("vector",
                                lambda e: e.tensor_copy(RECIPB, P7[:]),
                                ve, 1, waits=[bcm])
                    st["p7f"] = rcp
                    om = kb.op("vector",
                               lambda e, c=c, pb=pvbank: e.tensor_tensor(
                                   otst_slot(c % 2), P[pb][:], RECIPB,
                                   ALU.mult),
                               ve, 1, waits=[pv_stop, rcp, otd_wr_last])
                    pv_read[c % 2] = om
                    xnb_attn_last = om
                    otd_wr_last = kb.op("gpsimd",
                                        lambda e, h=h, c=c: e.dma_start(
                                            out=dotd[:, h * SEQ + c * CH:
                                                     h * SEQ + (c + 1) * CH],
                                            in_=otst_slot(c % 2)),
                                        s_otd, 16, waits=[om])
                return epi

            def proc_one():
                pts, j, njt, c, pvbank, ex = pend.pop(0)
                if j == 0 and pending_epi[0] is not None:
                    pending_epi[0]()
                    pending_epi[0] = None
                kb.op("tensor",
                      lambda e, pts=pts, j=j, njt=njt: e.matmul(
                          PROW[:], ONESCB[:], pt_slot(pts),
                          start=(j == 0), stop=(j == njt - 1)),
                      pe if j == njt - 1 else None, 1 if j == njt - 1 else 0,
                      waits=([ex, prow_rd, init_memset] if j == 0 else [ex]))
                sums_stop = (pe, pe.n)
                pvw = ([ex, pv_read[c % 2]] if j == 0 else [ex])
                pv = kb.op("tensor",
                           lambda e, pts=pts, j=j, njt=njt, pb=pvbank: e.matmul(
                               P[pb][:], vh[:, j * 128:(j + 1) * 128],
                               pt_slot(pts),
                               start=(j == 0), stop=(j == njt - 1)),
                           pe, 1, waits=pvw)
                pt_war[pts] = pv
                if j == njt - 1:
                    pending_epi[0] = make_epi(c, pvbank, sums_stop, pv)

            for c in range(NCH):
                pvbank = 4 + (c % 2)
                njt = 4 * c + 4
                for j in range(njt):
                    sbank = sj % 4
                    pts = sj % 4
                    sj += 1
                    smm = kb.op("tensor",
                                lambda e, sbk=sbank, j=j, c=c: e.matmul(
                                    P[sbk][:], kh[:, j * 128:(j + 1) * 128],
                                    qh[:, c * CH:(c + 1) * CH],
                                    start=True, stop=True),
                                pe, 1, waits=[qk_evac_last, pA_read[sbank]])
                    exp_dep = smm
                    if j >= 4 * c:
                        a0 = 384 - 128 * (j - 4 * c)
                        exp_dep = kb.op("vector",
                                        lambda e, sbk=sbank, a0=a0: e.tensor_tensor(
                                            P[sbk][:], P[sbk][:],
                                            MASKS[:, a0:a0 + 512], ALU.add),
                                        ve, 1, waits=[smm, init_loads])
                    ex = kb.op("scalar",
                               lambda e, pts=pts, sbk=sbank: e.activation(
                                   pt_slot(pts), P[sbk][:], AF.Exp,
                                   bias=NEGM[:], scale=SCL),
                               ae, 1, waits=[exp_dep, negm_ready, pt_war[pts]])
                    pA_read[sbank] = ex
                    exp_last = ex
                    pend.append((pts, j, njt, c, pvbank, ex))
                    while len(pend) > 3:
                        proc_one()
                if h >= 8:
                    emit_passB(2, pA_read)
                else:
                    emit_passA(5)
            while pend:
                proc_one()
            exp_last_prev = exp_last

        if pending_epi[0] is not None:
            pending_epi[0]()
            pending_epi[0] = None
        attn_pe_done = (pe, pe.n)

        woutp_war = [None, None]
        otst2_war = [None, None, None, None]
        sst_war = [None, None]
        pB_read = [pA_read[0], pA_read[1]]
        sd_wr_last = None
        sd_last2 = [None, None]

        def emit_otl(k):
            tt = k % NT
            osl = k % 4
            return kb.op("sync",
                         lambda e, tt=tt, o=osl: e.dma_start(
                             out=otst2_slot(o).rearrange("p (g c) -> p g c", g=H),
                             in_=dotd.rearrange("p (g s) -> p g s", g=H)[
                                 :, :, tt * 128:(tt + 1) * 128]),
                         s_ot2[osl], 16,
                         waits=[otd_wr_last, otst2_war[osl], xnt_attn_last_read])

        def emit_woutld(dmc, extra=None):
            return kb.op(
                "gpsimd",
                lambda e, dmc=dmc: e.dma_start(
                    out=woutp_slot(dmc % 2).rearrange("p (t m) -> p t m", t=ND),
                    in_=dwout[:, dmc * CH:(dmc + 1) * CH].rearrange(
                        "(t p) m -> p t m", p=128)),
                s_wop[dmc % 2], 16, waits=[xnt_attn_last_read, extra])

        wout_ld = [None] * 4
        wout_ld[0] = emit_woutld(0)
        wout_ld[1] = emit_woutld(1)

        otl_q = [emit_otl(0), emit_otl(1), emit_otl(2)]
        for dmc in range(4):
            wsl = dmc % 2
            if dmc >= 2:
                wout_ld[dmc] = emit_woutld(dmc, extra=woutp_war[wsl])
            for tt in range(NT):
                k = dmc * NT + tt
                osl = k % 4
                otl = otl_q.pop(0)
                if k + 3 < 4 * NT:
                    otl_q.append(emit_otl(k + 3))
                bank = (dmc * NT + tt) % 2
                for hh in range(H):
                    w = ([otl, wout_ld[dmc], pB_read[bank], attn_pe_done,
                          st["qtrans_bank_read"][bank]]
                         if hh == 0 else [])
                    kb.op("tensor",
                          lambda e, b=bank, o=osl, hh=hh, ww=wsl: e.matmul(
                              P[b][:], otst2_slot(o)[:, hh * 128:(hh + 1) * 128],
                              woutp_slot(ww)[:, hh * CH:(hh + 1) * CH],
                              start=(hh == 0), stop=(hh == H - 1)),
                          pe if hh == H - 1 else None, 1 if hh == H - 1 else 0,
                          waits=w)
                smm = (pe, pe.n)
                otst2_war[osl] = smm
                woutp_war[wsl] = smm
                ssl = k % 2
                ev = kb.op("scalar",
                           lambda e, b=bank, o=ssl: e.copy(
                               sst_slot(o), P[b][:]),
                           ae, 1, waits=[smm, sst_war[ssl], xnb_attn_last,
                                         exp_last_prev])
                pB_read[bank] = ev
                sd_wr_last = kb.op("gpsimd",
                                   lambda e, tt=tt, dmc=dmc, o=ssl: e.dma_start(
                                       out=dsd[tt * 128:(tt + 1) * 128,
                                               dmc * CH:(dmc + 1) * CH],
                                       in_=sst_slot(o)),
                                   s_sd[ssl], 16, waits=[ev])
                sst_war[ssl] = sd_wr_last
                sd_last2[ssl] = sd_wr_last
                emit_passB(1, pB_read)
            emit_passB(12, pB_read)
        emit_passB(len(units), pB_read)
        assert st["Bi"] == len(units), f"passB incomplete {st['Bi']}"
        assert st["B_pend"] is None, "passB pend not flushed"
        phaseB_pe_done = (pe, pe.n)

        b2b_ready = None
        for dmc in range(4):
            bm = kb.op("tensor",
                       lambda e, dmc=dmc: e.matmul(
                           P7[:], ONES1[:], B2ROW[:, dmc * CH:(dmc + 1) * CH],
                           start=True, stop=True),
                       pe, 1, waits=[st["p7f"], init_loads, attn_pe_done,
                                     b2b_ready])
            b2b_ready = kb.op("vector",
                              lambda e, dmc=dmc: e.tensor_copy(
                                  B2B[:, dmc * 512:(dmc + 1) * 512], P7[:]),
                              ve, 1, waits=[bm])
            st["p7f"] = b2b_ready

        pC_h_read = list(pB_read)
        pC_g_read = [None, None, None, None]
        w1p_war = [None, None]
        xnc_war = None
        w2m_war = [None, None, None, None]
        g_war = [None, None]
        st5_war = [None, None, None, None]
        xc_war = [None, None, None, None]
        zt_war = [None, None]
        ht_war = phaseB_pe_done
        y_last = None
        w1p_pend = {}

        def emit_xnc(c, first=False):
            return kb.op("sync",
                         lambda e, c=c: e.dma_start(
                             out=XNC.rearrange("p (d s) -> p d s", d=ND),
                             in_=dxnT.rearrange("p (d s) -> p d s", d=ND)[
                                 :, :, c * CH:(c + 1) * CH]),
                         s_xnc, 16,
                         waits=[xnc_war, xnt_wr,
                                st["qin_war"][0], st["qin_war"][1],
                                phaseB_pe_done if first else None])

        def emit_panel(c, fc):
            psl = fc % 2
            return kb.op("sync",
                         lambda e, fc=fc: e.dma_start(
                             out=w1p_slot(fc % 2).rearrange(
                                 "p (d f) -> p d f", d=ND),
                             in_=dw1t.rearrange("p (d f) -> p d f", d=ND)[
                                 :, :, fc * 512:(fc + 1) * 512]),
                         s_w1p[psl], 16,
                         waits=[st["w1t_done0"], st["w1t_done1"], w1p_war[psl],
                                st["qpanel_war"][0], st["qpanel_war"][1],
                                phaseB_pe_done if (c == 0 and fc < 2) else None])

        xnc_pend = emit_xnc(0, first=True)
        w1p_pend[(0, 0)] = emit_panel(0, 0)
        w1p_pend[(0, 1)] = emit_panel(0, 1)

        NFP = NF // 2
        w2seq = [(c_, d_, f_) for c_ in range(NCH) for d_ in range(4)
                 for f_ in range(NFP)]
        w2_pend = {}
        pC_sl = [None, None, None, None]
        cmb_st = {}
        cmb_xc = {}

        def emit_w2ld(n):
            c_, dmc, fp = w2seq[n]
            dsl = n % 4
            return kb.op("sync",
                         lambda e, dmc=dmc, fp=fp, m=dsl: e.dma_start(
                             out=w2m_dslot(m).rearrange(
                                 "p (b n) -> p b n", b=2),
                             in_=dw2t.rearrange("p (ff n) -> p ff n",
                                                ff=NF)[
                                 :, 2 * fp:2 * fp + 2,
                                 dmc * CH:(dmc + 1) * CH]),
                         s_w2m[dsl], 16,
                         waits=[st["w2t_done0"], st["w2t_done1"],
                                w2m_war[dsl],
                                sd_last2[0] if n < 4 else None,
                                sd_last2[1] if n < 4 else None,
                                xnb_attn_last if n < 4 else None])

        for n_ in range(4):
            w2_pend[n_] = emit_w2ld(n_)
        w2n = [4]

        for c in range(NCH):
            xnc_all = xnc_pend
            last_gelu_of_f = {}
            for fc in range(16):
                psl = fc % 2
                pl_all = w1p_pend.pop((c, fc))
                for fl in range(4):
                    f = fc * 4 + fl
                    bank = f % 2
                    for dt in range(ND):
                        w = ([pl_all, xnc_all, pC_h_read[bank],
                              st["qtrans_bank_read"][bank]] if dt == 0 else [])
                        kb.op("tensor",
                              lambda e, b=bank, p=psl, dt=dt, fl=fl: e.matmul(
                                  P[b][:],
                                  w1p_slot(p)[:, dt * 512 + fl * 128:
                                              dt * 512 + (fl + 1) * 128],
                                  xnc_d(dt),
                                  start=(dt == 0), stop=(dt == ND - 1)),
                              pe if dt == ND - 1 else None,
                              1 if dt == ND - 1 else 0, waits=w)
                    hmm_done = (pe, pe.n)
                    w1p_war[psl] = hmm_done
                    gl = kb.op("scalar",
                               lambda e, b=bank, f=f: e.activation(
                                   XNT[:, f * 512:(f + 1) * 512], P[b][:],
                                   AF.Gelu, bias=B1C[:, f:f + 1]),
                               ae, 1, waits=[hmm_done, b1c_ready, ht_war])
                    pC_h_read[bank] = gl
                    last_gelu_of_f[f] = gl
                nfc, nc_ = (fc + 2, c) if fc + 2 < 16 else (fc + 2 - 16, c + 1)
                if nc_ < NCH:
                    w1p_pend[(nc_, nfc)] = emit_panel(nc_, nfc)
            xnc_war = (pe, pe.n)
            if DBG_SCRATCH:
                kb.op("gpsimd",
                      lambda e, c=c: e.dma_start(
                          out=dbg_ht[:, c * NF * CH:(c + 1) * NF * CH],
                          in_=XNT[:, 0:NF * CH]),
                      s_init, 16, waits=[last_gelu_of_f[NF - 1]])
            if c + 1 < NCH:
                xnc_pend = emit_xnc(c + 1)

            for dmc in range(4):
                for fp in range(NFP):
                    dsl = fp % 4
                    nflat = (c * 4 + dmc) * NFP + fp
                    w2ld = w2_pend.pop(nflat)
                    for fh in range(2):
                        f = 2 * fp + fh
                        for tt in range(4):
                            gbank = 2 + tt
                            w = [w2ld, last_gelu_of_f[f]]
                            if f == 0:
                                w.append(pC_g_read[tt])
                            kb.op("tensor",
                                  lambda e, gb=gbank, f=f, tt=tt, m=dsl, fh=fh:
                                  e.matmul(
                                      P[gb][:], XNT[:, f * 512 + tt * 128:
                                                    f * 512 + (tt + 1) * 128],
                                      w2m_dslot(m)[:, fh * 512:(fh + 1) * 512],
                                      start=(f == 0), stop=(f == NF - 1)),
                                  pe if (fh == 1 and tt == 3) else None,
                                  1 if (fh == 1 and tt == 3) else 0, waits=w)
                    w2m_war[dsl] = (pe, pe.n)
                    if w2n[0] < len(w2seq):
                        w2_pend[w2n[0]] = emit_w2ld(w2n[0])
                        w2n[0] += 1
                    if fp == 16:
                        for tt in range(4):
                            trow = c * 4 + tt
                            cmb_st[tt] = kb.op(
                                "gpsimd",
                                lambda e, trow=trow, dmc=dmc, s=tt: e.dma_start(
                                    out=st5_slot(s),
                                    in_=dsd[trow * 128:(trow + 1) * 128,
                                            dmc * CH:(dmc + 1) * CH]),
                                s_st5[tt], 16,
                                waits=[sd_last2[0], sd_last2[1], st5_war[tt],
                                       st["qout_war"][0], st["qout_war"][1],
                                       st["prev_r"]])
                            cmb_xc[tt] = kb.op(
                                "gpsimd",
                                lambda e, trow=trow, dmc=dmc, s=tt: e.dma_start(
                                    out=xc_slot(s),
                                    in_=dx[trow * 128:(trow + 1) * 128,
                                           dmc * CH:(dmc + 1) * CH]),
                                s_xc[tt], 16,
                                waits=[xc_war[tt], st["qout_war"][0],
                                       st["qout_war"][1], st["prev_r"]])
                g_stop = (pe, pe.n)
                gpar = dmc % 2
                for tt in range(4):
                    zb = kb.op("vector",
                               lambda e, gb=2 + tt, dmc=dmc, tt=tt:
                               e.tensor_tensor(
                                   zt_slot(tt % 2), P[gb][:],
                                   B2B[:, dmc * 512:(dmc + 1) * 512], ALU.add),
                               ve, 1, waits=[g_stop, b2b_ready, zt_war[tt % 2]])
                    sl = kb.op("scalar",
                               lambda e, gp=gpar, tt=tt: e.activation(
                                   g_slice(gp, tt), zt_slot(tt % 2), AF.Silu),
                               ae, 1, waits=[zb, g_war[gpar]])
                    zt_war[tt % 2] = sl
                    pC_g_read[tt] = zb
                    pC_sl[tt] = sl
                for tt in range(4):
                    sg = kb.op("vector",
                               lambda e, s=tt, gp=gpar, tt=tt: e.tensor_tensor(
                                   st5_slot(s), st5_slot(s), g_slice(gp, tt),
                                   ALU.mult),
                               ve, 1, waits=[cmb_st[tt], pC_sl[tt]])
                    g_war[gpar] = sg
                    ad = kb.op("vector",
                               lambda e, s=tt: e.tensor_tensor(
                                   xc_slot(s), xc_slot(s), st5_slot(s),
                                   ALU.add),
                               ve, 1, waits=[sg, cmb_xc[tt]])
                    st5_war[tt] = ad
                    trow = c * 4 + tt
                    y_last = kb.op("gpsimd",
                                   lambda e, trow=trow, dmc=dmc, s=tt:
                                   e.dma_start(
                                       out=dy[trow * 128:(trow + 1) * 128,
                                              dmc * CH:(dmc + 1) * CH],
                                       in_=xc_slot(s)),
                                   s_y[tt], 16, waits=[ad])
                    xc_war[tt] = y_last
                ht_war = g_stop

        kb.op("sync", lambda e: e.nop(),
              waits=[xc_war[0], xc_war[1], xc_war[2], xc_war[3]])

        for s_ in kb.sems:
            s_.h = E(nc.semaphore(s_.name))
        block = E(nc.Block())

        @block.sync
        def _(e):
            kb.realize("sync", e)

        @block.scalar
        def _(e):
            kb.realize("scalar", e)

        @block.vector
        def _(e):
            kb.realize("vector", e)

        @block.tensor
        def _(e):
            kb.realize("tensor", e)

        @block.gpsimd
        def _(e):
            kb.realize("gpsimd", e)

    return nc


_NC_CACHE = None


def kernel(L0_attn_x, L0_attn_w_qkv, L0_attn_w_out, L0_ffn_w1, L0_ffn_b1,
           L0_ffn_w2, L0_ffn_b2):
    global _NC_CACHE
    x = np.ascontiguousarray(np.asarray(L0_attn_x, dtype=np.float32))
    wqkv = np.ascontiguousarray(np.asarray(L0_attn_w_qkv, dtype=np.float32))
    wout = np.ascontiguousarray(np.asarray(L0_attn_w_out, dtype=np.float32))
    w1 = np.ascontiguousarray(np.asarray(L0_ffn_w1, dtype=np.float32))
    b1 = np.ascontiguousarray(np.asarray(L0_ffn_b1, dtype=np.float32)).reshape(NF, 128)
    w2 = np.ascontiguousarray(np.asarray(L0_ffn_w2, dtype=np.float32))
    b2 = np.ascontiguousarray(np.asarray(L0_ffn_b2, dtype=np.float32)).reshape(1, DM)
    ident = np.eye(128, dtype=np.float32)
    identb = np.asarray(np.eye(128, dtype=np.float32).astype(ml_dtypes.bfloat16))
    ones1 = np.ones((1, 128), dtype=np.float32)
    masks = build_masks()

    if _NC_CACHE is None:
        _NC_CACHE = build_nc()
    nc = _NC_CACHE

    in_maps = []
    for i in range(N_CORES):
        in_maps.append({
            "x": np.ascontiguousarray(x[i]),
            "wqkv": wqkv, "wout": wout, "w1": w1, "b1": b1,
            "w2": w2, "b2": b2, "ident": ident, "identb": identb,
            "ones1": ones1, "masks": masks,
        })
    res = run_bass_kernel_spmd(nc, in_maps, core_ids=list(range(N_CORES)))
    out = np.stack([np.asarray(r["y"], dtype=np.float32) for r in res.results])
    return out
